# version 15
# baseline (speedup 1.0000x reference)
import math
from contextlib import ExitStack

import numpy as np

import concourse.bass as bass
import concourse.mybir as mybir
from concourse.bass_utils import run_bass_kernel_spmd

F32 = mybir.dt.float32
BF16 = mybir.dt.bfloat16
AF = mybir.ActivationFunctionType
OP = mybir.AluOpType
AX = mybir.AxisListType

D = 4096
NCH = D // 128
SEQ = 4096
BATCH = 2
NCORES = 8
H = 32
KVH = 8
DH = 128
WB = 1024
WCH = WB // 128
HB = 16
DHB = 64
BLK = 128
NBLK = 8
RK = 32
NL = 16
RC = 64
T = 1024
TE = T + 2
ST = 512
LT = 512
NSTRIP = SEQ // ST
EPS = 1e-5
ISQD = 1.0 / math.sqrt(DH)
ISQB = 1.0 / math.sqrt(DHB)
NEGBIG = -8.0e9

ENGINES = ("tensor", "vector", "scalar", "sync", "gpsimd")
ESEM = {"tensor": "T", "vector": "V", "scalar": "S", "gpsimd": "G",
        "sync": "Y"}


class KB:
    def __init__(self, phases=("p0", "p1", "p2", "p3", "p4"), debug=()):
        self.phases = phases
        self.debug = debug
        self.nc = bass.Bass("TRN2", target_bir_lowering=False)
        self.es = ExitStack()
        self.prog = {e: [] for e in ENGINES}
        self.semh = {}
        self.cnt = {}

    def sem(self, *names):
        for n in names:
            if n not in self.semh:
                self.semh[n] = self.es.enter_context(self.nc.semaphore(n))
                self.cnt[n] = 0

    def _waits(self, eng, toks):
        best = {}
        for t in toks:
            if t is None:
                continue
            n, v = t
            if v > 0:
                best[n] = max(best.get(n, 0), v)
        for n, v in best.items():
            sem = self.semh[n]

            def w(e, sem=sem, v=v):
                e.wait_ge(sem, v)
            self.prog[eng].append(w)

    def op(self, eng, fn, after=()):
        self._waits(eng, after)
        name = ESEM[eng]
        self.cnt[name] += 1
        v = self.cnt[name]
        sem = self.semh[name]

        def run(e, fn=fn, sem=sem):
            fn(e).then_inc(sem, 1)
        self.prog[eng].append(run)
        return (name, v)

    def dma(self, eng, sname, fn, after=()):
        self._waits(eng, after)
        self.cnt[sname] += 16
        v = self.cnt[sname]
        sem = self.semh[sname]

        def run(e, fn=fn, sem=sem):
            fn(e).then_inc(sem, 16)
        self.prog[eng].append(run)
        return (sname, v)

    def tok(self, name):
        return (name, self.cnt[name])

    def cc(self, kind, ins, outs, after=()):
        self._waits("gpsimd", after)
        self.cnt["cc"] += 1
        v = self.cnt["cc"]
        sem = self.semh["cc"]

        def run(e, ins=ins, outs=outs, sem=sem):
            e.collective_compute(
                kind, OP.bypass,
                replica_groups=[[0, 1, 2, 3], [4, 5, 6, 7]],
                ins=ins, outs=outs).then_inc(sem, 1)
        self.prog["gpsimd"].append(run)
        return ("cc", v)

    def barrier(self, skip=("cc",)):
        toks = [(n, v) for n, v in self.cnt.items()
                if v > 0 and n not in skip]
        for eng in ENGINES:
            self._waits(eng, toks)

    def sb(self, stack, name, shape, dt, side=None):
        return stack.enter_context(
            self.nc.sbuf_tensor(name, shape, dt, side=side))

    def ps(self, stack, name, shape, dt=F32):
        return stack.enter_context(self.nc.psum_tensor(name, shape, dt))

    def build(self):
        nc = self.nc

        def din(name, shape, dt=F32):
            return nc.dram_tensor(name, shape, dt, kind="ExternalInput").ap()

        i = self.i = {}
        i["xown"] = din("xown", [128, NCH, TE])
        i["cosd_own"] = din("cosd_own", [128, TE])
        i["sins_own"] = din("sins_own", [128, TE])
        i["wk"] = din("wk", [D, KVH * DH])
        i["wv"] = din("wv", [D, KVH * DH])
        i["wq_pk"] = din("wq_pk", [128, H, NCH, 128])
        i["bq_r"] = din("bq_r", [128, H])
        i["bk_r"] = din("bk_r", [128, KVH])
        i["bv_row"] = din("bv_row", [1, KVH * DH])
        i["conv_pk"] = din("conv_pk", [128, 3 * WCH, 3, NCH, 128])
        i["conv_b_r"] = din("conv_b_r", [128, 3 * WCH])
        i["wqb_pk"] = din("wqb_pk", [128, WCH, WCH, 128])
        i["wkb_pk"] = din("wkb_pk", [128, WCH, WCH, 128])
        i["wvb"] = din("wvb", [WB, HB * DHB])
        i["bqb_r"] = din("bqb_r", [128, WCH])
        i["bkb_r"] = din("bkb_r", [128, WCH])
        i["bvb_row"] = din("bvb_row", [1, HB * DHB])
        i["wo_pk"] = din("wo_pk", [128, WCH, WCH, 128])
        i["bo_r"] = din("bo_r", [128, WCH])
        i["projw_pk"] = din("projw_pk", [128, NCH, WCH, 128])
        i["projb_r"] = din("projb_r", [128, NCH])
        i["fker_pk"] = din("fker_pk", [128, NCH, NCH, 128])
        i["fbias_r"] = din("fbias_r", [128, NCH])
        i["a_pk"] = din("a_pk", [128, NL, NCH, RK])
        i["b_pk"] = din("b_pk", [128, NL, 1024])
        i["a_sh"] = din("a_sh", [128, NCH, RK])
        i["b_sh"] = din("b_sh", [RK, D])
        i["g3_row"] = din("g3_row", [1, 3 * NL])
        i["hmask"] = din("hmask", [128, 2])
        i["mask8"] = din("mask8", [128, BLK])

        self.y_t = nc.dram_tensor("y_t", [128, NCH, T], F32,
                                  kind="ExternalOutput").ap()

        def dscr(name, shape, dt):
            kind = "ExternalOutput" if name in self.debug else "Internal"
            return nc.dram_tensor(name, shape, dt, kind=kind).ap()

        self.qT_s = dscr("qT_s", [H, DH, TE], BF16)
        self.xh = dscr("xh", [128, NCH, TE], F32)
        self.nrm_d = dscr("nrm_d", [128, H + HB], F32)
        self.kag_in = dscr("kag_in", [KVH * DH, T], BF16)
        self.kag_out = dscr("kag_out", [4 * KVH * DH, T], BF16)
        self.vag_in = dscr("vag_in", [KVH * 8 * 128, DH], BF16)
        self.vag_out = dscr("vag_out", [4 * KVH * 8 * 128, DH], BF16)
        self.knag_in = dscr("knag_in", [1, KVH], F32)
        self.knag_out = dscr("knag_out", [4, KVH], F32)

        es = self.es
        self.ones_col = self.sb(es, "ones_col", [128, 1], BF16)
        self.ones_row = self.sb(es, "ones_row", [1, 128], F32)
        self.ones_mat = self.sb(es, "ones_mat", [128, 128], BF16)
        self.eps_t = self.sb(es, "eps_t", [1, 1], F32)
        self.negM = self.sb(es, "negM", [128, H], F32)
        self.negMb = self.sb(es, "negMb", [128, HB], F32)
        self.qn = self.sb(es, "qn", [1, H], F32)
        self.kn = self.sb(es, "kn", [1, KVH], F32)
        self.qnb = self.sb(es, "qnb", [1, HB], F32)
        self.knb = self.sb(es, "knb", [1, HB], F32)
        self.mrow = self.sb(es, "mrow", [1, ST], F32)
        self.msqrow = self.sb(es, "msqrow", [1, ST], F32)
        self.varrow = self.sb(es, "varrow", [1, ST], F32)
        self.m_bc = self.sb(es, "m_bc", [128, ST], BF16)
        self.rstd_bc = self.sb(es, "rstd_bc", [128, ST], BF16)
        self.nrm_t = self.sb(es, "nrm_t", [1, 1], F32)
        self.lntmp = self.sb(es, "lntmp", [128, ST], BF16)

        self.sem("T", "V", "S", "G", "Y")
        self.sem("xa0", "xa1", "cs", "ck", "sw0", "sw1", "sw2", "sw3",
                 "wres", "lnw0", "lnw1",
                 "ln0", "ln1", "kw0", "kw1", "wc0", "wc1", "wc2", "kv0", "kv1",
                 "qr0", "qr1", "qr2", "qr3", "xr0", "xr1", "xr2",
                 "st0", "st1", "st2", "ax",
                 "lr0", "lr1", "xb0", "xb1", "ag", "cc")

        self.t_init = self.op("vector", lambda v: (
            v.memset(self.ones_col[:], 1.0),
            v.memset(self.ones_row[:], 1.0),
            v.memset(self.eps_t[:], EPS),
            v.memset(self.qn[:], 0.0),
            v.memset(self.kn[:], 0.0),
            v.memset(self.ones_mat[:], 1.0))[-1])

        if "p0" in self.phases:
            self.p0()
        if "p2" in self.phases:
            self.barrier()
            sl = self.stLora = ExitStack()
            self.a_cat = self.sb(sl, "a_cat", [128, NCH, RC], BF16,
                                 side="right")
            self.a_catL = self.sb(sl, "a_catL", [128, NCH, RC], BF16,
                                  side="right")
            self.a_catR = self.sb(sl, "a_catR", [128, NCH, RC], BF16,
                                  side="right")
            self.b_cat = self.sb(sl, "b_cat", [RC, D], BF16, side="right")
            self.b_catL = self.sb(sl, "b_catL", [RC, D], BF16, side="right")
            self.b_catR = self.sb(sl, "b_catR", [RC, D], BF16, side="right")
            self.build_lora()
        if "p1" in self.phases:
            if "p2" not in self.phases:
                self.barrier()
            self.p1()
        if "p2" in self.phases:
            self.barrier()
            self.ffn(first=True)
        if "p3" in self.phases:
            self.barrier()
            self.p3()
        if "p4" in self.phases:
            self.barrier()
            self.ffn(first=False)
        if "p2" in self.phases:
            self.stLora.close()

        if "p4" not in self.phases:
            self.barrier()
            self.dma("sync", "st0",
                     lambda s: s.dma_start(out=self.y_t,
                                           in_=self.i["xown"][:, :, 1:1 + T]))
        if self.debug:
            self.barrier()
            self.dma("sync", "st0",
                     lambda s: s.dma_start(out=self.nrm_d[:, 0:H],
                                           in_=self.negM[:]))
            if "p3" in self.phases:
                self.dma("sync", "st0",
                         lambda s: s.dma_start(out=self.nrm_d[:, H:H + HB],
                                               in_=self.negMb[:]))

        self._waits("sync", [self.tok("st0"), self.tok("st1"),
                             self.tok("st2"), self.tok("kw0"),
                             self.tok("kw1"), self.tok("lnw0"),
                             self.tok("lnw1"), self.tok("ln0"),
                             self.tok("ln1"), self.tok("ag"),
                             self.tok("cc")])

        with nc.Block() as block:
            block.tensor(lambda t: [fn(t) for fn in self.prog["tensor"]])
            block.vector(lambda v: [fn(v) for fn in self.prog["vector"]])
            block.scalar(lambda sc: [fn(sc) for fn in self.prog["scalar"]])
            block.sync(lambda s: [fn(s) for fn in self.prog["sync"]])
            block.gpsimd(lambda g: [fn(g) for fn in self.prog["gpsimd"]])
        self.es.close()
        return nc

    def ln_stats(self, meansrc, sqsrc, sq_buf, sqb_off, wt, after,
                 ps_stat, ps_bcm, ps_bcr):
        p = self
        msh, mso = meansrc
        sqh, sqo = sqsrc
        t_sq = p.op("scalar", lambda sc: sc.activation(
            sq_buf[:, :, sqb_off:sqb_off + wt], sqh[:, :, sqo:sqo + wt],
            AF.Square), after=after)

        def stats(t):
            for kc in range(NCH):
                t.matmul(ps_stat[0:1, 0:wt], self.ones_col[:],
                         msh[:, kc, mso:mso + wt], start=(kc == 0),
                         stop=(kc == NCH - 1))
            for kc in range(NCH):
                last = t.matmul(ps_stat[0:1, ST:ST + wt], self.ones_col[:],
                                sq_buf[:, kc, sqb_off:sqb_off + wt],
                                start=(kc == 0), stop=(kc == NCH - 1))
            return last
        t_stat = p.op("tensor", stats, after=list(after) + [t_sq, self.t_init])
        t_rows = p.op("scalar", lambda sc: (
            sc.activation(self.mrow[0:1, 0:wt], ps_stat[0:1, 0:wt], AF.Copy,
                          scale=1.0 / D),
            sc.activation(self.msqrow[0:1, 0:wt], ps_stat[0:1, ST:ST + wt],
                          AF.Copy, scale=1.0 / D))[-1], after=[t_stat])
        t_v1 = p.op("vector", lambda v: v.tensor_tensor(
            self.varrow[0:1, 0:wt], self.mrow[0:1, 0:wt],
            self.mrow[0:1, 0:wt], op=OP.mult), after=[t_rows])
        t_v2 = p.op("vector", lambda v: v.tensor_tensor(
            self.varrow[0:1, 0:wt], self.msqrow[0:1, 0:wt],
            self.varrow[0:1, 0:wt], op=OP.subtract), after=[t_v1])
        t_l = p.op("scalar", lambda sc: sc.activation(
            self.varrow[0:1, 0:wt], self.varrow[0:1, 0:wt], AF.Ln,
            bias=self.eps_t[:]), after=[t_v2])
        t_r = p.op("scalar", lambda sc: sc.activation(
            self.varrow[0:1, 0:wt], self.varrow[0:1, 0:wt], AF.Exp,
            scale=-0.5), after=[t_l])
        t_bc = p.op("tensor", lambda t: (
            t.matmul(ps_bcm[:, 0:wt], self.ones_row[:], self.mrow[0:1, 0:wt],
                     start=True, stop=True),
            t.matmul(ps_bcr[:, 0:wt], self.ones_row[:],
                     self.varrow[0:1, 0:wt], start=True, stop=True))[-1],
            after=[t_r])
        t_e = p.op("vector", lambda v: (
            v.tensor_copy(self.m_bc[:, 0:wt], ps_bcm[:, 0:wt]),
            v.tensor_copy(self.rstd_bc[:, 0:wt], ps_bcr[:, 0:wt]))[-1],
            after=[t_bc])
        return t_e, t_stat, t_sq

    def ln_apply(self, src, soff, dst, doff, wt, after):
        p = self
        prev = None
        for kc in range(NCH):
            a = list(after) if kc == 0 else [prev]
            ta = p.op("vector", lambda v, kc=kc: v.tensor_tensor(
                self.lntmp[:, 0:wt], src[:, kc, soff:soff + wt],
                self.m_bc[:, 0:wt], op=OP.subtract), after=a)
            tb = p.op("vector", lambda v, kc=kc: v.tensor_tensor(
                dst[:, kc, doff:doff + wt], self.lntmp[:, 0:wt],
                self.rstd_bc[:, 0:wt], op=OP.mult), after=[ta])
            prev = tb
        return prev

    def p0(self):
        p = self
        stP = self.stP0 = ExitStack()
        lnst_b = [self.sb(stP, f"lnst{k}", [128, NCH, ST], BF16)
                  for k in range(2)]
        lntl = self.sb(stP, "lntl", [128, NCH, 2], BF16)
        st0 = ExitStack()
        xb_b = [self.sb(st0, "xb0", [128, NCH, ST], BF16)] * 2
        wres = self.sb(st0, "wres", [128, NCH, KVH * DH], BF16)
        cosd = self.sb(st0, "cosd", [128, ST], F32)
        sins = self.sb(st0, "sins", [128, ST], F32)
        kf = [self.sb(st0, f"kf{k}", [128, ST], F32) for k in range(4)]
        ksw = [self.sb(st0, f"ksw{k}", [128, ST], F32) for k in range(4)]
        kbf = [self.sb(st0, f"kbf{k}", [128, ST], BF16) for k in range(4)]
        ksq = self.sb(st0, "ksq", [128, ST], BF16)
        bk_sb = self.sb(st0, "bk_sb", [128, KVH], F32)
        bv_sb = self.sb(st0, "bv_sb", [1, KVH * DH], F32)
        ps_stat = self.ps(st0, "ps_stat0", [1, 2 * ST])
        ps_bcm = self.ps(st0, "ps_bcm0", [128, ST])
        ps_bcr = self.ps(st0, "ps_bcr0", [128, ST])
        ps_k = [self.ps(st0, f"ps_k{k}", [128, ST]) for k in range(2)]
        ps_n = self.ps(st0, "ps_n0", [1, 2 * ST])

        p.dma("sync", "cs", lambda s: s.dma_start(out=bk_sb[:],
                                                  in_=self.i["bk_r"]))
        p.dma("sync", "cs", lambda s: s.dma_start(out=bv_sb[:],
                                                  in_=self.i["bv_row"]))
        t_bk = t_bv = p.tok("cs")
        t_wk = p.dma("gpsimd", "wres", lambda g: g.dma_start(
            out=wres[:], in_=self.i["wk"].rearrange("(n p) d -> p n d", p=128),
            max_dma_last_dim=4096))

        kag_v = self.kag_in.rearrange("(j p) t -> j p t", p=DH)
        STRIPS = ((0, ST, lnst_b[0], True), (ST, ST, lnst_b[1], True),
                  (2 * ST, 2, lntl, False))
        xa = ["xa0", "xa1"]
        t_x = {}
        t_statdone = {}
        t_lnapp = {}
        t_kmm = {}
        t_kevac = {}
        t_kr1 = {}
        t_kr3 = {}
        t_kw = {}
        t_ksq = {}
        t_knm = None
        t_knr = None
        t_knrd = {}
        t_knx = None
        t_rope_last = None
        t_cs2 = None
        def emit_xload(s, a):
            off, wt = STRIPS[s][0], STRIPS[s][1]
            xbuf = xb_b[s % 2]
            t_x[s] = p.dma("gpsimd", xa[s % 2], lambda g, off=off, wt=wt,
                           xbuf=xbuf:
                           g.dma_start(out=xbuf[:, :, 0:wt],
                                       in_=self.i["xown"][:, :, off:off + wt],
                                       max_dma_last_dim=4096), after=a)
        emit_xload(0, [])
        gi = -1
        for s, (off, wt, lnst, do_k) in enumerate(STRIPS):
            xbuf = xb_b[s % 2]
            if do_k:
                a = [t_rope_last] if t_rope_last else []
                t_cs1 = p.dma("sync", "ck", lambda ss, off=off: ss.dma_start(
                    out=cosd[:], in_=self.i["cosd_own"][:, off:off + ST]),
                    after=a)
                t_cs2 = p.dma("sync", "ck", lambda ss, off=off: ss.dma_start(
                    out=sins[:], in_=self.i["sins_own"][:, off:off + ST]))
            st_after = [t_x[s]]
            t_bce, t_stat, _ = self.ln_stats((xbuf, 0), (xbuf, 0), lnst, 0,
                                             wt, st_after, ps_stat, ps_bcm,
                                             ps_bcr)
            t_statdone[s] = t_stat
            t_lnapp[s] = self.ln_apply(xbuf, 0, lnst, 0, wt, [t_bce, t_stat])
            if s + 1 < len(STRIPS):
                emit_xload(s + 1, [t_statdone[s], t_lnapp[s]])
            if not do_k:
                continue
            for ko in range(KVH):
                gi += 1
                kp = gi % 2
                k4 = gi % 4
                sw = f"sw{k4}"
                kw = "kw0" if kp == 0 else "kw1"
                mm_after = [t_lnapp[s], t_x[s], t_wk]
                if kp in t_kevac:
                    mm_after.append(t_kevac[kp])
                t_kmm[(s, ko)] = p.op("tensor", lambda t, ko=ko, kp=kp,
                                      lnst=lnst: [
                    t.matmul(ps_k[kp][:], wres[:, kc, ko * DH:(ko + 1) * DH],
                             lnst[:, kc, :], start=(kc == 0),
                             stop=(kc == NCH - 1))
                    for kc in range(NCH)][-1], after=mm_after)
                ev_after = [t_kmm[(s, ko)], t_bk]
                if k4 in t_kr3:
                    ev_after.append(t_kr3[k4])
                t_kevac[kp] = p.op("scalar", lambda sc, ko=ko, kp=kp, k4=k4:
                                   sc.activation(kf[k4][:], ps_k[kp][:],
                                                 AF.Identity,
                                                 bias=bk_sb[:, ko:ko + 1]),
                                   after=ev_after)
                sq_after = [t_kevac[kp]]
                if t_knm is not None:
                    sq_after.append(t_knm)
                t_ksq[k4] = p.op("scalar", lambda sc, k4=k4: sc.activation(
                    ksq[:], kf[k4][:], AF.Square), after=sq_after)
                np2 = gi % 2
                nm_after = [t_ksq[k4]]
                if t_knrd.get(np2) is not None:
                    nm_after.append(t_knrd[np2])
                t_knm = p.op("tensor", lambda t, np2=np2: t.matmul(
                    ps_n[0:1, np2 * ST:(np2 + 1) * ST], self.ones_col[:],
                    ksq[:], start=True, stop=True), after=nm_after)
                nr_after = [t_knm]
                if t_knx is not None:
                    nr_after.append(t_knx)
                t_knr = p.op("vector", lambda v, np2=np2: v.reduce_max(
                    self.nrm_t[:], ps_n[0:1, np2 * ST:(np2 + 1) * ST],
                    axis=AX.X), after=nr_after)
                t_knrd[np2] = t_knr
                t_knx = p.op("vector", lambda v, ko=ko: v.tensor_tensor(
                    self.kn[0:1, ko:ko + 1], self.kn[0:1, ko:ko + 1],
                    self.nrm_t[:], op=OP.max), after=[t_knr])
                sw_after = [t_kevac[kp]]
                if k4 in t_kr3:
                    sw_after.append(t_kr3[k4])
                p.dma("sync", sw, lambda ss, k4=k4: ss.dma_start(
                    out=ksw[k4][0:64, :], in_=kf[k4][64:128, :]),
                    after=sw_after)
                t_sw2 = p.dma("sync", sw, lambda ss, k4=k4: ss.dma_start(
                    out=ksw[k4][64:128, :], in_=kf[k4][0:64, :]))
                t_kr1[k4] = p.op("gpsimd", lambda v, k4=k4: v.tensor_tensor(
                    kf[k4][:], kf[k4][:], cosd[:], op=OP.mult),
                    after=[t_cs2, t_sw2, t_kevac[kp], t_ksq[k4]])
                t_r2 = p.op("gpsimd", lambda v, k4=k4: v.tensor_tensor(
                    ksw[k4][:], ksw[k4][:], sins[:], op=OP.mult),
                    after=[t_kr1[k4]])
                r3_after = [t_r2]
                if k4 in t_kw:
                    r3_after += [self.tok(kw)]
                t_kr3[k4] = p.op("gpsimd", lambda v, k4=k4: v.tensor_tensor(
                    kbf[k4][:], kf[k4][:], ksw[k4][:], op=OP.add),
                    after=r3_after)
                t_rope_last = t_kr3[k4]
                t_kw[k4] = p.dma("sync", kw,
                                 lambda ss, ko=ko, off=off, k4=k4:
                                 ss.dma_start(
                                     out=kag_v[ko, :, off:off + ST],
                                     in_=kbf[k4][:]), after=[t_kr3[k4]])

        p.dma("sync", "kw0", lambda ss: ss.dma_start(
            out=self.knag_in, in_=self.kn[:]), after=[t_knx])
        def flat(ap, rows):
            n = 1
            for st, sz in ap.ap:
                n *= sz
            return bass.AP(tensor=ap.tensor, offset=ap.offset,
                           ap=[[n // rows, rows], [1, n // rows]])
        t_ccK = self.t_ccK = self.cc(
            "AllGather", ins=[flat(self.kag_in, 64)],
            outs=[flat(self.kag_out, 256)],
            after=[self.tok("kw0"), self.tok("kw1")])
        t_ccN = self.cc("AllGather", ins=[self.knag_in.opt()],
                        outs=[self.knag_out.opt()])

        vag_v = self.vag_in.rearrange("(j g p) d -> j g p d", g=8, p=128)
        t_passa_T = self.tok("T")
        t_passa_V = self.tok("V")
        t_passa_S = self.tok("S")
        t_passa_k0 = self.tok("kw0")
        t_passa_k1 = self.tok("kw1")
        wv_v = self.i["wv"].rearrange("(n p) d -> p n d", p=128)
        t_wvh = []
        for hf, strm in enumerate(("wres", "xa1")):
            t_wvh.append(p.dma("gpsimd", strm, lambda g, hf=hf:
                               g.dma_start(
                                   out=wres[:, :, hf * 512:(hf + 1) * 512],
                                   in_=wv_v[:, :, hf * 512:(hf + 1) * 512],
                                   max_dma_last_dim=4096),
                               after=[t_passa_T] if hf == 0 else []))
        t_vmm = []
        t_vev = []
        t_vw = []
        for s in range(2):
            lbuf = lnst_b[s]
            for pc in range(4):
                for hf in range(2):
                    vi = s * 8 + pc * 2 + hf
                    kp = vi % 2
                    a = [t_lnapp[s], t_wvh[hf], t_bv]
                    if vi >= 2:
                        a.append(t_vev[vi - 2])
                    else:
                        a.append(t_passa_S)
                    t_vmm.append(p.op("tensor",
                        lambda t, pc=pc, hf=hf, kp=kp, lbuf=lbuf: (
                            t.matmul(ps_k[kp][:], self.ones_row[:],
                                     bv_sb[0:1, hf * 512:(hf + 1) * 512],
                                     start=True, stop=False),
                            [t.matmul(ps_k[kp][:],
                                      lbuf[:, kc, pc * 128:(pc + 1) * 128],
                                      wres[:, kc, hf * 512:(hf + 1) * 512],
                                      start=False, stop=(kc == NCH - 1))
                             for kc in range(NCH)])[1][-1], after=a))
                    a = [t_vmm[vi]]
                    if vi >= 2:
                        a.append(t_vw[vi - 2])
                    else:
                        a += [t_passa_k0, t_passa_k1, t_passa_S]
                    t_vev.append(p.op("scalar", lambda sc, kp=kp:
                                      sc.activation(kbf[kp][:], ps_k[kp][:],
                                                    AF.Copy), after=a))
                    g_ = s * 4 + pc
                    t_vw.append(p.dma("sync", "lnw0" if kp == 0 else "lnw1",
                        lambda ss, g_=g_, hf=hf, kp=kp: ss.dma_start(
                            out=vag_v[4 * hf:4 * hf + 4, g_, :, :]
                                .rearrange("k p d -> p k d"),
                            in_=kbf[kp][:].rearrange("p (k d) -> p k d", k=4)),
                        after=[t_vev[vi]]))

        t_ccV = self.t_ccV = self.cc(
            "AllGather", ins=[flat(self.vag_in, 64)],
            outs=[flat(self.vag_out, 256)],
            after=[self.tok("lnw0"), self.tok("lnw1")])

        st0.close()
        self.barrier()
        stC = ExitStack()
        wqcb = [self.sb(stC, f"wqcb{k}", [128, NCH, 128], BF16)
                for k in range(2)]
        cosdo = self.sb(stC, "cosdo", [128, TE], F32)
        sinso = self.sb(stC, "sinso", [128, TE], F32)
        qf = [self.sb(stC, f"qf{k}", [128, TE], F32) for k in range(4)]
        qsw = [self.sb(stC, f"qsw{k}", [128, TE], F32) for k in range(4)]
        qbf = [self.sb(stC, f"qbf{k}", [128, TE], BF16) for k in range(4)]
        qsq = self.sb(stC, "qsq", [128, TE], BF16)
        bq_sb = self.sb(stC, "bq_sb", [128, H], F32)
        ps_q = [self.ps(stC, f"ps_q{k}", [128, 1536]) for k in range(2)]
        ps_qn = self.ps(stC, "ps_qn", [1, 1024])

        p.dma("sync", "cs", lambda s: s.dma_start(
            out=cosdo[:], in_=self.i["cosd_own"]))
        p.dma("sync", "cs", lambda s: s.dma_start(
            out=sinso[:], in_=self.i["sins_own"]))
        p.dma("sync", "cs", lambda s: s.dma_start(out=bq_sb[:],
                                                  in_=self.i["bq_r"]))
        t_lno = t_cso2 = t_bq = p.tok("cs")
        lnseg = ((lnst_b[0], 0, ST), (lnst_b[1], ST, ST), (lntl, 2 * ST, 2))
        wc = ["wc0", "wc1"]
        t_wq = {}
        t_qmm = {}
        t_qev = {}
        t_qr1c = {}
        t_qr3c = {}
        t_qwc = {}
        t_qsqc = {}
        t_qnm = None
        t_qnr = None
        t_qnrd = {}
        for qh in range(H):
            qp = qh % 2
            q4 = qh % 4
            swq = f"sw{q4}"
            kwq = "ln0" if qp == 0 else "ln1"
            a = [t_qmm[qh - 2]] if qh >= 2 else []
            t_wq[qh] = p.dma("gpsimd", wc[qp], lambda g, qh=qh, qp=qp:
                             g.dma_start(out=wqcb[qp][:],
                                         in_=self.i["wq_pk"][:, qh],
                                         max_dma_last_dim=8192), after=a)
            a = [t_wq[qh], t_lno]
            if qh >= 2:
                a.append(t_qev[qh - 2])
            t_qmm[qh] = p.op("tensor", lambda t, qp=qp: [
                t.matmul(ps_q[qp][:, c0:c0 + wt], wqcb[qp][:, kc, :],
                         seg_h[:, kc, 0:wt], start=(kc == 0),
                         stop=(kc == NCH - 1))
                for (seg_h, c0, wt) in lnseg
                for kc in range(NCH)][-1], after=a)
            a = [t_qmm[qh], t_bq]
            if q4 in t_qr3c:
                a.append(t_qr3c[q4])
            t_qev[qh] = p.op("scalar", lambda sc, qh=qh, qp=qp, q4=q4:
                             sc.activation(
                                 qf[q4][:, 0:TE], ps_q[qp][:, 0:TE],
                                 AF.Identity,
                                 bias=bq_sb[:, qh:qh + 1]), after=a)
            sqa = [t_qev[qh]]
            if t_qnm is not None:
                sqa.append(t_qnm)
            t_qsqc[q4] = p.op("scalar", lambda sc, q4=q4: sc.activation(
                qsq[:], qf[q4][:], AF.Square), after=sqa)
            t_qnx = None
            for si, (sc0, sw) in enumerate(((0, 512), (512, 512), (1024, 2))):
                np2 = (qh * 3 + si) % 2
                nma = [t_qsqc[q4]]
                if t_qnrd.get(np2) is not None:
                    nma.append(t_qnrd[np2])
                t_qnm = p.op("tensor", lambda t, sc0=sc0, sw=sw, np2=np2:
                             t.matmul(
                    ps_qn[0:1, np2 * 512:np2 * 512 + sw], self.ones_col[:],
                    qsq[:, sc0:sc0 + sw], start=True, stop=True), after=nma)
                nra = [t_qnm]
                if t_qnx is not None:
                    nra.append(t_qnx)
                t_qnr = p.op("vector", lambda v, sw=sw, np2=np2: v.reduce_max(
                    self.nrm_t[:], ps_qn[0:1, np2 * 512:np2 * 512 + sw],
                    axis=AX.X), after=nra)
                t_qnrd[np2] = t_qnr
                t_qnx = p.op("vector", lambda v, qh=qh: v.tensor_tensor(
                    self.qn[0:1, qh:qh + 1], self.qn[0:1, qh:qh + 1],
                    self.nrm_t[:], op=OP.max), after=[t_qnr])
            t_qnr = t_qnx
            sw_after = [t_qev[qh]]
            if q4 in t_qr3c:
                sw_after.append(t_qr3c[q4])
            p.dma("sync", swq, lambda ss, q4=q4: ss.dma_start(
                out=qsw[q4][0:64, :], in_=qf[q4][64:128, :]),
                after=sw_after)
            t_s2 = p.dma("sync", swq, lambda ss, q4=q4: ss.dma_start(
                out=qsw[q4][64:128, :], in_=qf[q4][0:64, :]))
            t_qr1c[q4] = p.op("vector", lambda v, q4=q4: v.tensor_tensor(
                qf[q4][:], qf[q4][:], cosdo[:], op=OP.mult),
                after=[t_cso2, t_s2, t_qev[qh], t_qsqc[q4]])
            t_r2 = p.op("vector", lambda v, q4=q4: v.tensor_tensor(
                qsw[q4][:], qsw[q4][:], sinso[:], op=OP.mult),
                after=[t_qr1c[q4]])
            r3a = [t_r2]
            if q4 in t_qwc:
                r3a += [self.tok(kwq)]
            t_qr3c[q4] = p.op("vector", lambda v, q4=q4: v.tensor_tensor(
                qbf[q4][:], qf[q4][:], qsw[q4][:], op=OP.add), after=r3a)
            t_qwc[q4] = p.dma("sync", kwq,
                              lambda ss, qh=qh, q4=q4:
                              ss.dma_start(out=self.qT_s[qh],
                                           in_=qbf[q4][:]),
                              after=[t_qr3c[q4]])

        t_agl = None
        for r in range(4):
            a = [t_ccN, self.tok("S")] if r == 0 else []
            t_agl = p.dma("sync", "ag", lambda ss, r=r: ss.dma_start(
                out=self.mrow[0:1, 64 + r * KVH:64 + (r + 1) * KVH],
                in_=self.knag_out[r:r + 1, :]), after=a)
        t_kn = p.op("vector", lambda v: v.tensor_tensor(
            self.kn[0:1, :], self.mrow[0:1, 64:64 + KVH],
            self.mrow[0:1, 72:72 + KVH], op=OP.max),
            after=[t_agl, self.tok("V")])
        for c0 in (80, 88):
            t_kn = p.op("vector", lambda v, c0=c0: v.tensor_tensor(
                self.kn[0:1, :], self.kn[0:1, :],
                self.mrow[0:1, c0:c0 + KVH], op=OP.max), after=[t_kn])

        knb_ap = self.kn[:]
        kn_exp = bass.AP(tensor=knb_ap.tensor, offset=knb_ap.offset,
                         ap=[[knb_ap.ap[0][0], 1], [1, KVH], [0, 4]])
        t_m1 = p.op("vector", lambda v: v.tensor_tensor(
            self.mrow[0:1, 0:H].rearrange("p (a b) -> p a b", a=KVH),
            self.qn[:].rearrange("p (a b) -> p a b", a=KVH),
            kn_exp, op=OP.mult), after=[t_qnr, t_kn, self.tok("V")])
        t_m2 = p.op("scalar", lambda sc: sc.activation(
            self.mrow[0:1, 0:H], self.mrow[0:1, 0:H], AF.Ln,
            bias=self.eps_t[:]), after=[t_m1, self.tok("S")])
        t_m3 = p.op("scalar", lambda sc: sc.activation(
            self.mrow[0:1, 0:H], self.mrow[0:1, 0:H], AF.Exp, scale=0.5),
            after=[t_m2])
        t_m4 = p.op("scalar", lambda sc: sc.activation(
            self.msqrow[0:1, 0:H], self.mrow[0:1, 0:H], AF.Copy, scale=-ISQD,
            bias=-1.0), after=[t_m3])
        t_m5 = p.op("tensor", lambda t: t.matmul(
            ps_q[0][:, 0:H], self.ones_row[:], self.msqrow[0:1, 0:H],
            start=True, stop=True), after=[t_m4, self.tok("T")])
        t_m6 = p.op("vector", lambda v: v.tensor_copy(
            self.negM[:], ps_q[0][:, 0:H]), after=[t_m5])
        stC.close()
        stP.close()

    def p1(self):
        p = self
        guard = getattr(self, "t_psg_guard", None)
        st1 = ExitStack()
        kjb = [self.sb(st1, f"kjb{k}", [128, SEQ], BF16) for k in range(2)]
        vjb = [self.sb(st1, f"vjb{k}", [128, NCH, DH], BF16) for k in range(2)]
        qhb = [self.sb(st1, f"qhb{k}", [128, TE], BF16) for k in range(4)]
        pb = [self.sb(st1, f"pb{k}", [128, 1024], BF16) for k in range(2)]
        ptl = self.sb(st1, "ptl", [128, 64], BF16)
        dnl_sb = self.sb(st1, "dnl_sb", [128, ST], F32)
        rc_sb = self.sb(st1, "rc_sb", [128, ST], F32)
        tt_sb = self.sb(st1, "tt_sb", [128, ST], F32)
        x0c = [self.sb(st1, f"x0c{k}", [128, ST], F32) for k in range(2)]
        xoc = [self.sb(st1, f"xoc{k}", [128, ST], F32) for k in range(2)]
        x0t = self.sb(st1, "x0t", [128, 2], F32)
        xot = self.sb(st1, "xot", [128, 2], F32)
        ps_s = [self.ps(st1, f"ps_s{k}", [128, 1024]) for k in range(2)]
        ps_pv_b = [self.ps(st1, f"ps_pv{k}", [128, ST]) for k in range(2)]
        ps_dn = self.ps(st1, "ps_dn", [128, ST])
        ps_tl = self.ps(st1, "ps_tl", [128, 64])

        kv = ["kv0", "kv1"]
        qr = ["qr0", "qr1", "qr2", "qr3"]
        xr = ["xr0", "xr1"]
        kago = self.kag_out.rearrange("(r j p) t -> j p r t", r=4, p=DH)
        vago = self.vag_out.rearrange("(r j g p) d -> j p r g d", r=4, g=8,
                                      p=128)
        t_kvld = {}
        t_qld = {}
        t_jdone = {}
        t_qdone = {}
        t_exp = {}
        t_pvdn = {}
        t_upd = {}
        t_rc1p = None
        t_rc2p = None
        t_t1p = None
        t_uprev = None
        t_t1prev = [None, None]
        t_xwp = {}
        t_expt_g = None
        t_pvt_g = None

        for j in range(KVH):
            jp = j % 2
            a = [t_jdone[j - 2]] if j >= 2 else [self.t_ccK]
            p.dma("gpsimd", kv[jp], lambda g, j=j, jp=jp: g.dma_start(
                out=kjb[jp][:], in_=kago[j], max_dma_last_dim=8192),
                after=a)
            a2 = [] if j >= 2 else [self.t_ccV]
            for r in range(4):
                t_kvld[j] = p.dma("gpsimd", kv[jp], lambda g, j=j, jp=jp,
                                  r=r: g.dma_start(
                                      out=vjb[jp][:, r * 8:(r + 1) * 8, :],
                                      in_=vago[j][:, r],
                                      max_dma_last_dim=8192),
                                  after=a2 if r == 0 else [])
            for u in range(4):
                qh = 4 * j + u
                qp = qh % 4
                a = [t_qdone[qh - 4]] if qh >= 4 else []
                t_qld[qh] = p.dma("sync", qr[qp], lambda s, qh=qh, qp=qp:
                                  s.dma_start(out=qhb[qp][:],
                                              in_=self.qT_s[qh]), after=a)
                for qt in range(2):
                    xp = qt % 2
                    a = []
                    prev = (t_upd.get((qh - 1, 2)) if qt == 0
                            else t_upd.get((qh - 1, 1)))
                    if prev:
                        a.append(prev)
                    t_xld = p.dma("sync", xr[xp], lambda s, qh=qh, qt=qt,
                                  xp=xp: s.dma_start(
                        out=x0c[xp][:],
                        in_=self.i["xown"][:, qh, qt * ST:(qt + 1) * ST]),
                        after=a)
                    pvp = (qh * 2 + qt) % 2
                    ps_pv = ps_pv_b[pvp]

                    def mk_pvdn(pr, jp, ps_pv=ps_pv):
                        def pvdn(t, jp=jp, pr=pr, sp=pr % 2):
                            first = (pr == 0)
                            last = None
                            for h2 in range(2):
                                kc = pr * 2 + h2
                                lst = (pr == 15) and h2 == 1
                                t.matmul(ps_pv[:], vjb[jp][:, kc, :],
                                         pb[sp][:, h2 * ST:(h2 + 1) * ST],
                                         start=(first and h2 == 0), stop=lst)
                                last = t.matmul(ps_dn[:], self.ones_mat[:],
                                                pb[sp][:, h2 * ST:(h2 + 1) * ST],
                                                start=(first and h2 == 0),
                                                stop=lst)
                            return last
                        return pvdn

                    for pr in range(16):
                        sp = pr % 2

                        def smm(t, jp=jp, qp=qp, qt=qt, pr=pr, sp=sp):
                            last = None
                            for h2 in range(2):
                                kc = pr * 2 + h2
                                last = t.matmul(
                                    ps_s[sp][:, h2 * ST:(h2 + 1) * ST],
                                    kjb[jp][:, kc * 128:(kc + 1) * 128],
                                    qhb[qp][:, qt * ST:(qt + 1) * ST],
                                    start=True, stop=True)
                            return last
                        a = [t_kvld[j], t_qld[qh]]
                        if guard is not None and (j, u, qt, pr) == (0, 0, 0, 0):
                            a.append(guard)
                        if sp in t_exp:
                            a.append(t_exp[sp])
                        t_smm = p.op("tensor", smm, after=a)
                        if pr >= 1:
                            a = [t_exp[1 - sp]]
                            if pr == 1:
                                if t_t1prev[pvp] is not None:
                                    a.append(t_t1prev[pvp])
                                if t_rc1p:
                                    a.append(t_rc1p)
                            t_pvdn[1 - sp] = p.op("tensor", mk_pvdn(pr - 1, jp),
                                                  after=a)
                        a = [t_smm]
                        if sp in t_pvdn:
                            a.append(t_pvdn[sp])
                        t_exp[sp] = p.op("scalar", lambda sc, sp=sp, qh=qh:
                                         sc.activation(
                                             pb[sp][:], ps_s[sp][:], AF.Exp,
                                             scale=ISQD,
                                             bias=self.negM[:, qh:qh + 1]),
                                         after=a)
                    t_pvdn[1] = p.op("tensor", mk_pvdn(15, jp),
                                     after=[t_exp[1]])
                    a = [t_pvdn[1]]
                    if t_rc2p:
                        a.append(t_rc2p)
                    t_rc1p = p.op("scalar", lambda sc: sc.activation(
                        dnl_sb[:], ps_dn[:], AF.Ln), after=a)
                    a = [t_rc1p]
                    if t_t1p:
                        a.append(t_t1p)
                    t_rc2p = p.op("scalar", lambda sc: sc.activation(
                        rc_sb[:], dnl_sb[:], AF.Exp, scale=-1.0), after=a)
                    t_t1p = p.op("vector", lambda v, ps_pv=ps_pv:
                                 v.tensor_tensor(
                                     tt_sb[:], ps_pv[:], rc_sb[:],
                                     op=OP.mult),
                        after=[t_rc2p, t_pvdn[1]] +
                              ([t_uprev] if t_uprev else []))
                    t_t1prev[pvp] = t_t1p
                    t_u = p.op("vector", lambda v, xp=xp: v.tensor_tensor(
                        xoc[xp][:], tt_sb[:], x0c[xp][:], op=OP.add),
                        after=[t_t1p, t_xld] +
                              ([t_xwp[(qh - 1, qt)]] if (qh - 1, qt) in t_xwp
                               else []))
                    t_upd[(qh, qt)] = t_u
                    t_uprev = t_u
                    t_xwp[(qh, qt)] = p.dma("sync", "st0" if qt == 0 else "st1",
                        lambda s, qh=qh, qt=qt, xp=xp: s.dma_start(
                            out=self.xh[:, qh, qt * ST:(qt + 1) * ST],
                            in_=xoc[xp][:]), after=[t_u])
                t_xldt = p.dma("sync", xr[0], lambda s, qh=qh: s.dma_start(
                    out=x0t[:], in_=self.i["xown"][:, qh, T:TE]),
                    after=[t_upd[(qh, 1)]] +
                          ([t_xwp[(qh - 1, 2)]] if (qh - 1, 2) in t_xwp else []))
                def smmt(t, jp=jp, qp=qp):
                    last = None
                    for kc in range(NCH):
                        last = t.matmul(ps_tl[:, kc * 2:(kc + 1) * 2],
                                        kjb[jp][:, kc * 128:(kc + 1) * 128],
                                        qhb[qp][:, T:TE], start=True,
                                        stop=True)
                    return last
                a = [t_pvdn[1]]
                if t_expt_g:
                    a.append(t_expt_g)
                t_smt = p.op("tensor", smmt, after=a)
                a = [t_smt]
                if t_pvt_g:
                    a.append(t_pvt_g)
                t_expt_g = p.op("scalar", lambda sc, qh=qh: sc.activation(
                    ptl[:], ps_tl[:], AF.Exp, scale=ISQD,
                    bias=self.negM[:, qh:qh + 1]), after=a)

                def pvdnt(t, jp=jp):
                    last = None
                    for kc in range(NCH):
                        first = (kc == 0)
                        lst = (kc == NCH - 1)
                        t.matmul(ps_pv_b[0][:, 0:2], vjb[jp][:, kc, :],
                                 ptl[:, kc * 2:(kc + 1) * 2],
                                 start=first, stop=lst)
                        last = t.matmul(ps_dn[:, 0:2], self.ones_mat[:],
                                        ptl[:, kc * 2:(kc + 1) * 2],
                                        start=first, stop=lst)
                    return last
                t_pvt_g = p.op("tensor", pvdnt, after=[
                    t_expt_g, t_t1prev[0] or ("T", 0), t_rc1p])
                t_pvt = t_pvt_g
                t_rc1p = p.op("scalar", lambda sc: sc.activation(
                    dnl_sb[:, 0:2], ps_dn[:, 0:2], AF.Ln),
                    after=[t_pvt, t_rc2p])
                t_rc2p = p.op("scalar", lambda sc: sc.activation(
                    rc_sb[:, 0:2], dnl_sb[:, 0:2], AF.Exp, scale=-1.0),
                    after=[t_rc1p, t_t1p])
                t_t1p = p.op("vector", lambda v: v.tensor_tensor(
                    tt_sb[:, 0:2], ps_pv_b[0][:, 0:2], rc_sb[:, 0:2],
                    op=OP.mult), after=[t_rc2p, t_pvt] +
                    ([t_uprev] if t_uprev else []))
                t_t1prev[0] = t_t1p
                t_u = p.op("vector", lambda v: v.tensor_tensor(
                    xot[:], tt_sb[:, 0:2], x0t[:], op=OP.add),
                    after=[t_t1p, t_xldt])
                t_upd[(qh, 2)] = t_u
                t_uprev = t_u
                t_xwp[(qh, 2)] = p.dma("sync", "st2", lambda s, qh=qh:
                                       s.dma_start(out=self.xh[:, qh, T:TE],
                                                   in_=xot[:]), after=[t_u])
                t_qdone[qh] = self.tok("T")
            t_jdone[j] = self.tok("T")
        st1.close()

    def ffn(self, first):
        p = self
        tag = "2" if first else "4"
        stF = ExitStack()
        if first:
            mtiles = ((0, ST), (ST, ST), (T, 2))
            ltiles = [(k * LT, LT) for k in range(T // LT)] + [(T, 2)]
        else:
            mtiles = ((1, ST), (1 + ST, ST))
            ltiles = [(1 + k * LT, LT) for k in range(T // LT)]
        lnx = self.sb(stF, "lnx" + tag, [128, NCH, TE], BF16)
        axs = self.sb(stF, "axs" + tag, [RC, 3 * ST + 8], BF16)
        fb_sb = self.sb(stF, "fb" + tag, [128, NCH], F32)
        ps_stat = self.ps(stF, "ps_stat" + tag, [1, 2 * ST])
        ps_bcm = self.ps(stF, "ps_bcm" + tag, [128, ST])
        ps_bcr = self.ps(stF, "ps_bcr" + tag, [128, ST])
        ps_f = [self.ps(stF, f"ps_f{tag}{k}", [128, ST]) for k in range(3)]
        ps_ax = self.ps(stF, "ps_ax" + tag, [RC, ST])

        p.dma("sync", "cs", lambda s: s.dma_start(
            out=fb_sb[:], in_=self.i["fbias_r"]))
        t_lora = self.t_lora_done
        t_fb = p.tok("cs")

        stL = ExitStack()
        xbstr = [self.sb(stL, f"xb{tag}{k}", [128, NCH, LT], BF16)
                 for k in range(2)]
        t_prev = None
        t_lnd = {}
        t_statD = {}
        base_toks = [self.tok("st0"), self.tok("st1"), self.tok("st2"),
                     self.tok("T"), self.tok("V"), self.tok("S"), t_lora,
                     self.tok("ax")]
        t_xsD = {}
        for li, (c0, wt) in enumerate(ltiles):
            lp = li % 2
            a = list(base_toks)
            if li >= 2:
                a = [t_lnd[li - 2], t_statD[li - 2]]
            t_xsD[li] = p.dma("gpsimd", "xb0" if lp == 0 else "xb1",
                              lambda g, c0=c0, wt=wt, lp=lp: g.dma_start(
                                  out=xbstr[lp][:, :, 0:wt],
                                  in_=self.xh[:, :, c0:c0 + wt],
                                  max_dma_last_dim=4096), after=a)
            t_bce, t_stat, t_sq = self.ln_stats(
                (xbstr[lp], 0), (xbstr[lp], 0), lnx, c0, wt,
                [t_xsD[li]] + ([t_prev] if t_prev else []),
                ps_stat, ps_bcm, ps_bcr)
            t_statD[li] = t_stat
            t_lnd[li] = self.ln_apply(xbstr[lp], 0, lnx, c0, wt,
                                      [t_bce, t_stat])
            t_prev = t_lnd[li]
        t_ln_all = t_prev
        stL.close()

        t_axe = {}
        t_ax_prev = None
        for ti, (c0, wt) in enumerate(mtiles):
            a = [t_ln_all, t_lora]
            if t_ax_prev:
                a.append(t_ax_prev)
            t_ax = p.op("tensor", lambda t, c0=c0, wt=wt: [
                t.matmul(ps_ax[:, 0:wt], self.a_cat[:, kc, :],
                         lnx[:, kc, c0:c0 + wt], start=(kc == 0),
                         stop=(kc == NCH - 1)) for kc in range(NCH)][-1],
                after=a)
            t_axe[ti] = p.op("vector", lambda v, ti=ti, wt=wt: v.tensor_copy(
                axs[:, ti * ST:ti * ST + wt], ps_ax[:, 0:wt]), after=[t_ax])
            t_ax_prev = t_axe[ti]
        if first:
            t_axh = p.op("tensor", lambda t: ([
                t.matmul(ps_ax[:, 0:1], self.a_catL[:, kc, :],
                         lnx[:, kc, 0:1], start=(kc == 0),
                         stop=(kc == NCH - 1)) for kc in range(NCH)],
                [t.matmul(ps_ax[:, 4:5], self.a_catR[:, kc, :],
                          lnx[:, kc, TE - 1:TE], start=(kc == 0),
                          stop=(kc == NCH - 1)) for kc in range(NCH)])[1][-1],
                after=[t_ln_all, t_lora, t_ax_prev])
            t_axhe = p.op("vector", lambda v: v.tensor_copy(
                axs[:, 3 * ST:3 * ST + 8], ps_ax[:, 0:8]), after=[t_axh])
            t_axn = p.op("vector", lambda v: (
                v.tensor_scalar_mul(axs[:, 3 * ST + 2:3 * ST + 3],
                                    axs[:, 0:1], -1.0),
                v.tensor_scalar_mul(axs[:, 3 * ST + 6:3 * ST + 7],
                                    axs[:, 2 * ST + 1:2 * ST + 2], -1.0))[-1],
                after=[t_axhe])
            t_axall = t_axn
        else:
            t_axall = t_ax_prev

        stM = ExitStack()
        wcb = [self.sb(stM, f"fw{tag}{k}", [128, NCH, 128], BF16)
               for k in range(3)]
        x1c = [self.sb(stM, f"x1c{tag}{k}", [128, ST], F32) for k in range(2)]
        xoc = [self.sb(stM, f"xoc{tag}{k}", [128, ST], F32) for k in range(2)]
        wcs = ["wc0", "wc1"]
        t_w = {}
        t_mmL = []
        t_uL = []
        t_stL = []
        fi = 0
        nt = len(mtiles)
        for do in range(NCH):
            dp = do % 3
            a = ([t_mmL[(do - 2) * nt - 1]] if do >= 3 else
                 [t_lora, t_ln_all, t_axall])
            t_w[do] = p.dma("gpsimd", f"wc{do % 3}", lambda g, do=do, dp=dp:
                            g.dma_start(out=wcb[dp][:],
                                        in_=self.i["fker_pk"][:, do],
                                        max_dma_last_dim=8192), after=a)
            for ti, (c0, wt) in enumerate(mtiles):
                fp = fi % 3
                xp1 = fi % 2
                a = []
                if fi >= 2:
                    a.append(t_uL[fi - 2])
                else:
                    a += [t_ln_all, t_axall]
                t_x1 = p.dma("scalar", "xr0" if xp1 == 0 else "xr1",
                             lambda s, do=do, c0=c0, wt=wt, xp1=xp1:
                             s.dma_start(out=x1c[xp1][:, 0:wt],
                                         in_=self.xh[:, do, c0:c0 + wt]),
                             after=a)

                def fmm(t, do=do, dp=do % 3, ti=ti, wt=wt, fp=fp, c0=c0):
                    for kc in range(NCH):
                        t.matmul(ps_f[fp][:, 0:wt], wcb[dp][:, kc, :],
                                 lnx[:, kc, c0:c0 + wt], start=(kc == 0),
                                 stop=False)
                    last = t.matmul(ps_f[fp][:, 0:wt],
                                    self.b_cat[:, do * 128:(do + 1) * 128],
                                    axs[:, ti * ST:ti * ST + wt],
                                    start=False,
                                    stop=not (first and ti in (0, 2)))
                    if first and ti == 0:
                        t.matmul(ps_f[fp][:, 0:1],
                                 self.b_cat[:, do * 128:(do + 1) * 128],
                                 axs[:, 3 * ST + 2:3 * ST + 3],
                                 start=False, stop=False)
                        last = t.matmul(
                            ps_f[fp][:, 0:1],
                            self.b_catL[:, do * 128:(do + 1) * 128],
                            axs[:, 3 * ST:3 * ST + 1], start=False, stop=True)
                    if first and ti == 2:
                        t.matmul(ps_f[fp][:, 1:2],
                                 self.b_cat[:, do * 128:(do + 1) * 128],
                                 axs[:, 3 * ST + 6:3 * ST + 7],
                                 start=False, stop=False)
                        last = t.matmul(
                            ps_f[fp][:, 1:2],
                            self.b_catR[:, do * 128:(do + 1) * 128],
                            axs[:, 3 * ST + 4:3 * ST + 5], start=False,
                            stop=True)
                    return last
                a = [t_w[do], t_ln_all, t_axall]
                if fi >= 3:
                    a.append(t_uL[fi - 3])
                t_mmL.append(p.op("tensor", fmm, after=a))
                a = [t_mmL[fi], t_x1, t_fb]
                if fi >= 2:
                    a.append(t_stL[fi - 2])
                t_uL.append(p.op("vector", lambda v, do=do, wt=wt, fp=fp,
                                 xp1=xp1:
                                 v.scalar_tensor_tensor(
                                     xoc[xp1][:, 0:wt], ps_f[fp][:, 0:wt],
                                     fb_sb[:, do:do + 1], x1c[xp1][:, 0:wt],
                                     op0=OP.add, op1=OP.add), after=a))
                dst_ = self.xh if first else self.y_t
                dco = c0 if first else c0 - 1
                t_stL.append(p.dma("sync", "st0" if xp1 == 0 else "st1",
                    lambda s, do=do, dco=dco, wt=wt, xp1=xp1, dst_=dst_:
                    s.dma_start(out=dst_[:, do, dco:dco + wt],
                                in_=xoc[xp1][:, 0:wt]), after=[t_uL[fi]]))
                fi += 1
        stM.close()
        stF.close()

    def build_lora(self):
        p = self
        stO = ExitStack()
        g3bc = self.sb(stO, "g3bc", [128, 3 * NL], F32, side="right")
        g3row = self.sb(stO, "g3row", [1, 3 * NL], F32, side="right")
        aacc = self.sb(stO, "aacc", [128, 3, NCH, RK], F32, side="right")
        bacc = self.sb(stO, "bacc", [128, 3, 1024], F32, side="right")
        anb = [self.sb(stO, f"anb{k}", [128, NCH, RK], BF16, side="right")
               for k in range(2)]
        bnb = [self.sb(stO, f"bnb{k}", [128, 1024], BF16, side="right")
               for k in range(2)]
        ps_g = self.ps(stO, "ps_g", [128, 3 * NL])

        p.dma("sync", "cs", lambda s: s.dma_start(out=g3row[:],
                                                  in_=self.i["g3_row"]))
        t_g = p.tok("cs")
        t_gb = p.op("tensor", lambda t: t.matmul(
            ps_g[:], self.ones_row[:], g3row[:], start=True, stop=True),
            after=[t_g, self.t_init, self.tok("T"), self.tok("S"),
                   self.tok("V")])
        self.t_psg_guard = t_gb
        t_gbe = p.op("vector", lambda v: v.tensor_copy(g3bc[:], ps_g[:]),
                     after=[t_gb])
        lr = ["lr0", "lr1"]
        t_ld = {}
        t_acc = {}
        prev = None
        for n in range(NL):
            npr = n % 2
            a = [t_acc[n - 2]] if n >= 2 else [self.tok("T"), self.tok("S"),
                                               self.tok("V")]
            p.dma("gpsimd", lr[npr], lambda g, n=n, npr=npr:
                  g.dma_start(out=anb[npr][:], in_=self.i["a_pk"][:, n],
                              max_dma_last_dim=8192), after=a)
            t_ld2 = p.dma("gpsimd", lr[npr], lambda g, n=n, npr=npr:
                          g.dma_start(out=bnb[npr][:],
                                      in_=self.i["b_pk"][:, n],
                                      max_dma_last_dim=8192))
            t_ld[n] = t_ld2
            for gi in range(3):
                col = gi * NL + n
                if n == 0:
                    t_a = p.op("vector", lambda v, gi=gi, npr=npr, col=col:
                               v.tensor_scalar_mul(
                                   aacc[:, gi], anb[npr][:],
                                   g3bc[:, col:col + 1]),
                               after=[t_ld[n], t_gbe] +
                                     ([prev] if prev else []))
                    t_b = p.op("vector", lambda v, gi=gi, npr=npr, col=col:
                               v.tensor_scalar_mul(
                                   bacc[:, gi], bnb[npr][:],
                                   g3bc[:, col:col + 1]),
                               after=[t_ld[n], t_a])
                else:
                    t_a = p.op("vector", lambda v, gi=gi, npr=npr, col=col:
                               v.scalar_tensor_tensor(
                                   aacc[:, gi], anb[npr][:],
                                   g3bc[:, col:col + 1], aacc[:, gi],
                                   op0=OP.mult, op1=OP.add),
                               after=[t_ld[n], t_gbe] +
                                     ([prev] if prev else []))
                    t_b = p.op("vector", lambda v, gi=gi, npr=npr, col=col:
                               v.scalar_tensor_tensor(
                                   bacc[:, gi], bnb[npr][:],
                                   g3bc[:, col:col + 1], bacc[:, gi],
                                   op0=OP.mult, op1=OP.add),
                               after=[t_ld[n], t_a])
                prev = t_b
            t_acc[n] = prev
        t_cp = []
        for dstA, gi in ((self.a_cat, 1), (self.a_catL, 0), (self.a_catR, 2)):
            t_cp.append(p.op("vector", lambda v, dstA=dstA, gi=gi:
                             v.tensor_copy(dstA[:, :, 0:RK], aacc[:, gi]),
                             after=[t_acc[NL - 1]]))
        t_dma = []
        for k, (dstA, gi) in enumerate(((self.a_cat, 1), (self.a_catL, 0),
                                        (self.a_catR, 2))):
            t_dma.append(p.dma("gpsimd", "ax", lambda g, dstA=dstA:
                               g.dma_start(out=dstA[:, :, RK:RC],
                                           in_=self.i["a_sh"],
                                           max_dma_last_dim=8192),
                               after=[t_cp[k]]))
        for dstB, gi in ((self.b_cat, 1), (self.b_catL, 0), (self.b_catR, 2)):
            for q in range(4):
                t_dma.append(p.dma("gpsimd", "ax",
                    lambda g, dstB=dstB, gi=gi, q=q: g.dma_start(
                        out=dstB[0:RK, q * 1024:(q + 1) * 1024],
                        in_=bacc[q * RK:(q + 1) * RK, gi],
                        max_dma_last_dim=4096), after=[t_acc[NL - 1]]))
            t_dma.append(p.dma("gpsimd", "ax", lambda g, dstB=dstB:
                               g.dma_start(out=dstB[RK:RC, :],
                                           in_=self.i["b_sh"],
                                           max_dma_last_dim=8192)))
        stO.close()
        self.t_lora_done = self.tok("ax")
        return self.t_lora_done

    def p3(self):
        p = self
        st3 = ExitStack()
        st_ln3 = ExitStack()
        hm_sb = self.sb(st3, "hm_sb", [128, 2], F32)
        cb_sb = self.sb(st3, "cb_sb", [128, 3 * WCH], F32)
        bqb_sb = self.sb(st3, "bqb_sb", [128, WCH], F32)
        bkb_sb = self.sb(st3, "bkb_sb", [128, WCH], F32)
        bvb_sb = self.sb(st3, "bvb_sb", [1, WB], F32)
        bo_sb = self.sb(st3, "bo_sb", [128, WCH], F32)
        pjb_sb = self.sb(st3, "pjb_sb", [128, NCH], F32)
        mask_sb = self.sb(st3, "mask_sb", [128, BLK], F32)
        ps_c = [self.ps(st3, f"ps_c{k}", [128, ST]) for k in range(2)]

        for nm, dst in (("hmask", hm_sb), ("conv_b_r", cb_sb),
                        ("bqb_r", bqb_sb), ("bkb_r", bkb_sb),
                        ("bvb_row", bvb_sb), ("bo_r", bo_sb),
                        ("projb_r", pjb_sb), ("mask8", mask_sb)):
            p.dma("sync", "cs", lambda s, nm=nm, dst=dst: s.dma_start(
                out=dst[:], in_=self.i[nm]))
        t_c7 = t_c0 = t_c1 = t_c2 = t_c3 = t_c4 = t_c5 = t_c6 = p.tok("cs")
        ln3 = self.sb(st_ln3, "ln3", [128, NCH, TE], BF16)
        stL = ExitStack()
        xbstr = [self.sb(stL, f"xb3{k}", [128, NCH, LT], BF16)
                 for k in range(2)]
        ps_stat = self.ps(stL, "ps_stat3", [1, 2 * ST])
        ps_bcm = self.ps(stL, "ps_bcm3", [128, ST])
        ps_bcr = self.ps(stL, "ps_bcr3", [128, ST])
        ltiles = [(k * LT, LT) for k in range(T // LT)] + [(T, 2)]
        t_prev = None
        t_statD = {}
        t_lnd3 = {}
        base_toks = [self.tok("st0"), self.tok("st1"), self.tok("st2"),
                     self.tok("T"), self.tok("V"), self.tok("S")]
        for li, (c0, wt) in enumerate(ltiles):
            lp = li % 2
            a = list(base_toks) if li < 2 else [t_lnd3[li - 2],
                                               t_statD[li - 2]]
            t_xs = p.dma("gpsimd", "xb0" if lp == 0 else "xb1",
                         lambda g, c0=c0, wt=wt, lp=lp: g.dma_start(
                             out=xbstr[lp][:, :, 0:wt],
                             in_=self.xh[:, :, c0:c0 + wt],
                             max_dma_last_dim=4096), after=a)
            t_bce, t_stat, t_sq = self.ln_stats(
                (xbstr[lp], 0), (xbstr[lp], 0), ln3, c0, wt,
                [t_xs] + ([t_prev] if t_prev else []),
                ps_stat, ps_bcm, ps_bcr)
            t_statD[li] = t_stat
            t_lnd3[li] = self.ln_apply(xbstr[lp], 0, ln3, c0, wt,
                                       [t_bce, t_stat])
            t_prev = t_lnd3[li]
        stL.close()
        t_hm = p.op("vector", lambda v: (
            v.tensor_scalar_mul(ln3[:, :, 0:1], ln3[:, :, 0:1],
                                hm_sb[:, 0:1]),
            v.tensor_scalar_mul(ln3[:, :, TE - 1:TE], ln3[:, :, TE - 1:TE],
                                hm_sb[:, 1:2]))[-1], after=[t_prev, t_c7])
        t_ln3 = t_hm

        stCV = ExitStack()
        cwb = [self.sb(stCV, f"cwb{k}", [128, NCH, 128], BF16, side="right")
               for k in range(3)]
        qcv = self.sb(stCV, "qcv", [128, WCH, T], BF16, side="right")
        kcv = self.sb(stCV, "kcv", [128, WCH, T], BF16, side="right")
        vcv = self.sb(stCV, "vcv", [128, WCH, T], BF16, side="right")
        cwq = ["wc0", "wc1"]
        t_cwL = []
        t_cmmL = []
        t_cevL = []
        ci = 0
        for o in range(3 * WCH):
            dstbuf = (qcv, kcv, vcv)[o // WCH]
            oc = o % WCH
            for tap in range(3):
                wi_ = o * 3 + tap
                opr = wi_ % 3
                a = [t_cmmL[2 * (wi_ - 3) + 1]] if wi_ >= 3 else [t_ln3]
                t_cwL.append(p.dma("gpsimd", f"wc{wi_ % 3}",
                             lambda g, o=o, tap=tap, opr=opr: g.dma_start(
                                 out=cwb[opr][:],
                                 in_=self.i["conv_pk"][:, o, tap],
                                 max_dma_last_dim=8192), after=a))
                for qt in range(2):
                    fp = ci % 2

                    def cmm(t, opr=opr, tap=tap, qt=qt, fp=fp):
                        for kc in range(NCH):
                            last = t.matmul(
                                ps_c[fp][:], cwb[opr][:, kc, :],
                                ln3[:, kc, qt * ST + tap:qt * ST + tap + ST],
                                start=(tap == 0 and kc == 0),
                                stop=(tap == 2 and kc == NCH - 1))
                        return last
                    a = [t_cwL[wi_], t_ln3]
                    if tap == 0 and o >= 1:
                        a.append(t_cevL[(o - 1) * 2 + qt])
                    t_cmmL.append(p.op("tensor", cmm, after=a))
                    if tap == 2:
                        t_cevL.append(p.op("scalar",
                            lambda sc, o=o, oc=oc, qt=qt, fp=fp,
                            dstbuf=dstbuf: sc.activation(
                                dstbuf[:, oc, qt * ST:(qt + 1) * ST],
                                ps_c[fp][:], AF.Identity,
                                bias=cb_sb[:, o:o + 1]),
                            after=[t_cmmL[-1], t_c0]))
                    ci += 1
        t_conv_done = self.tok("S")
        st_ln3.close()

        stH = ExitStack()
        wsm = [self.sb(stH, f"wsm{k}", [128, WCH, 128], BF16)
               for k in range(3)]
        wvb_sb = self.sb(stH, "wvb_sb", [128, WCH, HB * DHB], BF16)
        qhx = self.sb(stH, "qhx", [128, WCH, T], BF16)
        khx = self.sb(stH, "khx", [128, WCH, T], BF16)
        vhx = self.sb(stH, "vhx", [128, NBLK, WB], BF16)
        t_wvb = p.dma("gpsimd", "wres", lambda g: g.dma_start(
            out=wvb_sb[:], in_=self.i["wvb"].rearrange("(n p) d -> p n d",
                                                       p=128),
            max_dma_last_dim=4096), after=[self.tok("T")])
        t_hwL = {}
        t_hmmL = []
        t_hevL = []
        hi = 0
        for which, (wpk, srcbuf, dstbuf, bias_sb, bias_tok) in enumerate((
                ("wqb_pk", qcv, qhx, bqb_sb, t_c1),
                ("wkb_pk", kcv, khx, bkb_sb, t_c2))):
            for he in range(WCH):
                idx = which * WCH + he
                hp = idx % 3
                a = [t_hmmL[-5]] if idx >= 3 else [self.tok("T")]
                t_hwL[idx] = p.dma("gpsimd", f"wc{idx % 3}",
                                   lambda g, wpk=wpk, he=he, hp=hp:
                                   g.dma_start(out=wsm[hp][:],
                                               in_=self.i[wpk][:, he],
                                               max_dma_last_dim=8192),
                                   after=a)
                for qt in range(2):
                    fp = hi % 2

                    def hmm(t, hp=hp, qt=qt, fp=fp, srcbuf=srcbuf):
                        for kc in range(WCH):
                            last = t.matmul(
                                ps_c[fp][:], wsm[hp][:, kc, :],
                                srcbuf[:, kc, qt * ST:(qt + 1) * ST],
                                start=(kc == 0), stop=(kc == WCH - 1))
                        return last
                    a = [t_hwL[idx], t_conv_done]
                    if hi >= 2:
                        a.append(t_hevL[hi - 2])
                    t_hmmL.append(p.op("tensor", hmm, after=a))
                    t_hevL.append(p.op("scalar", lambda sc, he=he, qt=qt,
                                       fp=fp, dstbuf=dstbuf, bias_sb=bias_sb:
                                       sc.activation(
                                           dstbuf[:, he, qt * ST:(qt + 1) * ST],
                                           ps_c[fp][:], AF.Identity,
                                           bias=bias_sb[:, he:he + 1]),
                                       after=[t_hmmL[hi], bias_tok]))
                    hi += 1
        t_vmmL = []
        t_vevL = []
        vi = 0
        for tc in range(NBLK):
            for hf in range(2):
                fp = vi % 2

                def vmm(t, tc=tc, hf=hf, fp=fp):
                    t.matmul(ps_c[fp][:], self.ones_row[:],
                             bvb_sb[0:1, hf * 512:(hf + 1) * 512], start=True,
                             stop=False)
                    for kc in range(WCH):
                        last = t.matmul(ps_c[fp][:],
                                        vcv[:, kc, tc * 128:(tc + 1) * 128],
                                        wvb_sb[:, kc, hf * 512:(hf + 1) * 512],
                                        start=False, stop=(kc == WCH - 1))
                    return last
                a = [t_conv_done, t_c3, t_wvb]
                if vi >= 2:
                    a.append(t_vevL[vi - 2])
                if vi == 0 and t_hevL:
                    a.append(t_hevL[-1])
                t_vmmL.append(p.op("tensor", vmm, after=a))
                t_vevL.append(p.op("scalar", lambda sc, tc=tc, hf=hf, fp=fp:
                                   sc.activation(
                                       vhx[:, tc, hf * 512:(hf + 1) * 512],
                                       ps_c[fp][:], AF.Copy),
                                   after=[t_vmmL[vi]]))
                vi += 1
        t_heads_done = self.tok("S")
        stCV.close()

        stN = ExitStack()
        nsq = self.sb(stN, "nsq", [128, WCH, T], BF16)
        ps_n3 = self.ps(stN, "ps_n3", [1, 1024])
        t_nprev = None
        t_nrd = None
        for which, (buf, dst) in enumerate(((qhx, self.qnb), (khx, self.knb))):
            a = [t_heads_done]
            if t_nprev:
                a.append(t_nprev)
            t_sqx = p.op("scalar", lambda sc, buf=buf: sc.activation(
                nsq[:], buf[:], AF.Square), after=a)
            for h in range(HB):
                base = 64 * (h % 2)
                ch = h // 2
                a = [t_sqx]
                if t_nrd:
                    a.append(t_nrd)
                t_nmm = p.op("tensor", lambda t, base=base, ch=ch: (
                    t.matmul(ps_n3[0:1, 0:512],
                             self.ones_col[base:base + 64, :],
                             nsq[base:base + 64, ch, 0:512], start=True,
                             stop=True),
                    t.matmul(ps_n3[0:1, 512:1024],
                             self.ones_col[base:base + 64, :],
                             nsq[base:base + 64, ch, 512:1024], start=True,
                             stop=True))[-1], after=a)
                t_nrd = p.op("vector", lambda v, h=h, dst=dst: v.reduce_max(
                    dst[0:1, h:h + 1], ps_n3[0:1, :], axis=AX.X),
                    after=[t_nmm])
            t_nprev = t_nmm
        t_mb1 = p.op("vector", lambda v: v.tensor_tensor(
            self.mrow[0:1, 0:HB], self.qnb[:], self.knb[:], op=OP.mult),
            after=[t_nrd])
        t_mb2 = p.op("scalar", lambda sc: sc.activation(
            self.mrow[0:1, 0:HB], self.mrow[0:1, 0:HB], AF.Ln,
            bias=self.eps_t[:]), after=[t_mb1, self.tok("S")])
        t_mb3 = p.op("scalar", lambda sc: sc.activation(
            self.mrow[0:1, 0:HB], self.mrow[0:1, 0:HB], AF.Exp, scale=0.5),
            after=[t_mb2])
        t_mb4 = p.op("scalar", lambda sc: sc.activation(
            self.msqrow[0:1, 0:HB], self.mrow[0:1, 0:HB], AF.Copy,
            scale=-ISQB, bias=-1.0), after=[t_mb3])
        t_mb5 = p.op("tensor", lambda t: t.matmul(
            ps_c[0][:, 0:HB], self.ones_row[:], self.msqrow[0:1, 0:HB],
            start=True, stop=True), after=[t_mb4, self.tok("T")])
        t_mb6 = p.op("vector", lambda v: v.tensor_copy(
            self.negMb[:], ps_c[0][:, 0:HB]), after=[t_mb5])
        stN.close()

        stB = ExitStack()
        sm_sb = self.sb(stB, "sm_sb", [128, NBLK * BLK], F32, side="right")
        pbx = self.sb(stB, "pbx", [128, NBLK * BLK], BF16, side="right")
        rcb_row = self.sb(stB, "rcb_row", [128, NBLK * BLK], F32, side="right")
        rcb_sb = self.sb(stB, "rcb_sb", [128, NBLK * BLK], F32, side="right")
        ohx = self.sb(stB, "ohx", [128, WCH, T], BF16, side="right")
        ps_sb3 = self.ps(stB, "ps_sb3", [128, NBLK * BLK])
        ps_db3 = self.ps(stB, "ps_db3", [128, NBLK * BLK])
        ps_pv3 = self.ps(stB, "ps_pv3", [128, NBLK * BLK])
        mb_ap = mask_sb[:]
        mask_bc = bass.AP(tensor=mb_ap.tensor, offset=mb_ap.offset,
                          ap=[[mb_ap.ap[0][0], 128], [0, NBLK], [1, BLK]])
        t_bs = None
        t_msk = None
        t_pb = None
        t_bd = None
        t_br1 = None
        t_br2 = None
        t_br4 = None
        t_bp = None
        t_orc = None
        for h in range(HB):
            base = 64 * (h % 2)
            ch = h // 2

            def bsm(t, base=base, ch=ch):
                for b in range(NBLK):
                    last = t.matmul(
                        ps_sb3[:, b * BLK:(b + 1) * BLK],
                        khx[base:base + 64, ch, b * BLK:(b + 1) * BLK],
                        qhx[base:base + 64, ch, b * BLK:(b + 1) * BLK],
                        start=True, stop=True)
                return last
            a = [t_heads_done, t_mb6]
            if t_msk:
                a.append(t_msk)
            t_bs = p.op("tensor", bsm, after=a)
            a = [t_bs, t_c6]
            if t_pb:
                a.append(t_pb)
            t_msk = p.op("vector", lambda v: v.tensor_tensor(
                sm_sb[:].rearrange("p (b c) -> p b c", b=NBLK),
                ps_sb3[:].rearrange("p (b c) -> p b c", b=NBLK),
                mask_bc, op=OP.add), after=a)
            a = [t_msk]
            if t_bd:
                a.append(t_bd)
            if t_bp:
                a.append(t_bp)
            t_pb = p.op("scalar", lambda sc, h=h: sc.activation(
                pbx[:], sm_sb[:], AF.Exp, scale=ISQB,
                bias=self.negMb[:, h:h + 1]), after=a)

            def bdn(t):
                for b in range(NBLK):
                    last = t.matmul(ps_db3[:, b * BLK:(b + 1) * BLK],
                                    self.ones_mat[:],
                                    pbx[:, b * BLK:(b + 1) * BLK],
                                    start=True, stop=True)
                return last
            a = [t_pb]
            if t_br1:
                a.append(t_br1)
            if t_br4:
                a.append(t_br4)
            t_bd = p.op("tensor", bdn, after=a)
            a = [t_bd]
            if t_br2:
                a.append(t_br2)
            t_br1 = p.op("scalar", lambda sc: sc.activation(
                rcb_row[:], ps_db3[:], AF.Ln), after=a)
            a = [t_br1]
            if t_br4:
                a.append(t_br4)
            t_br2 = p.op("scalar", lambda sc: sc.activation(
                rcb_sb[:], rcb_row[:], AF.Exp, scale=-1.0), after=a)
            t_br4 = t_br2

            def bpv(t, h=h, base=base):
                for b in range(NBLK):
                    last = t.matmul(
                        ps_pv3[base:base + 64, b * BLK:(b + 1) * BLK],
                        vhx[:, b, h * DHB:(h + 1) * DHB],
                        pbx[:, b * BLK:(b + 1) * BLK], start=True, stop=True)
                return last
            a = [t_pb]
            if t_orc:
                a.append(t_orc)
            t_bp = p.op("tensor", bpv, after=a)
            t_orc = p.op("vector", lambda v, base=base, ch=ch:
                         v.tensor_tensor(ohx[base:base + 64, ch, :],
                                         ps_pv3[base:base + 64, :],
                                         rcb_sb[base:base + 64, :],
                                         op=OP.mult), after=[t_bp, t_br4])
        t_attn_done = self.tok("V")
        stH.close()
        stB_keepohx = ohx
        stW = ExitStack()
        wsm2 = [self.sb(stW, f"wsm2{k}", [128, WCH, 128], BF16)
                for k in range(3)]
        o1x = self.sb(stW, "o1x", [128, WCH, T], BF16)
        t_woL = {}
        t_womL = []
        t_woeL = []
        wi = 0
        for wc_ in range(WCH):
            wp = wc_ % 3
            a = [t_womL[-5]] if wc_ >= 3 else [self.tok("T")]
            t_woL[wc_] = p.dma("gpsimd", f"wc{wc_ % 3}",
                               lambda g, wc_=wc_, wp=wp:
                               g.dma_start(out=wsm2[wp][:],
                                           in_=self.i["wo_pk"][:, wc_],
                                           max_dma_last_dim=8192), after=a)
            for qt in range(2):
                fp = wi % 2

                def womm(t, wp=wp, qt=qt, fp=fp):
                    for kc in range(WCH):
                        last = t.matmul(ps_c[fp][:], wsm2[wp][:, kc, :],
                                        ohx[:, kc, qt * ST:(qt + 1) * ST],
                                        start=(kc == 0), stop=(kc == WCH - 1))
                    return last
                a = [t_woL[wc_], t_attn_done]
                if wi >= 2:
                    a.append(t_woeL[wi - 2])
                t_womL.append(p.op("tensor", womm, after=a))
                t_woeL.append(p.op("scalar", lambda sc, wc_=wc_, qt=qt, fp=fp:
                                   sc.activation(
                                       o1x[:, wc_, qt * ST:(qt + 1) * ST],
                                       ps_c[fp][:], AF.Identity,
                                       bias=bo_sb[:, wc_:wc_ + 1]),
                                   after=[t_womL[wi], t_c4]))
                wi += 1
        t_wo_done = self.tok("S")
        stB.close()

        stP = ExitStack()
        wsm3 = [self.sb(stP, f"wsm3{k}", [128, WCH, 128], BF16, side="right")
                for k in range(3)]
        x2c = [self.sb(stP, f"x2c{k}", [128, ST], F32, side="right")
               for k in range(2)]
        xoc = [self.sb(stP, f"xoc3{k}", [128, ST], F32, side="right")
               for k in range(2)]
        t_pwL = {}
        t_pmmL = []
        t_puL = []
        t_pstL = []
        pi = 0
        for do in range(NCH):
            dp = do % 3
            a = [t_pmmL[-5]] if do >= 3 else [self.tok("T")]
            t_pwL[do] = p.dma("gpsimd", f"wc{do % 3}", lambda g, do=do, dp=dp:
                              g.dma_start(out=wsm3[dp][:],
                                          in_=self.i["projw_pk"][:, do],
                                          max_dma_last_dim=8192), after=a)
            for qt in range(2):
                fp = pi % 2
                a = []
                if pi >= 2:
                    a.append(t_puL[pi - 2])
                else:
                    a += [self.tok("T"), self.tok("V"), self.tok("S")]
                t_x2 = p.dma("sync", "xr0" if fp == 0 else "xr1",
                             lambda s, do=do, qt=qt, fp=fp: s.dma_start(
                                 out=x2c[fp][:],
                                 in_=self.xh[:, do, 1 + qt * ST:1 + (qt + 1) * ST]),
                             after=a)

                def pmm(t, dp=dp, qt=qt, fp=fp):
                    for kc in range(WCH):
                        last = t.matmul(ps_c[fp][:], wsm3[dp][:, kc, :],
                                        o1x[:, kc, qt * ST:(qt + 1) * ST],
                                        start=(kc == 0), stop=(kc == WCH - 1))
                    return last
                a = [t_pwL[do], t_wo_done]
                if pi >= 2:
                    a.append(t_puL[pi - 2])
                t_pmmL.append(p.op("tensor", pmm, after=a))
                a = [t_pmmL[pi], t_x2, t_c5]
                if pi >= 2:
                    a.append(t_pstL[pi - 2])
                t_puL.append(p.op("vector", lambda v, do=do, fp=fp:
                                  v.scalar_tensor_tensor(
                                      xoc[fp][:], ps_c[fp][:],
                                      pjb_sb[:, do:do + 1], x2c[fp][:],
                                      op0=OP.add, op1=OP.add), after=a))
                t_pstL.append(p.dma("sync", "st0" if fp == 0 else "st1",
                    lambda s, do=do, qt=qt, fp=fp: s.dma_start(
                        out=self.xh[:, do, 1 + qt * ST:1 + (qt + 1) * ST],
                        in_=xoc[fp][:]), after=[t_puL[pi]]))
                pi += 1
        stW.close()
        stP.close()
        st3.close()


def _host_prep(inputs, core):
    x = np.asarray(inputs["L0_attn_a_x"])
    b, s0 = core // 4, (core % 4) * T
    out = {}
    xt = np.ascontiguousarray(x[b].T)
    ext = np.roll(xt, -(s0 - 1), axis=1)[:, 0:TE]
    out["xown"] = np.ascontiguousarray(
        ext.reshape(NCH, 128, TE).transpose(1, 0, 2))
    inv = (1.0 / 10000.0 ** (np.arange(0, DH, 2, dtype=np.float32) / DH)
           ).astype(np.float32)
    ang = np.arange(SEQ, dtype=np.float32)[:, None] * inv[None, :]
    cos = np.cos(ang).astype(np.float32).T
    sin = np.sin(ang).astype(np.float32).T
    cosd = np.concatenate([cos, cos], axis=0)
    sins = np.concatenate([-sin, sin], axis=0)
    out["cosd_own"] = np.ascontiguousarray(
        np.roll(cosd, -(s0 - 1), axis=1)[:, 0:TE])
    out["sins_own"] = np.ascontiguousarray(
        np.roll(sins, -(s0 - 1), axis=1)[:, 0:TE])
    out["wk"] = np.asarray(inputs["L0_attn_a_wk"])
    out["wv"] = np.asarray(inputs["L0_attn_a_wv"])
    wq = np.asarray(inputs["L0_attn_a_wq"])
    out["wq_pk"] = np.ascontiguousarray(
        wq.reshape(NCH, 128, H, 128).transpose(1, 2, 0, 3))
    out["bq_r"] = np.ascontiguousarray(
        np.asarray(inputs["L0_attn_a_bq"]).reshape(H, 128).T)
    out["bk_r"] = np.ascontiguousarray(
        np.asarray(inputs["L0_attn_a_bk"]).reshape(KVH, 128).T)
    out["bv_row"] = np.asarray(inputs["L0_attn_a_bv"]).reshape(1, KVH * DH)
    cw = np.asarray(inputs["L0_attn_b_conv_w"])
    out["conv_pk"] = np.ascontiguousarray(
        cw.reshape(3, NCH, 128, 3 * WCH, 128).transpose(2, 3, 0, 1, 4))
    out["conv_b_r"] = np.ascontiguousarray(
        np.asarray(inputs["L0_attn_b_conv_b"]).reshape(3 * WCH, 128).T)
    for nm, key in (("wqb_pk", "L0_attn_b_wq"), ("wkb_pk", "L0_attn_b_wk")):
        w = np.asarray(inputs[key]).reshape(WB, HB * DHB)
        out[nm] = np.ascontiguousarray(
            w.reshape(WCH, 128, WCH, 128).transpose(1, 2, 0, 3))
    out["wvb"] = np.ascontiguousarray(
        np.asarray(inputs["L0_attn_b_wv"]).reshape(WB, HB * DHB))
    out["bqb_r"] = np.ascontiguousarray(
        np.asarray(inputs["L0_attn_b_bq"]).reshape(WCH, 128).T)
    out["bkb_r"] = np.ascontiguousarray(
        np.asarray(inputs["L0_attn_b_bk"]).reshape(WCH, 128).T)
    out["bvb_row"] = np.asarray(inputs["L0_attn_b_bv"]).reshape(1, HB * DHB)
    wo = np.asarray(inputs["L0_attn_b_wo"]).reshape(HB * DHB, WB)
    out["wo_pk"] = np.ascontiguousarray(
        wo.reshape(WCH, 128, WCH, 128).transpose(1, 2, 0, 3))
    out["bo_r"] = np.ascontiguousarray(
        np.asarray(inputs["L0_attn_b_bo"]).reshape(WCH, 128).T)
    pw = np.asarray(inputs["L0_attn_b_proj_w"])
    out["projw_pk"] = np.ascontiguousarray(
        pw.reshape(WCH, 128, NCH, 128).transpose(1, 2, 0, 3))
    out["projb_r"] = np.ascontiguousarray(
        np.asarray(inputs["L0_attn_b_proj_b"]).reshape(NCH, 128).T)
    fk = np.asarray(inputs["L0_ffn_kernel"])
    out["fker_pk"] = np.ascontiguousarray(
        fk.reshape(NCH, 128, NCH, 128).transpose(1, 2, 0, 3))
    out["fbias_r"] = np.ascontiguousarray(
        np.asarray(inputs["L0_ffn_bias"]).reshape(NCH, 128).T)
    la = np.asarray(inputs["L0_ffn_lora_A"])
    out["a_pk"] = np.ascontiguousarray(
        la.reshape(NL, RK, NCH, 128).transpose(3, 0, 2, 1))
    lb = np.asarray(inputs["L0_ffn_lora_B"])
    out["b_pk"] = np.ascontiguousarray(
        lb.transpose(0, 2, 1).reshape(NL, RK, 4, 1024)
        .transpose(2, 1, 0, 3).reshape(128, NL, 1024))
    out["a_sh"] = np.ascontiguousarray(
        np.asarray(inputs["L0_ffn_lora_A_shared"]).T.reshape(NCH, 128, RK)
        .transpose(1, 0, 2))
    out["b_sh"] = np.ascontiguousarray(
        np.asarray(inputs["L0_ffn_lora_B_shared"]).T)
    gate = np.asarray(inputs["L0_ffn_gate"])
    g3 = np.zeros((3, NL), dtype=np.float32)
    g3[1] = gate[core]
    if s0 > 0:
        g3[0] = gate[core - 1]
    if s0 + T < SEQ:
        g3[2] = gate[core + 1]
    out["g3_row"] = np.ascontiguousarray(g3.reshape(1, 3 * NL))
    hm = np.zeros((128, 2), dtype=np.float32)
    hm[:, 0] = 1.0 if s0 > 0 else 0.0
    hm[:, 1] = 1.0 if s0 + T < SEQ else 0.0
    out["hmask"] = hm
    mk = np.where(np.arange(BLK)[:, None] <= np.arange(BLK)[None, :], 0.0,
                  NEGBIG).astype(np.float32)
    out["mask8"] = np.ascontiguousarray(mk)
    for k in out:
        out[k] = np.ascontiguousarray(out[k], dtype=np.float32)
    return out


_NC_CACHE = {}


def get_nc(phases=("p0", "p1", "p2", "p3", "p4"), debug=()):
    key = (tuple(phases), tuple(debug))
    if key not in _NC_CACHE:
        kb = KB(phases=phases, debug=debug)
        _NC_CACHE[key] = kb.build()
    return _NC_CACHE[key]


def kernel(**inputs):
    nc = get_nc()
    in_maps = [_host_prep(inputs, c) for c in range(NCORES)]
    res = run_bass_kernel_spmd(nc, in_maps, core_ids=list(range(NCORES)))
    y = np.empty((BATCH, SEQ, D), dtype=np.float32)
    for c in range(NCORES):
        b, s0 = c // 4, (c % 4) * T
        yt = res.results[c]["y_t"]
        y[b, s0:s0 + T, :] = yt.transpose(2, 1, 0).reshape(T, D)
    return y


# version 16
# speedup vs baseline: 1.0094x; 1.0094x over previous
import math
from contextlib import ExitStack

import numpy as np

import concourse.bass as bass
import concourse.mybir as mybir
from concourse.bass_utils import run_bass_kernel_spmd

F32 = mybir.dt.float32
BF16 = mybir.dt.bfloat16
AF = mybir.ActivationFunctionType
OP = mybir.AluOpType
AX = mybir.AxisListType

D = 4096
NCH = D // 128
SEQ = 4096
BATCH = 2
NCORES = 8
H = 32
KVH = 8
DH = 128
WB = 1024
WCH = WB // 128
HB = 16
DHB = 64
BLK = 128
NBLK = 8
RK = 32
NL = 16
RC = 64
T = 1024
TE = T + 2
ST = 512
LT = 256
NSTRIP = SEQ // ST
EPS = 1e-5
ISQD = 1.0 / math.sqrt(DH)
ISQB = 1.0 / math.sqrt(DHB)
NEGBIG = -8.0e9

ENGINES = ("tensor", "vector", "scalar", "sync", "gpsimd")
ESEM = {"tensor": "T", "vector": "V", "scalar": "S", "gpsimd": "G",
        "sync": "Y"}


class KB:
    def __init__(self, phases=("p0", "p1", "p2", "p3", "p4"), debug=()):
        self.phases = phases
        self.debug = debug
        self.nc = bass.Bass("TRN2", target_bir_lowering=False)
        self.es = ExitStack()
        self.prog = {e: [] for e in ENGINES}
        self.semh = {}
        self.cnt = {}

    def sem(self, *names):
        for n in names:
            if n not in self.semh:
                self.semh[n] = self.es.enter_context(self.nc.semaphore(n))
                self.cnt[n] = 0

    def _waits(self, eng, toks):
        best = {}
        for t in toks:
            if t is None:
                continue
            n, v = t
            if v > 0:
                best[n] = max(best.get(n, 0), v)
        for n, v in best.items():
            sem = self.semh[n]

            def w(e, sem=sem, v=v):
                e.wait_ge(sem, v)
            self.prog[eng].append(w)

    def op(self, eng, fn, after=()):
        self._waits(eng, after)
        name = ESEM[eng]
        self.cnt[name] += 1
        v = self.cnt[name]
        sem = self.semh[name]

        def run(e, fn=fn, sem=sem):
            fn(e).then_inc(sem, 1)
        self.prog[eng].append(run)
        return (name, v)

    def dma(self, eng, sname, fn, after=()):
        self._waits(eng, after)
        self.cnt[sname] += 16
        v = self.cnt[sname]
        sem = self.semh[sname]

        def run(e, fn=fn, sem=sem):
            fn(e).then_inc(sem, 16)
        self.prog[eng].append(run)
        return (sname, v)

    def tok(self, name):
        return (name, self.cnt[name])

    def cc(self, kind, ins, outs, after=()):
        self._waits("gpsimd", after)
        self.cnt["cc"] += 1
        v = self.cnt["cc"]
        sem = self.semh["cc"]

        def run(e, ins=ins, outs=outs, sem=sem):
            e.collective_compute(
                kind, OP.bypass,
                replica_groups=[[0, 1, 2, 3], [4, 5, 6, 7]],
                ins=ins, outs=outs).then_inc(sem, 1)
        self.prog["gpsimd"].append(run)
        return ("cc", v)

    def barrier(self, skip=("cc",)):
        toks = [(n, v) for n, v in self.cnt.items()
                if v > 0 and n not in skip]
        for eng in ENGINES:
            self._waits(eng, toks)

    def sb(self, stack, name, shape, dt, side=None):
        return stack.enter_context(
            self.nc.sbuf_tensor(name, shape, dt, side=side))

    def ps(self, stack, name, shape, dt=F32):
        return stack.enter_context(self.nc.psum_tensor(name, shape, dt))

    def build(self):
        nc = self.nc

        def din(name, shape, dt=F32):
            return nc.dram_tensor(name, shape, dt, kind="ExternalInput").ap()

        i = self.i = {}
        i["xown"] = din("xown", [128, NCH, TE])
        i["cosd_own"] = din("cosd_own", [128, TE])
        i["sins_own"] = din("sins_own", [128, TE])
        i["wk"] = din("wk", [D, KVH * DH])
        i["wv"] = din("wv", [D, KVH * DH])
        i["wq_pk"] = din("wq_pk", [128, H, NCH, 128])
        i["bq_r"] = din("bq_r", [128, H])
        i["bk_r"] = din("bk_r", [128, KVH])
        i["bv_row"] = din("bv_row", [1, KVH * DH])
        i["conv_pk"] = din("conv_pk", [128, 3 * WCH, 3, NCH, 128])
        i["conv_b_r"] = din("conv_b_r", [128, 3 * WCH])
        i["wqb_pk"] = din("wqb_pk", [128, WCH, WCH, 128])
        i["wkb_pk"] = din("wkb_pk", [128, WCH, WCH, 128])
        i["wvb"] = din("wvb", [WB, HB * DHB])
        i["bqb_r"] = din("bqb_r", [128, WCH])
        i["bkb_r"] = din("bkb_r", [128, WCH])
        i["bvb_row"] = din("bvb_row", [1, HB * DHB])
        i["wo_pk"] = din("wo_pk", [128, WCH, WCH, 128])
        i["bo_r"] = din("bo_r", [128, WCH])
        i["projw_pk"] = din("projw_pk", [128, NCH, WCH, 128])
        i["projb_r"] = din("projb_r", [128, NCH])
        i["fker_pk"] = din("fker_pk", [128, NCH, NCH, 128])
        i["fbias_r"] = din("fbias_r", [128, NCH])
        i["a_pk"] = din("a_pk", [128, NL, NCH, RK])
        i["b_pk"] = din("b_pk", [128, NL, 1024])
        i["a_sh"] = din("a_sh", [128, NCH, RK])
        i["b_sh"] = din("b_sh", [RK, D])
        i["g3_row"] = din("g3_row", [1, 3 * NL])
        i["hmask"] = din("hmask", [128, 2])
        i["mask8"] = din("mask8", [128, BLK])

        self.y_t = nc.dram_tensor("y_t", [128, NCH, T], F32,
                                  kind="ExternalOutput").ap()

        def dscr(name, shape, dt):
            kind = "ExternalOutput" if name in self.debug else "Internal"
            return nc.dram_tensor(name, shape, dt, kind=kind).ap()

        self.qT_s = dscr("qT_s", [H, DH, TE], BF16)
        self.xh = dscr("xh", [128, NCH, TE], F32)
        self.nrm_d = dscr("nrm_d", [128, H + HB], F32)
        self.kag_in = [dscr(f"kag_in{s}", [KVH * DH, ST], BF16)
                       for s in range(2)]
        self.kag_out = [dscr(f"kag_out{s}", [4 * KVH * DH, ST], BF16)
                        for s in range(2)]
        self.vag_in = dscr("vag_in", [KVH * 8 * 128, DH], BF16)
        self.vag_out = dscr("vag_out", [4 * KVH * 8 * 128, DH], BF16)
        self.knag_in = dscr("knag_in", [1, KVH], F32)
        self.knag_out = dscr("knag_out", [4, KVH], F32)

        es = self.es
        self.ones_col = self.sb(es, "ones_col", [128, 1], BF16)
        self.ones_row = self.sb(es, "ones_row", [1, 128], F32)
        self.ones_mat = self.sb(es, "ones_mat", [128, 128], BF16)
        self.eps_t = self.sb(es, "eps_t", [1, 1], F32)
        self.negM = self.sb(es, "negM", [128, H], F32)
        self.negMb = self.sb(es, "negMb", [128, HB], F32)
        self.qn = self.sb(es, "qn", [1, H], F32)
        self.kn = self.sb(es, "kn", [1, KVH], F32)
        self.qnb = self.sb(es, "qnb", [1, HB], F32)
        self.knb = self.sb(es, "knb", [1, HB], F32)
        self.mrow = self.sb(es, "mrow", [1, ST], F32)
        self.msqrow = self.sb(es, "msqrow", [1, ST], F32)
        self.varrow = self.sb(es, "varrow", [1, ST], F32)
        self.m_bc = self.sb(es, "m_bc", [128, ST], BF16)
        self.rstd_bc = self.sb(es, "rstd_bc", [128, ST], BF16)
        self.nrm_t = self.sb(es, "nrm_t", [1, 1], F32)
        self.lntmp = self.sb(es, "lntmp", [128, ST], BF16)

        self.sem("T", "V", "S", "G", "Y")
        self.sem("xa0", "xa1", "cs", "ck", "sw0", "sw1", "sw2", "sw3",
                 "wres", "lnw0", "lnw1",
                 "ln0", "ln1", "kw0", "kw1", "wc0", "wc1", "wc2", "kv0", "kv1",
                 "qr0", "qr1", "qr2", "qr3", "xr0", "xr1", "xr2",
                 "st0", "st1", "st2", "ax",
                 "lr0", "lr1", "xb0", "xb1", "ag", "cc", "kx0", "kx1")

        self.t_init = self.op("vector", lambda v: (
            v.memset(self.ones_col[:], 1.0),
            v.memset(self.ones_row[:], 1.0),
            v.memset(self.eps_t[:], EPS),
            v.memset(self.qn[:], 0.0),
            v.memset(self.kn[:], 0.0),
            v.memset(self.ones_mat[:], 1.0))[-1])

        if "p0" in self.phases:
            self.p0()
        if "p2" in self.phases:
            self.barrier()
            sl = self.stLora = ExitStack()
            self.a_cat = self.sb(sl, "a_cat", [128, NCH, RC], BF16,
                                 side="right")
            self.a_catL = self.sb(sl, "a_catL", [128, NCH, RC], BF16,
                                  side="right")
            self.a_catR = self.sb(sl, "a_catR", [128, NCH, RC], BF16,
                                  side="right")
            self.b_cat = self.sb(sl, "b_cat", [RC, D], BF16, side="right")
            self.b_catL = self.sb(sl, "b_catL", [RC, D], BF16, side="right")
            self.b_catR = self.sb(sl, "b_catR", [RC, D], BF16, side="right")
            self.build_lora()
        if "p1" in self.phases:
            if "p2" not in self.phases:
                self.barrier()
            self.p1()
        if "p2" in self.phases:
            self.barrier()
            self.ffn(first=True)
        if "p3" in self.phases:
            self.barrier()
            self.p3()
        if "p4" in self.phases:
            self.barrier()
            self.ffn(first=False)
        if "p2" in self.phases:
            self.stLora.close()

        if "p4" not in self.phases:
            self.barrier()
            self.dma("sync", "st0",
                     lambda s: s.dma_start(out=self.y_t,
                                           in_=self.i["xown"][:, :, 1:1 + T]))
        if self.debug:
            self.barrier()
            self.dma("sync", "st0",
                     lambda s: s.dma_start(out=self.nrm_d[:, 0:H],
                                           in_=self.negM[:]))
            if "p3" in self.phases:
                self.dma("sync", "st0",
                         lambda s: s.dma_start(out=self.nrm_d[:, H:H + HB],
                                               in_=self.negMb[:]))

        self._waits("sync", [self.tok("st0"), self.tok("st1"),
                             self.tok("st2"), self.tok("kw0"),
                             self.tok("kw1"), self.tok("lnw0"),
                             self.tok("lnw1"), self.tok("ln0"),
                             self.tok("ln1"), self.tok("ag"),
                             self.tok("cc")])

        with nc.Block() as block:
            block.tensor(lambda t: [fn(t) for fn in self.prog["tensor"]])
            block.vector(lambda v: [fn(v) for fn in self.prog["vector"]])
            block.scalar(lambda sc: [fn(sc) for fn in self.prog["scalar"]])
            block.sync(lambda s: [fn(s) for fn in self.prog["sync"]])
            block.gpsimd(lambda g: [fn(g) for fn in self.prog["gpsimd"]])
        self.es.close()
        return nc

    def ln_stats(self, meansrc, sqsrc, sq_buf, sqb_off, wt, after,
                 ps_stat, ps_bcm, ps_bcr):
        p = self
        msh, mso = meansrc
        sqh, sqo = sqsrc
        t_sq = p.op("scalar", lambda sc: sc.activation(
            sq_buf[:, :, sqb_off:sqb_off + wt], sqh[:, :, sqo:sqo + wt],
            AF.Square), after=after)

        def stats(t):
            for kc in range(NCH):
                t.matmul(ps_stat[0:1, 0:wt], self.ones_col[:],
                         msh[:, kc, mso:mso + wt], start=(kc == 0),
                         stop=(kc == NCH - 1))
            for kc in range(NCH):
                last = t.matmul(ps_stat[0:1, ST:ST + wt], self.ones_col[:],
                                sq_buf[:, kc, sqb_off:sqb_off + wt],
                                start=(kc == 0), stop=(kc == NCH - 1))
            return last
        t_stat = p.op("tensor", stats, after=list(after) + [t_sq, self.t_init])
        t_rows = p.op("scalar", lambda sc: (
            sc.activation(self.mrow[0:1, 0:wt], ps_stat[0:1, 0:wt], AF.Copy,
                          scale=1.0 / D),
            sc.activation(self.msqrow[0:1, 0:wt], ps_stat[0:1, ST:ST + wt],
                          AF.Copy, scale=1.0 / D))[-1], after=[t_stat])
        t_v1 = p.op("vector", lambda v: v.tensor_tensor(
            self.varrow[0:1, 0:wt], self.mrow[0:1, 0:wt],
            self.mrow[0:1, 0:wt], op=OP.mult), after=[t_rows])
        t_v2 = p.op("vector", lambda v: v.tensor_tensor(
            self.varrow[0:1, 0:wt], self.msqrow[0:1, 0:wt],
            self.varrow[0:1, 0:wt], op=OP.subtract), after=[t_v1])
        t_l = p.op("scalar", lambda sc: sc.activation(
            self.varrow[0:1, 0:wt], self.varrow[0:1, 0:wt], AF.Ln,
            bias=self.eps_t[:]), after=[t_v2])
        t_r = p.op("scalar", lambda sc: sc.activation(
            self.varrow[0:1, 0:wt], self.varrow[0:1, 0:wt], AF.Exp,
            scale=-0.5), after=[t_l])
        t_bc = p.op("tensor", lambda t: (
            t.matmul(ps_bcm[:, 0:wt], self.ones_row[:], self.mrow[0:1, 0:wt],
                     start=True, stop=True),
            t.matmul(ps_bcr[:, 0:wt], self.ones_row[:],
                     self.varrow[0:1, 0:wt], start=True, stop=True))[-1],
            after=[t_r])
        t_e = p.op("vector", lambda v: (
            v.tensor_copy(self.m_bc[:, 0:wt], ps_bcm[:, 0:wt]),
            v.tensor_copy(self.rstd_bc[:, 0:wt], ps_bcr[:, 0:wt]))[-1],
            after=[t_bc])
        return t_e, t_stat, t_sq

    def ln_apply(self, src, soff, dst, doff, wt, after):
        p = self
        prev = None
        for kc in range(NCH):
            a = list(after) if kc == 0 else [prev]
            ta = p.op("vector", lambda v, kc=kc: v.tensor_tensor(
                self.lntmp[:, 0:wt], src[:, kc, soff:soff + wt],
                self.m_bc[:, 0:wt], op=OP.subtract), after=a)
            tb = p.op("vector", lambda v, kc=kc: v.tensor_tensor(
                dst[:, kc, doff:doff + wt], self.lntmp[:, 0:wt],
                self.rstd_bc[:, 0:wt], op=OP.mult), after=[ta])
            prev = tb
        return prev

    def p0(self):
        p = self
        stP = self.stP0 = ExitStack()
        lnst_b = [self.sb(stP, f"lnst{k}", [128, NCH, ST], BF16)
                  for k in range(2)]
        lntl = self.sb(stP, "lntl", [128, NCH, 2], BF16)
        st0 = ExitStack()
        xb_b = [self.sb(st0, "xb0", [128, NCH, ST], BF16)] * 2
        wres = self.sb(st0, "wres", [128, NCH, KVH * DH], BF16)
        cosd = self.sb(st0, "cosd", [128, ST], F32)
        sins = self.sb(st0, "sins", [128, ST], F32)
        kf = [self.sb(st0, f"kf{k}", [128, ST], F32) for k in range(4)]
        ksw = [self.sb(st0, f"ksw{k}", [128, ST], F32) for k in range(4)]
        kbf = [self.sb(st0, f"kbf{k}", [128, ST], BF16) for k in range(4)]
        ksq = self.sb(st0, "ksq", [128, ST], BF16)
        bk_sb = self.sb(st0, "bk_sb", [128, KVH], F32)
        bv_sb = self.sb(st0, "bv_sb", [1, KVH * DH], F32)
        ps_stat = self.ps(st0, "ps_stat0", [1, 2 * ST])
        ps_bcm = self.ps(st0, "ps_bcm0", [128, ST])
        ps_bcr = self.ps(st0, "ps_bcr0", [128, ST])
        ps_k = [self.ps(st0, f"ps_k{k}", [128, ST]) for k in range(2)]
        ps_n = self.ps(st0, "ps_n0", [1, 2 * ST])

        p.dma("sync", "cs", lambda s: s.dma_start(out=bk_sb[:],
                                                  in_=self.i["bk_r"]))
        p.dma("sync", "cs", lambda s: s.dma_start(out=bv_sb[:],
                                                  in_=self.i["bv_row"]))
        t_bk = t_bv = p.tok("cs")
        t_wk = p.dma("gpsimd", "wres", lambda g: g.dma_start(
            out=wres[:], in_=self.i["wk"].rearrange("(n p) d -> p n d", p=128),
            max_dma_last_dim=4096))

        kag_v = [t.rearrange("(j p) t -> j p t", p=DH) for t in self.kag_in]
        STRIPS = ((0, ST, lnst_b[0], True), (ST, ST, lnst_b[1], True),
                  (2 * ST, 2, lntl, False))
        xa = ["xa0", "xa1"]
        t_x = {}
        t_statdone = {}
        t_lnapp = {}
        t_kmm = {}
        t_kevac = {}
        t_kr1 = {}
        t_kr3 = {}
        t_kw = {}
        t_ksq = {}
        t_knm = None
        t_knr = None
        t_knrd = {}
        t_knx = None
        t_rope_last = None
        t_cs2 = None
        def emit_xload(s, a):
            off, wt = STRIPS[s][0], STRIPS[s][1]
            xbuf = xb_b[s % 2]
            t_x[s] = p.dma("gpsimd", xa[s % 2], lambda g, off=off, wt=wt,
                           xbuf=xbuf:
                           g.dma_start(out=xbuf[:, :, 0:wt],
                                       in_=self.i["xown"][:, :, off:off + wt],
                                       max_dma_last_dim=4096), after=a)
        def flat(ap, rows):
            n = 1
            for st_, sz in ap.ap:
                n *= sz
            return bass.AP(tensor=ap.tensor, offset=ap.offset,
                           ap=[[n // rows, rows], [1, n // rows]])
        emit_xload(0, [])
        gi = -1
        for s, (off, wt, lnst, do_k) in enumerate(STRIPS):
            xbuf = xb_b[s % 2]
            if do_k:
                a = [t_rope_last] if t_rope_last else []
                t_cs1 = p.dma("sync", "ck", lambda ss, off=off: ss.dma_start(
                    out=cosd[:], in_=self.i["cosd_own"][:, off:off + ST]),
                    after=a)
                t_cs2 = p.dma("sync", "ck", lambda ss, off=off: ss.dma_start(
                    out=sins[:], in_=self.i["sins_own"][:, off:off + ST]))
            st_after = [t_x[s]]
            t_bce, t_stat, _ = self.ln_stats((xbuf, 0), (xbuf, 0), lnst, 0,
                                             wt, st_after, ps_stat, ps_bcm,
                                             ps_bcr)
            t_statdone[s] = t_stat
            t_lnapp[s] = self.ln_apply(xbuf, 0, lnst, 0, wt, [t_bce, t_stat])
            if s + 1 < len(STRIPS):
                emit_xload(s + 1, [t_statdone[s], t_lnapp[s]])
            if not do_k:
                continue
            for ko in range(KVH):
                gi += 1
                kp = gi % 2
                k4 = gi % 4
                sw = f"sw{k4}"
                kw = ("kw0" if kp == 0 else "kw1") if s == 0 else \
                     ("kx0" if kp == 0 else "kx1")
                mm_after = [t_lnapp[s], t_x[s], t_wk]
                if kp in t_kevac:
                    mm_after.append(t_kevac[kp])
                t_kmm[(s, ko)] = p.op("tensor", lambda t, ko=ko, kp=kp,
                                      lnst=lnst: [
                    t.matmul(ps_k[kp][:], wres[:, kc, ko * DH:(ko + 1) * DH],
                             lnst[:, kc, :], start=(kc == 0),
                             stop=(kc == NCH - 1))
                    for kc in range(NCH)][-1], after=mm_after)
                ev_after = [t_kmm[(s, ko)], t_bk]
                if k4 in t_kr3:
                    ev_after.append(t_kr3[k4])
                t_kevac[kp] = p.op("scalar", lambda sc, ko=ko, kp=kp, k4=k4:
                                   sc.activation(kf[k4][:], ps_k[kp][:],
                                                 AF.Identity,
                                                 bias=bk_sb[:, ko:ko + 1]),
                                   after=ev_after)
                sq_after = [t_kevac[kp]]
                if t_knm is not None:
                    sq_after.append(t_knm)
                t_ksq[k4] = p.op("scalar", lambda sc, k4=k4: sc.activation(
                    ksq[:], kf[k4][:], AF.Square), after=sq_after)
                np2 = gi % 2
                nm_after = [t_ksq[k4]]
                if t_knrd.get(np2) is not None:
                    nm_after.append(t_knrd[np2])
                t_knm = p.op("tensor", lambda t, np2=np2: t.matmul(
                    ps_n[0:1, np2 * ST:(np2 + 1) * ST], self.ones_col[:],
                    ksq[:], start=True, stop=True), after=nm_after)
                nr_after = [t_knm]
                if t_knx is not None:
                    nr_after.append(t_knx)
                t_knr = p.op("vector", lambda v, np2=np2: v.reduce_max(
                    self.nrm_t[:], ps_n[0:1, np2 * ST:(np2 + 1) * ST],
                    axis=AX.X), after=nr_after)
                t_knrd[np2] = t_knr
                t_knx = p.op("vector", lambda v, ko=ko: v.tensor_tensor(
                    self.kn[0:1, ko:ko + 1], self.kn[0:1, ko:ko + 1],
                    self.nrm_t[:], op=OP.max), after=[t_knr])
                sw_after = [t_kevac[kp]]
                if k4 in t_kr3:
                    sw_after.append(t_kr3[k4])
                p.dma("sync", sw, lambda ss, k4=k4: ss.dma_start(
                    out=ksw[k4][0:64, :], in_=kf[k4][64:128, :]),
                    after=sw_after)
                t_sw2 = p.dma("sync", sw, lambda ss, k4=k4: ss.dma_start(
                    out=ksw[k4][64:128, :], in_=kf[k4][0:64, :]))
                t_kr1[k4] = p.op("gpsimd", lambda v, k4=k4: v.tensor_tensor(
                    kf[k4][:], kf[k4][:], cosd[:], op=OP.mult),
                    after=[t_cs2, t_sw2, t_kevac[kp], t_ksq[k4]])
                t_r2 = p.op("gpsimd", lambda v, k4=k4: v.tensor_tensor(
                    ksw[k4][:], ksw[k4][:], sins[:], op=OP.mult),
                    after=[t_kr1[k4]])
                r3_after = [t_r2]
                if k4 in t_kw:
                    r3_after += [self.tok(t_kw[k4][0])]
                t_kr3[k4] = p.op("gpsimd", lambda v, k4=k4: v.tensor_tensor(
                    kbf[k4][:], kf[k4][:], ksw[k4][:], op=OP.add),
                    after=r3_after)
                t_rope_last = t_kr3[k4]
                t_kw[k4] = p.dma("sync", kw,
                                 lambda ss, ko=ko, s=s, k4=k4:
                                 ss.dma_start(
                                     out=kag_v[s][ko],
                                     in_=kbf[k4][:]), after=[t_kr3[k4]])
            if s == 0:
                t_ccK0 = self.cc(
                    "AllGather", ins=[flat(self.kag_in[0], 64)],
                    outs=[flat(self.kag_out[0], 256)],
                    after=[self.tok("kw0"), self.tok("kw1")])

        p.dma("sync", "kx0", lambda ss: ss.dma_start(
            out=self.knag_in, in_=self.kn[:]), after=[t_knx])
        t_ccK = self.t_ccK = self.cc(
            "AllGather", ins=[flat(self.kag_in[1], 64)],
            outs=[flat(self.kag_out[1], 256)],
            after=[self.tok("kx0"), self.tok("kx1")])
        t_ccN = self.cc("AllGather", ins=[self.knag_in.opt()],
                        outs=[self.knag_out.opt()])

        vag_v = self.vag_in.rearrange("(j g p) d -> j g p d", g=8, p=128)
        t_passa_T = self.tok("T")
        t_passa_V = self.tok("V")
        t_passa_S = self.tok("S")
        t_passa_k0 = self.tok("kx0")
        t_passa_k1 = self.tok("kx1")
        wv_v = self.i["wv"].rearrange("(n p) d -> p n d", p=128)
        t_wvh = []
        for hf, strm in enumerate(("wres", "xa1")):
            t_wvh.append(p.dma("gpsimd", strm, lambda g, hf=hf:
                               g.dma_start(
                                   out=wres[:, :, hf * 512:(hf + 1) * 512],
                                   in_=wv_v[:, :, hf * 512:(hf + 1) * 512],
                                   max_dma_last_dim=4096),
                               after=[t_passa_T] if hf == 0 else []))
        t_vmm = []
        t_vev = []
        t_vw = []
        for s in range(2):
            lbuf = lnst_b[s]
            for pc in range(4):
                for hf in range(2):
                    vi = s * 8 + pc * 2 + hf
                    kp = vi % 2
                    a = [t_lnapp[s], t_wvh[hf], t_bv]
                    if vi >= 2:
                        a.append(t_vev[vi - 2])
                    else:
                        a.append(t_passa_S)
                    t_vmm.append(p.op("tensor",
                        lambda t, pc=pc, hf=hf, kp=kp, lbuf=lbuf: (
                            t.matmul(ps_k[kp][:], self.ones_row[:],
                                     bv_sb[0:1, hf * 512:(hf + 1) * 512],
                                     start=True, stop=False),
                            [t.matmul(ps_k[kp][:],
                                      lbuf[:, kc, pc * 128:(pc + 1) * 128],
                                      wres[:, kc, hf * 512:(hf + 1) * 512],
                                      start=False, stop=(kc == NCH - 1))
                             for kc in range(NCH)])[1][-1], after=a))
                    a = [t_vmm[vi]]
                    if vi >= 2:
                        a.append(t_vw[vi - 2])
                    else:
                        a += [t_passa_k0, t_passa_k1, t_passa_S]
                    t_vev.append(p.op("scalar", lambda sc, kp=kp:
                                      sc.activation(kbf[kp][:], ps_k[kp][:],
                                                    AF.Copy), after=a))
                    g_ = s * 4 + pc
                    t_vw.append(p.dma("sync", "lnw0" if kp == 0 else "lnw1",
                        lambda ss, g_=g_, hf=hf, kp=kp: ss.dma_start(
                            out=vag_v[4 * hf:4 * hf + 4, g_, :, :]
                                .rearrange("k p d -> p k d"),
                            in_=kbf[kp][:].rearrange("p (k d) -> p k d", k=4)),
                        after=[t_vev[vi]]))

        t_ccV = self.t_ccV = self.cc(
            "AllGather", ins=[flat(self.vag_in, 64)],
            outs=[flat(self.vag_out, 256)],
            after=[self.tok("lnw0"), self.tok("lnw1")])

        st0.close()
        self.barrier()
        stC = ExitStack()
        wqcb = [self.sb(stC, f"wqcb{k}", [128, NCH, 128], BF16)
                for k in range(2)]
        cosdo = self.sb(stC, "cosdo", [128, TE], F32)
        sinso = self.sb(stC, "sinso", [128, TE], F32)
        qf = [self.sb(stC, f"qf{k}", [128, TE], F32) for k in range(4)]
        qsw = [self.sb(stC, f"qsw{k}", [128, TE], F32) for k in range(4)]
        qbf = [self.sb(stC, f"qbf{k}", [128, TE], BF16) for k in range(4)]
        qsq = self.sb(stC, "qsq", [128, TE], BF16)
        bq_sb = self.sb(stC, "bq_sb", [128, H], F32)
        ps_q = [self.ps(stC, f"ps_q{k}", [128, 1536]) for k in range(2)]
        ps_qn = self.ps(stC, "ps_qn", [1, 1024])

        p.dma("sync", "cs", lambda s: s.dma_start(
            out=cosdo[:], in_=self.i["cosd_own"]))
        p.dma("sync", "cs", lambda s: s.dma_start(
            out=sinso[:], in_=self.i["sins_own"]))
        p.dma("sync", "cs", lambda s: s.dma_start(out=bq_sb[:],
                                                  in_=self.i["bq_r"]))
        t_lno = t_cso2 = t_bq = p.tok("cs")
        lnseg = ((lnst_b[0], 0, ST), (lnst_b[1], ST, ST), (lntl, 2 * ST, 2))
        wc = ["wc0", "wc1"]
        t_wq = {}
        t_qmm = {}
        t_qev = {}
        t_qr1c = {}
        t_qr3c = {}
        t_qwc = {}
        t_qsqc = {}
        t_qnm = None
        t_qnr = None
        t_qnrd = {}
        for qh in range(H):
            qp = qh % 2
            q4 = qh % 4
            swq = f"sw{q4}"
            kwq = "ln0" if qp == 0 else "ln1"
            a = [t_qmm[qh - 2]] if qh >= 2 else []
            t_wq[qh] = p.dma("gpsimd", wc[qp], lambda g, qh=qh, qp=qp:
                             g.dma_start(out=wqcb[qp][:],
                                         in_=self.i["wq_pk"][:, qh],
                                         max_dma_last_dim=8192), after=a)
            a = [t_wq[qh], t_lno]
            if qh >= 2:
                a.append(t_qev[qh - 2])
            t_qmm[qh] = p.op("tensor", lambda t, qp=qp: [
                t.matmul(ps_q[qp][:, c0:c0 + wt], wqcb[qp][:, kc, :],
                         seg_h[:, kc, 0:wt], start=(kc == 0),
                         stop=(kc == NCH - 1))
                for (seg_h, c0, wt) in lnseg
                for kc in range(NCH)][-1], after=a)
            a = [t_qmm[qh], t_bq]
            if q4 in t_qr3c:
                a.append(t_qr3c[q4])
            t_qev[qh] = p.op("scalar", lambda sc, qh=qh, qp=qp, q4=q4:
                             sc.activation(
                                 qf[q4][:, 0:TE], ps_q[qp][:, 0:TE],
                                 AF.Identity,
                                 bias=bq_sb[:, qh:qh + 1]), after=a)
            sqa = [t_qev[qh]]
            if t_qnm is not None:
                sqa.append(t_qnm)
            t_qsqc[q4] = p.op("scalar", lambda sc, q4=q4: sc.activation(
                qsq[:], qf[q4][:], AF.Square), after=sqa)
            t_qnx = None
            for si, (sc0, sw) in enumerate(((0, 512), (512, 512), (1024, 2))):
                np2 = (qh * 3 + si) % 2
                nma = [t_qsqc[q4]]
                if t_qnrd.get(np2) is not None:
                    nma.append(t_qnrd[np2])
                t_qnm = p.op("tensor", lambda t, sc0=sc0, sw=sw, np2=np2:
                             t.matmul(
                    ps_qn[0:1, np2 * 512:np2 * 512 + sw], self.ones_col[:],
                    qsq[:, sc0:sc0 + sw], start=True, stop=True), after=nma)
                nra = [t_qnm]
                if t_qnx is not None:
                    nra.append(t_qnx)
                t_qnr = p.op("vector", lambda v, sw=sw, np2=np2: v.reduce_max(
                    self.nrm_t[:], ps_qn[0:1, np2 * 512:np2 * 512 + sw],
                    axis=AX.X), after=nra)
                t_qnrd[np2] = t_qnr
                t_qnx = p.op("vector", lambda v, qh=qh: v.tensor_tensor(
                    self.qn[0:1, qh:qh + 1], self.qn[0:1, qh:qh + 1],
                    self.nrm_t[:], op=OP.max), after=[t_qnr])
            t_qnr = t_qnx
            sw_after = [t_qev[qh]]
            if q4 in t_qr3c:
                sw_after.append(t_qr3c[q4])
            p.dma("sync", swq, lambda ss, q4=q4: ss.dma_start(
                out=qsw[q4][0:64, :], in_=qf[q4][64:128, :]),
                after=sw_after)
            t_s2 = p.dma("sync", swq, lambda ss, q4=q4: ss.dma_start(
                out=qsw[q4][64:128, :], in_=qf[q4][0:64, :]))
            t_qr1c[q4] = p.op("vector", lambda v, q4=q4: v.tensor_tensor(
                qf[q4][:], qf[q4][:], cosdo[:], op=OP.mult),
                after=[t_cso2, t_s2, t_qev[qh], t_qsqc[q4]])
            t_r2 = p.op("vector", lambda v, q4=q4: v.tensor_tensor(
                qsw[q4][:], qsw[q4][:], sinso[:], op=OP.mult),
                after=[t_qr1c[q4]])
            r3a = [t_r2]
            if q4 in t_qwc:
                r3a += [self.tok(kwq)]
            t_qr3c[q4] = p.op("vector", lambda v, q4=q4: v.tensor_tensor(
                qbf[q4][:], qf[q4][:], qsw[q4][:], op=OP.add), after=r3a)
            t_qwc[q4] = p.dma("sync", kwq,
                              lambda ss, qh=qh, q4=q4:
                              ss.dma_start(out=self.qT_s[qh],
                                           in_=qbf[q4][:]),
                              after=[t_qr3c[q4]])

        t_agl = None
        for r in range(4):
            a = [t_ccN, self.tok("S")] if r == 0 else []
            t_agl = p.dma("sync", "ag", lambda ss, r=r: ss.dma_start(
                out=self.mrow[0:1, 64 + r * KVH:64 + (r + 1) * KVH],
                in_=self.knag_out[r:r + 1, :]), after=a)
        t_kn = p.op("vector", lambda v: v.tensor_tensor(
            self.kn[0:1, :], self.mrow[0:1, 64:64 + KVH],
            self.mrow[0:1, 72:72 + KVH], op=OP.max),
            after=[t_agl, self.tok("V")])
        for c0 in (80, 88):
            t_kn = p.op("vector", lambda v, c0=c0: v.tensor_tensor(
                self.kn[0:1, :], self.kn[0:1, :],
                self.mrow[0:1, c0:c0 + KVH], op=OP.max), after=[t_kn])

        knb_ap = self.kn[:]
        kn_exp = bass.AP(tensor=knb_ap.tensor, offset=knb_ap.offset,
                         ap=[[knb_ap.ap[0][0], 1], [1, KVH], [0, 4]])
        t_m1 = p.op("vector", lambda v: v.tensor_tensor(
            self.mrow[0:1, 0:H].rearrange("p (a b) -> p a b", a=KVH),
            self.qn[:].rearrange("p (a b) -> p a b", a=KVH),
            kn_exp, op=OP.mult), after=[t_qnr, t_kn, self.tok("V")])
        t_m2 = p.op("scalar", lambda sc: sc.activation(
            self.mrow[0:1, 0:H], self.mrow[0:1, 0:H], AF.Ln,
            bias=self.eps_t[:]), after=[t_m1, self.tok("S")])
        t_m3 = p.op("scalar", lambda sc: sc.activation(
            self.mrow[0:1, 0:H], self.mrow[0:1, 0:H], AF.Exp, scale=0.5),
            after=[t_m2])
        t_m4 = p.op("scalar", lambda sc: sc.activation(
            self.msqrow[0:1, 0:H], self.mrow[0:1, 0:H], AF.Copy, scale=-ISQD,
            bias=-1.0), after=[t_m3])
        t_m5 = p.op("tensor", lambda t: t.matmul(
            ps_q[0][:, 0:H], self.ones_row[:], self.msqrow[0:1, 0:H],
            start=True, stop=True), after=[t_m4, self.tok("T")])
        t_m6 = p.op("vector", lambda v: v.tensor_copy(
            self.negM[:], ps_q[0][:, 0:H]), after=[t_m5])
        stC.close()
        stP.close()

    def p1(self):
        p = self
        guard = getattr(self, "t_psg_guard", None)
        st1 = ExitStack()
        kjb = [self.sb(st1, f"kjb{k}", [128, SEQ], BF16) for k in range(2)]
        vjb = [self.sb(st1, f"vjb{k}", [128, NCH, DH], BF16) for k in range(2)]
        qhb = [self.sb(st1, f"qhb{k}", [128, TE], BF16) for k in range(4)]
        pb = [self.sb(st1, f"pb{k}", [128, 1024], BF16) for k in range(2)]
        ptl = self.sb(st1, "ptl", [128, 16], BF16)
        dnl_sb = self.sb(st1, "dnl_sb", [128, ST], F32)
        rc_sb = self.sb(st1, "rc_sb", [128, ST], F32)
        tt_sb = self.sb(st1, "tt_sb", [128, ST], F32)
        x0c = [self.sb(st1, f"x0c{k}", [128, ST], F32) for k in range(2)]
        xoc = [self.sb(st1, f"xoc{k}", [128, ST], F32) for k in range(2)]
        x0t = self.sb(st1, "x0t", [128, 2], F32)
        xot = self.sb(st1, "xot", [128, 2], F32)
        ps_s = [self.ps(st1, f"ps_s{k}", [128, 1024]) for k in range(2)]
        ps_pv_b = [self.ps(st1, f"ps_pv{k}", [128, ST]) for k in range(2)]
        ps_dn = self.ps(st1, "ps_dn", [128, ST])
        ps_tl = self.ps(st1, "ps_tl", [128, 16])

        kv = ["kv0", "kv1"]
        qr = ["qr0", "qr1", "qr2", "qr3"]
        xr = ["xr0", "xr1"]
        kago = [t.rearrange("(r j p) t -> j p r t", r=4, p=DH)
                for t in self.kag_out]
        vago = self.vag_out.rearrange("(r j g p) d -> j p r g d", r=4, g=8,
                                      p=128)
        t_kvld = {}
        t_qld = {}
        t_jdone = {}
        t_qdone = {}
        t_exp = {}
        t_pvdn = {}
        t_upd = {}
        t_rc1p = None
        t_rc2p = None
        t_t1p = None
        t_uprev = None
        t_t1prev = [None, None]
        t_xwp = {}

        for j in range(KVH):
            jp = j % 2
            a = [t_jdone[j - 2]] if j >= 2 else [self.t_ccK]
            kjb_v = kjb[jp][:].rearrange("p (r s t) -> p r s t", r=4, s=2)
            for s_ in range(2):
                p.dma("gpsimd", kv[jp], lambda g, j=j, jp=jp, s_=s_,
                      kjb_v=kjb_v: g.dma_start(
                          out=kjb_v[:, :, s_, :], in_=kago[s_][j],
                          max_dma_last_dim=8192),
                      after=a if s_ == 0 else [])
            a2 = [] if j >= 2 else [self.t_ccV]
            for r in range(4):
                t_kvld[j] = p.dma("gpsimd", kv[jp], lambda g, j=j, jp=jp,
                                  r=r: g.dma_start(
                                      out=vjb[jp][:, r * 8:(r + 1) * 8, :],
                                      in_=vago[j][:, r],
                                      max_dma_last_dim=8192),
                                  after=a2 if r == 0 else [])
            for u in range(4):
                qh = 4 * j + u
                qp = qh % 4
                a = [t_qdone[qh - 4]] if qh >= 4 else []
                t_qld[qh] = p.dma("sync", qr[qp], lambda s, qh=qh, qp=qp:
                                  s.dma_start(out=qhb[qp][:],
                                              in_=self.qT_s[qh]), after=a)
                for qt in range(2):
                    xp = qt % 2
                    a = []
                    prev = (t_upd.get((qh - 1, 2)) if qt == 0
                            else t_upd.get((qh - 1, 1)))
                    if prev:
                        a.append(prev)
                    t_xld = p.dma("sync", xr[xp], lambda s, qh=qh, qt=qt,
                                  xp=xp: s.dma_start(
                        out=x0c[xp][:],
                        in_=self.i["xown"][:, qh, qt * ST:(qt + 1) * ST]),
                        after=a)
                    pvp = (qh * 2 + qt) % 2
                    ps_pv = ps_pv_b[pvp]

                    def mk_pvdn(pr, jp, ps_pv=ps_pv):
                        def pvdn(t, jp=jp, pr=pr, sp=pr % 2):
                            first = (pr == 0)
                            last = None
                            for h2 in range(2):
                                kc = pr * 2 + h2
                                lst = (pr == 15) and h2 == 1
                                t.matmul(ps_pv[:], vjb[jp][:, kc, :],
                                         pb[sp][:, h2 * ST:(h2 + 1) * ST],
                                         start=(first and h2 == 0), stop=lst)
                                last = t.matmul(ps_dn[:], self.ones_mat[:],
                                                pb[sp][:, h2 * ST:(h2 + 1) * ST],
                                                start=(first and h2 == 0),
                                                stop=lst)
                            return last
                        return pvdn

                    for pr in range(16):
                        sp = pr % 2

                        def smm(t, jp=jp, qp=qp, qt=qt, pr=pr, sp=sp):
                            last = None
                            for h2 in range(2):
                                kc = pr * 2 + h2
                                last = t.matmul(
                                    ps_s[sp][:, h2 * ST:(h2 + 1) * ST],
                                    kjb[jp][:, kc * 128:(kc + 1) * 128],
                                    qhb[qp][:, qt * ST:(qt + 1) * ST],
                                    start=True, stop=True)
                            return last
                        a = [t_kvld[j], t_qld[qh]]
                        if guard is not None and (j, u, qt, pr) == (0, 0, 0, 0):
                            a.append(guard)
                        if sp in t_exp:
                            a.append(t_exp[sp])
                        t_smm = p.op("tensor", smm, after=a)
                        if pr >= 1:
                            a = [t_exp[1 - sp]]
                            if pr == 1:
                                if t_t1prev[pvp] is not None:
                                    a.append(t_t1prev[pvp])
                                if t_rc1p:
                                    a.append(t_rc1p)
                            t_pvdn[1 - sp] = p.op("tensor", mk_pvdn(pr - 1, jp),
                                                  after=a)
                        a = [t_smm]
                        if sp in t_pvdn:
                            a.append(t_pvdn[sp])
                        t_exp[sp] = p.op("scalar", lambda sc, sp=sp, qh=qh:
                                         sc.activation(
                                             pb[sp][:], ps_s[sp][:], AF.Exp,
                                             scale=ISQD,
                                             bias=self.negM[:, qh:qh + 1]),
                                         after=a)
                    t_pvdn[1] = p.op("tensor", mk_pvdn(15, jp),
                                     after=[t_exp[1]])
                    a = [t_pvdn[1]]
                    if t_rc2p:
                        a.append(t_rc2p)
                    t_rc1p = p.op("scalar", lambda sc: sc.activation(
                        dnl_sb[:], ps_dn[:], AF.Ln), after=a)
                    a = [t_rc1p]
                    if t_t1p:
                        a.append(t_t1p)
                    t_rc2p = p.op("scalar", lambda sc: sc.activation(
                        rc_sb[:], dnl_sb[:], AF.Exp, scale=-1.0), after=a)
                    t_t1p = p.op("vector", lambda v, ps_pv=ps_pv:
                                 v.tensor_tensor(
                                     tt_sb[:], ps_pv[:], rc_sb[:],
                                     op=OP.mult),
                        after=[t_rc2p, t_pvdn[1]] +
                              ([t_uprev] if t_uprev else []))
                    t_t1prev[pvp] = t_t1p
                    t_u = p.op("vector", lambda v, xp=xp: v.tensor_tensor(
                        xoc[xp][:], tt_sb[:], x0c[xp][:], op=OP.add),
                        after=[t_t1p, t_xld] +
                              ([t_xwp[(qh - 1, qt)]] if (qh - 1, qt) in t_xwp
                               else []))
                    t_upd[(qh, qt)] = t_u
                    t_uprev = t_u
                    t_xwp[(qh, qt)] = p.dma("sync", "st0" if qt == 0 else "st1",
                        lambda s, qh=qh, qt=qt, xp=xp: s.dma_start(
                            out=self.xh[:, qh, qt * ST:(qt + 1) * ST],
                            in_=xoc[xp][:]), after=[t_u])
                t_xldt = p.dma("sync", xr[0], lambda s, qh=qh: s.dma_start(
                    out=x0t[:], in_=self.i["xown"][:, qh, T:TE]),
                    after=[t_upd[(qh, 1)]] +
                          ([t_xwp[(qh - 1, 2)]] if (qh - 1, 2) in t_xwp else []))
                t_expt = None
                t_pvt = None
                for kg in range(4):
                    def smmt(t, jp=jp, qp=qp, kg=kg):
                        last = None
                        for k8 in range(8):
                            kc = kg * 8 + k8
                            last = t.matmul(ps_tl[:, k8 * 2:(k8 + 1) * 2],
                                            kjb[jp][:, kc * 128:(kc + 1) * 128],
                                            qhb[qp][:, T:TE], start=True,
                                            stop=True)
                        return last
                    a = [t_expt] if t_expt else [t_pvdn[1]]
                    t_smt = p.op("tensor", smmt, after=a)
                    a = [t_smt]
                    if t_pvt:
                        a.append(t_pvt)
                    t_expt = p.op("scalar", lambda sc, qh=qh: sc.activation(
                        ptl[:], ps_tl[:], AF.Exp, scale=ISQD,
                        bias=self.negM[:, qh:qh + 1]), after=a)

                    def pvdnt(t, jp=jp, kg=kg):
                        last = None
                        for k8 in range(8):
                            kc = kg * 8 + k8
                            first = (kc == 0)
                            lst = (kc == 31)
                            t.matmul(ps_pv_b[0][:, 0:2], vjb[jp][:, kc, :],
                                     ptl[:, k8 * 2:(k8 + 1) * 2],
                                     start=first, stop=lst)
                            last = t.matmul(ps_dn[:, 0:2], self.ones_mat[:],
                                            ptl[:, k8 * 2:(k8 + 1) * 2],
                                            start=first, stop=lst)
                        return last
                    a = [t_expt]
                    if kg == 0:
                        a += [t_t1prev[0] or ("T", 0), t_rc1p]
                    t_pvt = p.op("tensor", pvdnt, after=a)
                t_rc1p = p.op("scalar", lambda sc: sc.activation(
                    dnl_sb[:, 0:2], ps_dn[:, 0:2], AF.Ln),
                    after=[t_pvt, t_rc2p])
                t_rc2p = p.op("scalar", lambda sc: sc.activation(
                    rc_sb[:, 0:2], dnl_sb[:, 0:2], AF.Exp, scale=-1.0),
                    after=[t_rc1p, t_t1p])
                t_t1p = p.op("vector", lambda v: v.tensor_tensor(
                    tt_sb[:, 0:2], ps_pv_b[0][:, 0:2], rc_sb[:, 0:2],
                    op=OP.mult), after=[t_rc2p, t_pvt] +
                    ([t_uprev] if t_uprev else []))
                t_t1prev[0] = t_t1p
                t_u = p.op("vector", lambda v: v.tensor_tensor(
                    xot[:], tt_sb[:, 0:2], x0t[:], op=OP.add),
                    after=[t_t1p, t_xldt])
                t_upd[(qh, 2)] = t_u
                t_uprev = t_u
                t_xwp[(qh, 2)] = p.dma("sync", "st2", lambda s, qh=qh:
                                       s.dma_start(out=self.xh[:, qh, T:TE],
                                                   in_=xot[:]), after=[t_u])
                t_qdone[qh] = self.tok("T")
            t_jdone[j] = self.tok("T")
        st1.close()

    def ffn(self, first):
        p = self
        tag = "2" if first else "4"
        stF = ExitStack()
        if first:
            mtiles = ((0, ST), (ST, ST), (T, 2))
            ltiles = [(k * LT, LT) for k in range(4)] + [(T, 2)]
        else:
            mtiles = ((1, ST), (1 + ST, ST))
            ltiles = [(1 + k * LT, LT) for k in range(4)]
        lnx = self.sb(stF, "lnx" + tag, [128, NCH, TE], BF16)
        axs = self.sb(stF, "axs" + tag, [RC, 3 * ST + 8], BF16)
        fb_sb = self.sb(stF, "fb" + tag, [128, NCH], F32)
        ps_stat = self.ps(stF, "ps_stat" + tag, [1, 2 * ST])
        ps_bcm = self.ps(stF, "ps_bcm" + tag, [128, ST])
        ps_bcr = self.ps(stF, "ps_bcr" + tag, [128, ST])
        ps_f = [self.ps(stF, f"ps_f{tag}{k}", [128, ST]) for k in range(3)]
        ps_ax = self.ps(stF, "ps_ax" + tag, [RC, ST])

        p.dma("sync", "cs", lambda s: s.dma_start(
            out=fb_sb[:], in_=self.i["fbias_r"]))
        t_lora = self.t_lora_done
        t_fb = p.tok("cs")

        stL = ExitStack()
        xbstr = [self.sb(stL, f"xb{tag}{k}", [128, NCH, LT], BF16)
                 for k in range(2)]
        t_prev = None
        t_lnd = {}
        t_statD = {}
        base_toks = [self.tok("st0"), self.tok("st1"), self.tok("st2"),
                     self.tok("T"), self.tok("V"), self.tok("S"), t_lora,
                     self.tok("ax")]
        t_xsD = {}
        for li, (c0, wt) in enumerate(ltiles):
            lp = li % 2
            a = list(base_toks)
            if li >= 2:
                a = [t_lnd[li - 2], t_statD[li - 2]]
            t_xsD[li] = p.dma("gpsimd", "xb0" if lp == 0 else "xb1",
                              lambda g, c0=c0, wt=wt, lp=lp: g.dma_start(
                                  out=xbstr[lp][:, :, 0:wt],
                                  in_=self.xh[:, :, c0:c0 + wt],
                                  max_dma_last_dim=4096), after=a)
            t_bce, t_stat, t_sq = self.ln_stats(
                (xbstr[lp], 0), (xbstr[lp], 0), lnx, c0, wt,
                [t_xsD[li]] + ([t_prev] if t_prev else []),
                ps_stat, ps_bcm, ps_bcr)
            t_statD[li] = t_stat
            t_lnd[li] = self.ln_apply(xbstr[lp], 0, lnx, c0, wt,
                                      [t_bce, t_stat])
            t_prev = t_lnd[li]
        t_ln_all = t_prev
        stL.close()

        t_axe = {}
        t_ax_prev = None
        for ti, (c0, wt) in enumerate(mtiles):
            a = [t_ln_all, t_lora]
            if t_ax_prev:
                a.append(t_ax_prev)
            t_ax = p.op("tensor", lambda t, c0=c0, wt=wt: [
                t.matmul(ps_ax[:, 0:wt], self.a_cat[:, kc, :],
                         lnx[:, kc, c0:c0 + wt], start=(kc == 0),
                         stop=(kc == NCH - 1)) for kc in range(NCH)][-1],
                after=a)
            t_axe[ti] = p.op("vector", lambda v, ti=ti, wt=wt: v.tensor_copy(
                axs[:, ti * ST:ti * ST + wt], ps_ax[:, 0:wt]), after=[t_ax])
            t_ax_prev = t_axe[ti]
        if first:
            t_axh = p.op("tensor", lambda t: ([
                t.matmul(ps_ax[:, 0:1], self.a_catL[:, kc, :],
                         lnx[:, kc, 0:1], start=(kc == 0),
                         stop=(kc == NCH - 1)) for kc in range(NCH)],
                [t.matmul(ps_ax[:, 4:5], self.a_catR[:, kc, :],
                          lnx[:, kc, TE - 1:TE], start=(kc == 0),
                          stop=(kc == NCH - 1)) for kc in range(NCH)])[1][-1],
                after=[t_ln_all, t_lora, t_ax_prev])
            t_axhe = p.op("vector", lambda v: v.tensor_copy(
                axs[:, 3 * ST:3 * ST + 8], ps_ax[:, 0:8]), after=[t_axh])
            t_axn = p.op("vector", lambda v: (
                v.tensor_scalar_mul(axs[:, 3 * ST + 2:3 * ST + 3],
                                    axs[:, 0:1], -1.0),
                v.tensor_scalar_mul(axs[:, 3 * ST + 6:3 * ST + 7],
                                    axs[:, 2 * ST + 1:2 * ST + 2], -1.0))[-1],
                after=[t_axhe])
            t_axall = t_axn
        else:
            t_axall = t_ax_prev

        stM = ExitStack()
        wcb = [self.sb(stM, f"fw{tag}{k}", [128, NCH, 128], BF16)
               for k in range(3)]
        x1c = [self.sb(stM, f"x1c{tag}{k}", [128, ST], F32) for k in range(2)]
        xoc = [self.sb(stM, f"xoc{tag}{k}", [128, ST], F32) for k in range(2)]
        wcs = ["wc0", "wc1"]
        t_w = {}
        t_mmL = []
        t_uL = []
        t_stL = []
        fi = 0
        nt = len(mtiles)
        for do in range(NCH):
            dp = do % 3
            a = ([t_mmL[(do - 2) * nt - 1]] if do >= 3 else
                 [t_lora, t_ln_all, t_axall])
            t_w[do] = p.dma("gpsimd", f"wc{do % 3}", lambda g, do=do, dp=dp:
                            g.dma_start(out=wcb[dp][:],
                                        in_=self.i["fker_pk"][:, do],
                                        max_dma_last_dim=8192), after=a)
            for ti, (c0, wt) in enumerate(mtiles):
                fp = fi % 3
                xp1 = fi % 2
                a = []
                if fi >= 2:
                    a.append(t_uL[fi - 2])
                else:
                    a += [t_ln_all, t_axall]
                t_x1 = p.dma("scalar", "xr0" if xp1 == 0 else "xr1",
                             lambda s, do=do, c0=c0, wt=wt, xp1=xp1:
                             s.dma_start(out=x1c[xp1][:, 0:wt],
                                         in_=self.xh[:, do, c0:c0 + wt]),
                             after=a)

                def fmm(t, do=do, dp=do % 3, ti=ti, wt=wt, fp=fp, c0=c0):
                    for kc in range(NCH):
                        t.matmul(ps_f[fp][:, 0:wt], wcb[dp][:, kc, :],
                                 lnx[:, kc, c0:c0 + wt], start=(kc == 0),
                                 stop=False)
                    last = t.matmul(ps_f[fp][:, 0:wt],
                                    self.b_cat[:, do * 128:(do + 1) * 128],
                                    axs[:, ti * ST:ti * ST + wt],
                                    start=False,
                                    stop=not (first and ti in (0, 2)))
                    if first and ti == 0:
                        t.matmul(ps_f[fp][:, 0:1],
                                 self.b_cat[:, do * 128:(do + 1) * 128],
                                 axs[:, 3 * ST + 2:3 * ST + 3],
                                 start=False, stop=False)
                        last = t.matmul(
                            ps_f[fp][:, 0:1],
                            self.b_catL[:, do * 128:(do + 1) * 128],
                            axs[:, 3 * ST:3 * ST + 1], start=False, stop=True)
                    if first and ti == 2:
                        t.matmul(ps_f[fp][:, 1:2],
                                 self.b_cat[:, do * 128:(do + 1) * 128],
                                 axs[:, 3 * ST + 6:3 * ST + 7],
                                 start=False, stop=False)
                        last = t.matmul(
                            ps_f[fp][:, 1:2],
                            self.b_catR[:, do * 128:(do + 1) * 128],
                            axs[:, 3 * ST + 4:3 * ST + 5], start=False,
                            stop=True)
                    return last
                a = [t_w[do], t_ln_all, t_axall]
                if fi >= 3:
                    a.append(t_uL[fi - 3])
                t_mmL.append(p.op("tensor", fmm, after=a))
                a = [t_mmL[fi], t_x1, t_fb]
                if fi >= 2:
                    a.append(t_stL[fi - 2])
                t_uL.append(p.op("vector", lambda v, do=do, wt=wt, fp=fp,
                                 xp1=xp1:
                                 v.scalar_tensor_tensor(
                                     xoc[xp1][:, 0:wt], ps_f[fp][:, 0:wt],
                                     fb_sb[:, do:do + 1], x1c[xp1][:, 0:wt],
                                     op0=OP.add, op1=OP.add), after=a))
                dst_ = self.xh if first else self.y_t
                dco = c0 if first else c0 - 1
                t_stL.append(p.dma("sync", "st0" if xp1 == 0 else "st1",
                    lambda s, do=do, dco=dco, wt=wt, xp1=xp1, dst_=dst_:
                    s.dma_start(out=dst_[:, do, dco:dco + wt],
                                in_=xoc[xp1][:, 0:wt]), after=[t_uL[fi]]))
                fi += 1
        stM.close()
        stF.close()

    def build_lora(self):
        p = self
        stO = ExitStack()
        g3bc = self.sb(stO, "g3bc", [128, 3 * NL], F32, side="right")
        g3row = self.sb(stO, "g3row", [1, 3 * NL], F32, side="right")
        aacc = self.sb(stO, "aacc", [128, 3, NCH, RK], F32, side="right")
        bacc = self.sb(stO, "bacc", [128, 3, 1024], F32, side="right")
        anb = [self.sb(stO, f"anb{k}", [128, NCH, RK], BF16, side="right")
               for k in range(2)]
        bnb = [self.sb(stO, f"bnb{k}", [128, 1024], BF16, side="right")
               for k in range(2)]
        ps_g = self.ps(stO, "ps_g", [128, 3 * NL])

        p.dma("sync", "cs", lambda s: s.dma_start(out=g3row[:],
                                                  in_=self.i["g3_row"]))
        t_g = p.tok("cs")
        t_gb = p.op("tensor", lambda t: t.matmul(
            ps_g[:], self.ones_row[:], g3row[:], start=True, stop=True),
            after=[t_g, self.t_init, self.tok("T"), self.tok("S"),
                   self.tok("V")])
        self.t_psg_guard = t_gb
        t_gbe = p.op("vector", lambda v: v.tensor_copy(g3bc[:], ps_g[:]),
                     after=[t_gb])
        lr = ["lr0", "lr1"]
        t_ld = {}
        t_acc = {}
        prev = None
        for n in range(NL):
            npr = n % 2
            a = [t_acc[n - 2]] if n >= 2 else [self.tok("T"), self.tok("S"),
                                               self.tok("V")]
            p.dma("gpsimd", lr[npr], lambda g, n=n, npr=npr:
                  g.dma_start(out=anb[npr][:], in_=self.i["a_pk"][:, n],
                              max_dma_last_dim=8192), after=a)
            t_ld2 = p.dma("gpsimd", lr[npr], lambda g, n=n, npr=npr:
                          g.dma_start(out=bnb[npr][:],
                                      in_=self.i["b_pk"][:, n],
                                      max_dma_last_dim=8192))
            t_ld[n] = t_ld2
            for gi in range(3):
                col = gi * NL + n
                if n == 0:
                    t_a = p.op("vector", lambda v, gi=gi, npr=npr, col=col:
                               v.tensor_scalar_mul(
                                   aacc[:, gi], anb[npr][:],
                                   g3bc[:, col:col + 1]),
                               after=[t_ld[n], t_gbe] +
                                     ([prev] if prev else []))
                    t_b = p.op("vector", lambda v, gi=gi, npr=npr, col=col:
                               v.tensor_scalar_mul(
                                   bacc[:, gi], bnb[npr][:],
                                   g3bc[:, col:col + 1]),
                               after=[t_ld[n], t_a])
                else:
                    t_a = p.op("vector", lambda v, gi=gi, npr=npr, col=col:
                               v.scalar_tensor_tensor(
                                   aacc[:, gi], anb[npr][:],
                                   g3bc[:, col:col + 1], aacc[:, gi],
                                   op0=OP.mult, op1=OP.add),
                               after=[t_ld[n], t_gbe] +
                                     ([prev] if prev else []))
                    t_b = p.op("vector", lambda v, gi=gi, npr=npr, col=col:
                               v.scalar_tensor_tensor(
                                   bacc[:, gi], bnb[npr][:],
                                   g3bc[:, col:col + 1], bacc[:, gi],
                                   op0=OP.mult, op1=OP.add),
                               after=[t_ld[n], t_a])
                prev = t_b
            t_acc[n] = prev
        t_cp = []
        for dstA, gi in ((self.a_cat, 1), (self.a_catL, 0), (self.a_catR, 2)):
            t_cp.append(p.op("vector", lambda v, dstA=dstA, gi=gi:
                             v.tensor_copy(dstA[:, :, 0:RK], aacc[:, gi]),
                             after=[t_acc[NL - 1]]))
        t_dma = []
        for k, (dstA, gi) in enumerate(((self.a_cat, 1), (self.a_catL, 0),
                                        (self.a_catR, 2))):
            t_dma.append(p.dma("gpsimd", "ax", lambda g, dstA=dstA:
                               g.dma_start(out=dstA[:, :, RK:RC],
                                           in_=self.i["a_sh"],
                                           max_dma_last_dim=8192),
                               after=[t_cp[k]]))
        for dstB, gi in ((self.b_cat, 1), (self.b_catL, 0), (self.b_catR, 2)):
            for q in range(4):
                t_dma.append(p.dma("gpsimd", "ax",
                    lambda g, dstB=dstB, gi=gi, q=q: g.dma_start(
                        out=dstB[0:RK, q * 1024:(q + 1) * 1024],
                        in_=bacc[q * RK:(q + 1) * RK, gi],
                        max_dma_last_dim=4096), after=[t_acc[NL - 1]]))
            t_dma.append(p.dma("gpsimd", "ax", lambda g, dstB=dstB:
                               g.dma_start(out=dstB[RK:RC, :],
                                           in_=self.i["b_sh"],
                                           max_dma_last_dim=8192)))
        stO.close()
        self.t_lora_done = self.tok("ax")
        return self.t_lora_done

    def p3(self):
        p = self
        st3 = ExitStack()
        st_ln3 = ExitStack()
        hm_sb = self.sb(st3, "hm_sb", [128, 2], F32)
        cb_sb = self.sb(st3, "cb_sb", [128, 3 * WCH], F32)
        bqb_sb = self.sb(st3, "bqb_sb", [128, WCH], F32)
        bkb_sb = self.sb(st3, "bkb_sb", [128, WCH], F32)
        bvb_sb = self.sb(st3, "bvb_sb", [1, WB], F32)
        bo_sb = self.sb(st3, "bo_sb", [128, WCH], F32)
        pjb_sb = self.sb(st3, "pjb_sb", [128, NCH], F32)
        mask_sb = self.sb(st3, "mask_sb", [128, BLK], F32)
        ps_c = [self.ps(st3, f"ps_c{k}", [128, ST]) for k in range(2)]

        for nm, dst in (("hmask", hm_sb), ("conv_b_r", cb_sb),
                        ("bqb_r", bqb_sb), ("bkb_r", bkb_sb),
                        ("bvb_row", bvb_sb), ("bo_r", bo_sb),
                        ("projb_r", pjb_sb), ("mask8", mask_sb)):
            p.dma("sync", "cs", lambda s, nm=nm, dst=dst: s.dma_start(
                out=dst[:], in_=self.i[nm]))
        t_c7 = t_c0 = t_c1 = t_c2 = t_c3 = t_c4 = t_c5 = t_c6 = p.tok("cs")
        ln3 = self.sb(st_ln3, "ln3", [128, NCH, TE], BF16)
        stL = ExitStack()
        xbstr = [self.sb(stL, f"xb3{k}", [128, NCH, LT], BF16)
                 for k in range(2)]
        ps_stat = self.ps(stL, "ps_stat3", [1, 2 * ST])
        ps_bcm = self.ps(stL, "ps_bcm3", [128, ST])
        ps_bcr = self.ps(stL, "ps_bcr3", [128, ST])
        ltiles = [(k * LT, LT) for k in range(4)] + [(T, 2)]
        t_prev = None
        t_statD = {}
        t_lnd3 = {}
        base_toks = [self.tok("st0"), self.tok("st1"), self.tok("st2"),
                     self.tok("T"), self.tok("V"), self.tok("S")]
        for li, (c0, wt) in enumerate(ltiles):
            lp = li % 2
            a = list(base_toks) if li < 2 else [t_lnd3[li - 2],
                                               t_statD[li - 2]]
            t_xs = p.dma("gpsimd", "xb0" if lp == 0 else "xb1",
                         lambda g, c0=c0, wt=wt, lp=lp: g.dma_start(
                             out=xbstr[lp][:, :, 0:wt],
                             in_=self.xh[:, :, c0:c0 + wt],
                             max_dma_last_dim=4096), after=a)
            t_bce, t_stat, t_sq = self.ln_stats(
                (xbstr[lp], 0), (xbstr[lp], 0), ln3, c0, wt,
                [t_xs] + ([t_prev] if t_prev else []),
                ps_stat, ps_bcm, ps_bcr)
            t_statD[li] = t_stat
            t_lnd3[li] = self.ln_apply(xbstr[lp], 0, ln3, c0, wt,
                                       [t_bce, t_stat])
            t_prev = t_lnd3[li]
        stL.close()
        t_hm = p.op("vector", lambda v: (
            v.tensor_scalar_mul(ln3[:, :, 0:1], ln3[:, :, 0:1],
                                hm_sb[:, 0:1]),
            v.tensor_scalar_mul(ln3[:, :, TE - 1:TE], ln3[:, :, TE - 1:TE],
                                hm_sb[:, 1:2]))[-1], after=[t_prev, t_c7])
        t_ln3 = t_hm

        stCV = ExitStack()
        cwb = [self.sb(stCV, f"cwb{k}", [128, NCH, 128], BF16, side="right")
               for k in range(3)]
        qcv = self.sb(stCV, "qcv", [128, WCH, T], BF16, side="right")
        kcv = self.sb(stCV, "kcv", [128, WCH, T], BF16, side="right")
        vcv = self.sb(stCV, "vcv", [128, WCH, T], BF16, side="right")
        cwq = ["wc0", "wc1"]
        t_cwL = []
        t_cmmL = []
        t_cevL = []
        ci = 0
        for o in range(3 * WCH):
            dstbuf = (qcv, kcv, vcv)[o // WCH]
            oc = o % WCH
            for tap in range(3):
                wi_ = o * 3 + tap
                opr = wi_ % 3
                a = [t_cmmL[2 * (wi_ - 3) + 1]] if wi_ >= 3 else [t_ln3]
                t_cwL.append(p.dma("gpsimd", f"wc{wi_ % 3}",
                             lambda g, o=o, tap=tap, opr=opr: g.dma_start(
                                 out=cwb[opr][:],
                                 in_=self.i["conv_pk"][:, o, tap],
                                 max_dma_last_dim=8192), after=a))
                for qt in range(2):
                    fp = ci % 2

                    def cmm(t, opr=opr, tap=tap, qt=qt, fp=fp):
                        for kc in range(NCH):
                            last = t.matmul(
                                ps_c[fp][:], cwb[opr][:, kc, :],
                                ln3[:, kc, qt * ST + tap:qt * ST + tap + ST],
                                start=(tap == 0 and kc == 0),
                                stop=(tap == 2 and kc == NCH - 1))
                        return last
                    a = [t_cwL[wi_], t_ln3]
                    if tap == 0 and o >= 1:
                        a.append(t_cevL[(o - 1) * 2 + qt])
                    t_cmmL.append(p.op("tensor", cmm, after=a))
                    if tap == 2:
                        t_cevL.append(p.op("scalar",
                            lambda sc, o=o, oc=oc, qt=qt, fp=fp,
                            dstbuf=dstbuf: sc.activation(
                                dstbuf[:, oc, qt * ST:(qt + 1) * ST],
                                ps_c[fp][:], AF.Identity,
                                bias=cb_sb[:, o:o + 1]),
                            after=[t_cmmL[-1], t_c0]))
                    ci += 1
        t_conv_done = self.tok("S")
        st_ln3.close()

        stH = ExitStack()
        wsm = [self.sb(stH, f"wsm{k}", [128, WCH, 128], BF16)
               for k in range(3)]
        wvb_sb = self.sb(stH, "wvb_sb", [128, WCH, HB * DHB], BF16)
        qhx = self.sb(stH, "qhx", [128, WCH, T], BF16)
        khx = self.sb(stH, "khx", [128, WCH, T], BF16)
        vhx = self.sb(stH, "vhx", [128, NBLK, WB], BF16)
        t_wvb = p.dma("gpsimd", "wres", lambda g: g.dma_start(
            out=wvb_sb[:], in_=self.i["wvb"].rearrange("(n p) d -> p n d",
                                                       p=128),
            max_dma_last_dim=4096), after=[self.tok("T")])
        t_hwL = {}
        t_hmmL = []
        t_hevL = []
        hi = 0
        for which, (wpk, srcbuf, dstbuf, bias_sb, bias_tok) in enumerate((
                ("wqb_pk", qcv, qhx, bqb_sb, t_c1),
                ("wkb_pk", kcv, khx, bkb_sb, t_c2))):
            for he in range(WCH):
                idx = which * WCH + he
                hp = idx % 3
                a = [t_hmmL[-5]] if idx >= 3 else [self.tok("T")]
                t_hwL[idx] = p.dma("gpsimd", f"wc{idx % 3}",
                                   lambda g, wpk=wpk, he=he, hp=hp:
                                   g.dma_start(out=wsm[hp][:],
                                               in_=self.i[wpk][:, he],
                                               max_dma_last_dim=8192),
                                   after=a)
                for qt in range(2):
                    fp = hi % 2

                    def hmm(t, hp=hp, qt=qt, fp=fp, srcbuf=srcbuf):
                        for kc in range(WCH):
                            last = t.matmul(
                                ps_c[fp][:], wsm[hp][:, kc, :],
                                srcbuf[:, kc, qt * ST:(qt + 1) * ST],
                                start=(kc == 0), stop=(kc == WCH - 1))
                        return last
                    a = [t_hwL[idx], t_conv_done]
                    if hi >= 2:
                        a.append(t_hevL[hi - 2])
                    t_hmmL.append(p.op("tensor", hmm, after=a))
                    t_hevL.append(p.op("scalar", lambda sc, he=he, qt=qt,
                                       fp=fp, dstbuf=dstbuf, bias_sb=bias_sb:
                                       sc.activation(
                                           dstbuf[:, he, qt * ST:(qt + 1) * ST],
                                           ps_c[fp][:], AF.Identity,
                                           bias=bias_sb[:, he:he + 1]),
                                       after=[t_hmmL[hi], bias_tok]))
                    hi += 1
        t_vmmL = []
        t_vevL = []
        vi = 0
        for tc in range(NBLK):
            for hf in range(2):
                fp = vi % 2

                def vmm(t, tc=tc, hf=hf, fp=fp):
                    t.matmul(ps_c[fp][:], self.ones_row[:],
                             bvb_sb[0:1, hf * 512:(hf + 1) * 512], start=True,
                             stop=False)
                    for kc in range(WCH):
                        last = t.matmul(ps_c[fp][:],
                                        vcv[:, kc, tc * 128:(tc + 1) * 128],
                                        wvb_sb[:, kc, hf * 512:(hf + 1) * 512],
                                        start=False, stop=(kc == WCH - 1))
                    return last
                a = [t_conv_done, t_c3, t_wvb]
                if vi >= 2:
                    a.append(t_vevL[vi - 2])
                if vi == 0 and t_hevL:
                    a.append(t_hevL[-1])
                t_vmmL.append(p.op("tensor", vmm, after=a))
                t_vevL.append(p.op("scalar", lambda sc, tc=tc, hf=hf, fp=fp:
                                   sc.activation(
                                       vhx[:, tc, hf * 512:(hf + 1) * 512],
                                       ps_c[fp][:], AF.Copy),
                                   after=[t_vmmL[vi]]))
                vi += 1
        t_heads_done = self.tok("S")
        stCV.close()

        stN = ExitStack()
        nsq = self.sb(stN, "nsq", [128, WCH, T], BF16)
        ps_n3 = self.ps(stN, "ps_n3", [1, 1024])
        t_nprev = None
        t_nrd = None
        for which, (buf, dst) in enumerate(((qhx, self.qnb), (khx, self.knb))):
            a = [t_heads_done]
            if t_nprev:
                a.append(t_nprev)
            t_sqx = p.op("scalar", lambda sc, buf=buf: sc.activation(
                nsq[:], buf[:], AF.Square), after=a)
            for h in range(HB):
                base = 64 * (h % 2)
                ch = h // 2
                a = [t_sqx]
                if t_nrd:
                    a.append(t_nrd)
                t_nmm = p.op("tensor", lambda t, base=base, ch=ch: (
                    t.matmul(ps_n3[0:1, 0:512],
                             self.ones_col[base:base + 64, :],
                             nsq[base:base + 64, ch, 0:512], start=True,
                             stop=True),
                    t.matmul(ps_n3[0:1, 512:1024],
                             self.ones_col[base:base + 64, :],
                             nsq[base:base + 64, ch, 512:1024], start=True,
                             stop=True))[-1], after=a)
                t_nrd = p.op("vector", lambda v, h=h, dst=dst: v.reduce_max(
                    dst[0:1, h:h + 1], ps_n3[0:1, :], axis=AX.X),
                    after=[t_nmm])
            t_nprev = t_nmm
        t_mb1 = p.op("vector", lambda v: v.tensor_tensor(
            self.mrow[0:1, 0:HB], self.qnb[:], self.knb[:], op=OP.mult),
            after=[t_nrd])
        t_mb2 = p.op("scalar", lambda sc: sc.activation(
            self.mrow[0:1, 0:HB], self.mrow[0:1, 0:HB], AF.Ln,
            bias=self.eps_t[:]), after=[t_mb1, self.tok("S")])
        t_mb3 = p.op("scalar", lambda sc: sc.activation(
            self.mrow[0:1, 0:HB], self.mrow[0:1, 0:HB], AF.Exp, scale=0.5),
            after=[t_mb2])
        t_mb4 = p.op("scalar", lambda sc: sc.activation(
            self.msqrow[0:1, 0:HB], self.mrow[0:1, 0:HB], AF.Copy,
            scale=-ISQB, bias=-1.0), after=[t_mb3])
        t_mb5 = p.op("tensor", lambda t: t.matmul(
            ps_c[0][:, 0:HB], self.ones_row[:], self.msqrow[0:1, 0:HB],
            start=True, stop=True), after=[t_mb4, self.tok("T")])
        t_mb6 = p.op("vector", lambda v: v.tensor_copy(
            self.negMb[:], ps_c[0][:, 0:HB]), after=[t_mb5])
        stN.close()

        stB = ExitStack()
        sm_sb = self.sb(stB, "sm_sb", [128, NBLK * BLK], F32, side="right")
        pbx = self.sb(stB, "pbx", [128, NBLK * BLK], BF16, side="right")
        rcb_row = self.sb(stB, "rcb_row", [128, NBLK * BLK], F32, side="right")
        rcb_sb = self.sb(stB, "rcb_sb", [128, NBLK * BLK], F32, side="right")
        ohx = self.sb(stB, "ohx", [128, WCH, T], BF16, side="right")
        ps_sb3 = self.ps(stB, "ps_sb3", [128, NBLK * BLK])
        ps_db3 = self.ps(stB, "ps_db3", [128, NBLK * BLK])
        ps_pv3 = self.ps(stB, "ps_pv3", [128, NBLK * BLK])
        mb_ap = mask_sb[:]
        mask_bc = bass.AP(tensor=mb_ap.tensor, offset=mb_ap.offset,
                          ap=[[mb_ap.ap[0][0], 128], [0, NBLK], [1, BLK]])
        t_bs = None
        t_msk = None
        t_pb = None
        t_bd = None
        t_br1 = None
        t_br2 = None
        t_br4 = None
        t_bp = None
        t_orc = None
        for h in range(HB):
            base = 64 * (h % 2)
            ch = h // 2

            def bsm(t, base=base, ch=ch):
                for b in range(NBLK):
                    last = t.matmul(
                        ps_sb3[:, b * BLK:(b + 1) * BLK],
                        khx[base:base + 64, ch, b * BLK:(b + 1) * BLK],
                        qhx[base:base + 64, ch, b * BLK:(b + 1) * BLK],
                        start=True, stop=True)
                return last
            a = [t_heads_done, t_mb6]
            if t_msk:
                a.append(t_msk)
            t_bs = p.op("tensor", bsm, after=a)
            a = [t_bs, t_c6]
            if t_pb:
                a.append(t_pb)
            t_msk = p.op("vector", lambda v: v.tensor_tensor(
                sm_sb[:].rearrange("p (b c) -> p b c", b=NBLK),
                ps_sb3[:].rearrange("p (b c) -> p b c", b=NBLK),
                mask_bc, op=OP.add), after=a)
            a = [t_msk]
            if t_bd:
                a.append(t_bd)
            if t_bp:
                a.append(t_bp)
            t_pb = p.op("scalar", lambda sc, h=h: sc.activation(
                pbx[:], sm_sb[:], AF.Exp, scale=ISQB,
                bias=self.negMb[:, h:h + 1]), after=a)

            def bdn(t):
                for b in range(NBLK):
                    last = t.matmul(ps_db3[:, b * BLK:(b + 1) * BLK],
                                    self.ones_mat[:],
                                    pbx[:, b * BLK:(b + 1) * BLK],
                                    start=True, stop=True)
                return last
            a = [t_pb]
            if t_br1:
                a.append(t_br1)
            if t_br4:
                a.append(t_br4)
            t_bd = p.op("tensor", bdn, after=a)
            a = [t_bd]
            if t_br2:
                a.append(t_br2)
            t_br1 = p.op("scalar", lambda sc: sc.activation(
                rcb_row[:], ps_db3[:], AF.Ln), after=a)
            a = [t_br1]
            if t_br4:
                a.append(t_br4)
            t_br2 = p.op("scalar", lambda sc: sc.activation(
                rcb_sb[:], rcb_row[:], AF.Exp, scale=-1.0), after=a)
            t_br4 = t_br2

            def bpv(t, h=h, base=base):
                for b in range(NBLK):
                    last = t.matmul(
                        ps_pv3[base:base + 64, b * BLK:(b + 1) * BLK],
                        vhx[:, b, h * DHB:(h + 1) * DHB],
                        pbx[:, b * BLK:(b + 1) * BLK], start=True, stop=True)
                return last
            a = [t_pb]
            if t_orc:
                a.append(t_orc)
            t_bp = p.op("tensor", bpv, after=a)
            t_orc = p.op("vector", lambda v, base=base, ch=ch:
                         v.tensor_tensor(ohx[base:base + 64, ch, :],
                                         ps_pv3[base:base + 64, :],
                                         rcb_sb[base:base + 64, :],
                                         op=OP.mult), after=[t_bp, t_br4])
        t_attn_done = self.tok("V")
        stH.close()
        stB_keepohx = ohx
        stW = ExitStack()
        wsm2 = [self.sb(stW, f"wsm2{k}", [128, WCH, 128], BF16)
                for k in range(3)]
        o1x = self.sb(stW, "o1x", [128, WCH, T], BF16)
        t_woL = {}
        t_womL = []
        t_woeL = []
        wi = 0
        for wc_ in range(WCH):
            wp = wc_ % 3
            a = [t_womL[-5]] if wc_ >= 3 else [self.tok("T")]
            t_woL[wc_] = p.dma("gpsimd", f"wc{wc_ % 3}",
                               lambda g, wc_=wc_, wp=wp:
                               g.dma_start(out=wsm2[wp][:],
                                           in_=self.i["wo_pk"][:, wc_],
                                           max_dma_last_dim=8192), after=a)
            for qt in range(2):
                fp = wi % 2

                def womm(t, wp=wp, qt=qt, fp=fp):
                    for kc in range(WCH):
                        last = t.matmul(ps_c[fp][:], wsm2[wp][:, kc, :],
                                        ohx[:, kc, qt * ST:(qt + 1) * ST],
                                        start=(kc == 0), stop=(kc == WCH - 1))
                    return last
                a = [t_woL[wc_], t_attn_done]
                if wi >= 2:
                    a.append(t_woeL[wi - 2])
                t_womL.append(p.op("tensor", womm, after=a))
                t_woeL.append(p.op("scalar", lambda sc, wc_=wc_, qt=qt, fp=fp:
                                   sc.activation(
                                       o1x[:, wc_, qt * ST:(qt + 1) * ST],
                                       ps_c[fp][:], AF.Identity,
                                       bias=bo_sb[:, wc_:wc_ + 1]),
                                   after=[t_womL[wi], t_c4]))
                wi += 1
        t_wo_done = self.tok("S")
        stB.close()

        stP = ExitStack()
        wsm3 = [self.sb(stP, f"wsm3{k}", [128, WCH, 128], BF16, side="right")
                for k in range(3)]
        x2c = [self.sb(stP, f"x2c{k}", [128, ST], F32, side="right")
               for k in range(2)]
        xoc = [self.sb(stP, f"xoc3{k}", [128, ST], F32, side="right")
               for k in range(2)]
        t_pwL = {}
        t_pmmL = []
        t_puL = []
        t_pstL = []
        pi = 0
        for do in range(NCH):
            dp = do % 3
            a = [t_pmmL[-5]] if do >= 3 else [self.tok("T")]
            t_pwL[do] = p.dma("gpsimd", f"wc{do % 3}", lambda g, do=do, dp=dp:
                              g.dma_start(out=wsm3[dp][:],
                                          in_=self.i["projw_pk"][:, do],
                                          max_dma_last_dim=8192), after=a)
            for qt in range(2):
                fp = pi % 2
                a = []
                if pi >= 2:
                    a.append(t_puL[pi - 2])
                else:
                    a += [self.tok("T"), self.tok("V"), self.tok("S")]
                t_x2 = p.dma("sync", "xr0" if fp == 0 else "xr1",
                             lambda s, do=do, qt=qt, fp=fp: s.dma_start(
                                 out=x2c[fp][:],
                                 in_=self.xh[:, do, 1 + qt * ST:1 + (qt + 1) * ST]),
                             after=a)

                def pmm(t, dp=dp, qt=qt, fp=fp):
                    for kc in range(WCH):
                        last = t.matmul(ps_c[fp][:], wsm3[dp][:, kc, :],
                                        o1x[:, kc, qt * ST:(qt + 1) * ST],
                                        start=(kc == 0), stop=(kc == WCH - 1))
                    return last
                a = [t_pwL[do], t_wo_done]
                if pi >= 2:
                    a.append(t_puL[pi - 2])
                t_pmmL.append(p.op("tensor", pmm, after=a))
                a = [t_pmmL[pi], t_x2, t_c5]
                if pi >= 2:
                    a.append(t_pstL[pi - 2])
                t_puL.append(p.op("vector", lambda v, do=do, fp=fp:
                                  v.scalar_tensor_tensor(
                                      xoc[fp][:], ps_c[fp][:],
                                      pjb_sb[:, do:do + 1], x2c[fp][:],
                                      op0=OP.add, op1=OP.add), after=a))
                t_pstL.append(p.dma("sync", "st0" if fp == 0 else "st1",
                    lambda s, do=do, qt=qt, fp=fp: s.dma_start(
                        out=self.xh[:, do, 1 + qt * ST:1 + (qt + 1) * ST],
                        in_=xoc[fp][:]), after=[t_puL[pi]]))
                pi += 1
        stW.close()
        stP.close()
        st3.close()


def _host_prep(inputs, core):
    x = np.asarray(inputs["L0_attn_a_x"])
    b, s0 = core // 4, (core % 4) * T
    out = {}
    xt = np.ascontiguousarray(x[b].T)
    ext = np.roll(xt, -(s0 - 1), axis=1)[:, 0:TE]
    out["xown"] = np.ascontiguousarray(
        ext.reshape(NCH, 128, TE).transpose(1, 0, 2))
    inv = (1.0 / 10000.0 ** (np.arange(0, DH, 2, dtype=np.float32) / DH)
           ).astype(np.float32)
    ang = np.arange(SEQ, dtype=np.float32)[:, None] * inv[None, :]
    cos = np.cos(ang).astype(np.float32).T
    sin = np.sin(ang).astype(np.float32).T
    cosd = np.concatenate([cos, cos], axis=0)
    sins = np.concatenate([-sin, sin], axis=0)
    out["cosd_own"] = np.ascontiguousarray(
        np.roll(cosd, -(s0 - 1), axis=1)[:, 0:TE])
    out["sins_own"] = np.ascontiguousarray(
        np.roll(sins, -(s0 - 1), axis=1)[:, 0:TE])
    out["wk"] = np.asarray(inputs["L0_attn_a_wk"])
    out["wv"] = np.asarray(inputs["L0_attn_a_wv"])
    wq = np.asarray(inputs["L0_attn_a_wq"])
    out["wq_pk"] = np.ascontiguousarray(
        wq.reshape(NCH, 128, H, 128).transpose(1, 2, 0, 3))
    out["bq_r"] = np.ascontiguousarray(
        np.asarray(inputs["L0_attn_a_bq"]).reshape(H, 128).T)
    out["bk_r"] = np.ascontiguousarray(
        np.asarray(inputs["L0_attn_a_bk"]).reshape(KVH, 128).T)
    out["bv_row"] = np.asarray(inputs["L0_attn_a_bv"]).reshape(1, KVH * DH)
    cw = np.asarray(inputs["L0_attn_b_conv_w"])
    out["conv_pk"] = np.ascontiguousarray(
        cw.reshape(3, NCH, 128, 3 * WCH, 128).transpose(2, 3, 0, 1, 4))
    out["conv_b_r"] = np.ascontiguousarray(
        np.asarray(inputs["L0_attn_b_conv_b"]).reshape(3 * WCH, 128).T)
    for nm, key in (("wqb_pk", "L0_attn_b_wq"), ("wkb_pk", "L0_attn_b_wk")):
        w = np.asarray(inputs[key]).reshape(WB, HB * DHB)
        out[nm] = np.ascontiguousarray(
            w.reshape(WCH, 128, WCH, 128).transpose(1, 2, 0, 3))
    out["wvb"] = np.ascontiguousarray(
        np.asarray(inputs["L0_attn_b_wv"]).reshape(WB, HB * DHB))
    out["bqb_r"] = np.ascontiguousarray(
        np.asarray(inputs["L0_attn_b_bq"]).reshape(WCH, 128).T)
    out["bkb_r"] = np.ascontiguousarray(
        np.asarray(inputs["L0_attn_b_bk"]).reshape(WCH, 128).T)
    out["bvb_row"] = np.asarray(inputs["L0_attn_b_bv"]).reshape(1, HB * DHB)
    wo = np.asarray(inputs["L0_attn_b_wo"]).reshape(HB * DHB, WB)
    out["wo_pk"] = np.ascontiguousarray(
        wo.reshape(WCH, 128, WCH, 128).transpose(1, 2, 0, 3))
    out["bo_r"] = np.ascontiguousarray(
        np.asarray(inputs["L0_attn_b_bo"]).reshape(WCH, 128).T)
    pw = np.asarray(inputs["L0_attn_b_proj_w"])
    out["projw_pk"] = np.ascontiguousarray(
        pw.reshape(WCH, 128, NCH, 128).transpose(1, 2, 0, 3))
    out["projb_r"] = np.ascontiguousarray(
        np.asarray(inputs["L0_attn_b_proj_b"]).reshape(NCH, 128).T)
    fk = np.asarray(inputs["L0_ffn_kernel"])
    out["fker_pk"] = np.ascontiguousarray(
        fk.reshape(NCH, 128, NCH, 128).transpose(1, 2, 0, 3))
    out["fbias_r"] = np.ascontiguousarray(
        np.asarray(inputs["L0_ffn_bias"]).reshape(NCH, 128).T)
    la = np.asarray(inputs["L0_ffn_lora_A"])
    out["a_pk"] = np.ascontiguousarray(
        la.reshape(NL, RK, NCH, 128).transpose(3, 0, 2, 1))
    lb = np.asarray(inputs["L0_ffn_lora_B"])
    out["b_pk"] = np.ascontiguousarray(
        lb.transpose(0, 2, 1).reshape(NL, RK, 4, 1024)
        .transpose(2, 1, 0, 3).reshape(128, NL, 1024))
    out["a_sh"] = np.ascontiguousarray(
        np.asarray(inputs["L0_ffn_lora_A_shared"]).T.reshape(NCH, 128, RK)
        .transpose(1, 0, 2))
    out["b_sh"] = np.ascontiguousarray(
        np.asarray(inputs["L0_ffn_lora_B_shared"]).T)
    gate = np.asarray(inputs["L0_ffn_gate"])
    g3 = np.zeros((3, NL), dtype=np.float32)
    g3[1] = gate[core]
    if s0 > 0:
        g3[0] = gate[core - 1]
    if s0 + T < SEQ:
        g3[2] = gate[core + 1]
    out["g3_row"] = np.ascontiguousarray(g3.reshape(1, 3 * NL))
    hm = np.zeros((128, 2), dtype=np.float32)
    hm[:, 0] = 1.0 if s0 > 0 else 0.0
    hm[:, 1] = 1.0 if s0 + T < SEQ else 0.0
    out["hmask"] = hm
    mk = np.where(np.arange(BLK)[:, None] <= np.arange(BLK)[None, :], 0.0,
                  NEGBIG).astype(np.float32)
    out["mask8"] = np.ascontiguousarray(mk)
    for k in out:
        out[k] = np.ascontiguousarray(out[k], dtype=np.float32)
    return out


_NC_CACHE = {}


def get_nc(phases=("p0", "p1", "p2", "p3", "p4"), debug=()):
    key = (tuple(phases), tuple(debug))
    if key not in _NC_CACHE:
        kb = KB(phases=phases, debug=debug)
        _NC_CACHE[key] = kb.build()
    return _NC_CACHE[key]


def kernel(**inputs):
    nc = get_nc()
    in_maps = [_host_prep(inputs, c) for c in range(NCORES)]
    res = run_bass_kernel_spmd(nc, in_maps, core_ids=list(range(NCORES)))
    y = np.empty((BATCH, SEQ, D), dtype=np.float32)
    for c in range(NCORES):
        b, s0 = c // 4, (c % 4) * T
        yt = res.results[c]["y_t"]
        y[b, s0:s0 + T, :] = yt.transpose(2, 1, 0).reshape(T, D)
    return y


# version 17 (final)
# speedup vs baseline: 1.0231x; 1.0136x over previous
import math
from contextlib import ExitStack

import numpy as np

import concourse.bass as bass
import concourse.mybir as mybir
from concourse.bass_utils import run_bass_kernel_spmd

F32 = mybir.dt.float32
BF16 = mybir.dt.bfloat16
AF = mybir.ActivationFunctionType
OP = mybir.AluOpType
AX = mybir.AxisListType

D = 4096
NCH = D // 128
SEQ = 4096
BATCH = 2
NCORES = 8
H = 32
KVH = 8
DH = 128
WB = 1024
WCH = WB // 128
HB = 16
DHB = 64
BLK = 128
NBLK = 8
RK = 32
NL = 16
RC = 64
T = 1024
TE = T + 2
ST = 512
LT = 256
NSTRIP = SEQ // ST
EPS = 1e-5
ISQD = 1.0 / math.sqrt(DH)
ISQB = 1.0 / math.sqrt(DHB)
NEGBIG = -8.0e9

ENGINES = ("tensor", "vector", "scalar", "sync", "gpsimd")
ESEM = {"tensor": "T", "vector": "V", "scalar": "S", "gpsimd": "G",
        "sync": "Y"}


class KB:
    def __init__(self, phases=("p0", "p1", "p2", "p3", "p4"), debug=()):
        self.phases = phases
        self.debug = debug
        self.nc = bass.Bass("TRN2", target_bir_lowering=False)
        self.es = ExitStack()
        self.prog = {e: [] for e in ENGINES}
        self.semh = {}
        self.cnt = {}

    def sem(self, *names):
        for n in names:
            if n not in self.semh:
                self.semh[n] = self.es.enter_context(self.nc.semaphore(n))
                self.cnt[n] = 0

    def _waits(self, eng, toks):
        best = {}
        for t in toks:
            if t is None:
                continue
            n, v = t
            if v > 0:
                best[n] = max(best.get(n, 0), v)
        for n, v in best.items():
            sem = self.semh[n]

            def w(e, sem=sem, v=v):
                e.wait_ge(sem, v)
            self.prog[eng].append(w)

    def op(self, eng, fn, after=()):
        self._waits(eng, after)
        name = ESEM[eng]
        self.cnt[name] += 1
        v = self.cnt[name]
        sem = self.semh[name]

        def run(e, fn=fn, sem=sem):
            fn(e).then_inc(sem, 1)
        self.prog[eng].append(run)
        return (name, v)

    def dma(self, eng, sname, fn, after=()):
        self._waits(eng, after)
        self.cnt[sname] += 16
        v = self.cnt[sname]
        sem = self.semh[sname]

        def run(e, fn=fn, sem=sem):
            fn(e).then_inc(sem, 16)
        self.prog[eng].append(run)
        return (sname, v)

    def tok(self, name):
        return (name, self.cnt[name])

    def cc(self, kind, ins, outs, after=()):
        self._waits("gpsimd", after)
        self.cnt["cc"] += 1
        v = self.cnt["cc"]
        sem = self.semh["cc"]

        def run(e, ins=ins, outs=outs, sem=sem):
            e.collective_compute(
                kind, OP.bypass,
                replica_groups=[[0, 1, 2, 3], [4, 5, 6, 7]],
                ins=ins, outs=outs).then_inc(sem, 1)
        self.prog["gpsimd"].append(run)
        return ("cc", v)

    def barrier(self, skip=("cc",)):
        toks = [(n, v) for n, v in self.cnt.items()
                if v > 0 and n not in skip]
        for eng in ENGINES:
            self._waits(eng, toks)

    def sb(self, stack, name, shape, dt, side=None):
        return stack.enter_context(
            self.nc.sbuf_tensor(name, shape, dt, side=side))

    def ps(self, stack, name, shape, dt=F32):
        return stack.enter_context(self.nc.psum_tensor(name, shape, dt))

    def build(self):
        nc = self.nc

        def din(name, shape, dt=F32):
            return nc.dram_tensor(name, shape, dt, kind="ExternalInput").ap()

        i = self.i = {}
        i["xown"] = din("xown", [128, NCH, TE])
        i["cosd_own"] = din("cosd_own", [128, TE])
        i["sins_own"] = din("sins_own", [128, TE])
        i["wk"] = din("wk", [D, KVH * DH])
        i["wv"] = din("wv", [D, KVH * DH])
        i["wq_pk"] = din("wq_pk", [128, H, NCH, 128])
        i["bq_r"] = din("bq_r", [128, H])
        i["bk_r"] = din("bk_r", [128, KVH])
        i["bv_row"] = din("bv_row", [1, KVH * DH])
        i["conv_pk"] = din("conv_pk", [128, 3 * WCH, 3, NCH, 128])
        i["conv_b_r"] = din("conv_b_r", [128, 3 * WCH])
        i["wqb_pk"] = din("wqb_pk", [128, WCH, WCH, 128])
        i["wkb_pk"] = din("wkb_pk", [128, WCH, WCH, 128])
        i["wvb"] = din("wvb", [WB, HB * DHB])
        i["bqb_r"] = din("bqb_r", [128, WCH])
        i["bkb_r"] = din("bkb_r", [128, WCH])
        i["bvb_row"] = din("bvb_row", [1, HB * DHB])
        i["wo_pk"] = din("wo_pk", [128, WCH, WCH, 128])
        i["bo_r"] = din("bo_r", [128, WCH])
        i["projw_pk"] = din("projw_pk", [128, NCH, WCH, 128])
        i["projb_r"] = din("projb_r", [128, NCH])
        i["fker_pk"] = din("fker_pk", [128, NCH, NCH, 128])
        i["fbias_r"] = din("fbias_r", [128, NCH])
        i["a_pk"] = din("a_pk", [128, NL, NCH, RK])
        i["b_pk"] = din("b_pk", [128, NL, 1024])
        i["a_sh"] = din("a_sh", [128, NCH, RK])
        i["b_sh"] = din("b_sh", [RK, D])
        i["g3_row"] = din("g3_row", [1, 3 * NL])
        i["hmask"] = din("hmask", [128, 2])
        i["mask8"] = din("mask8", [128, BLK])

        self.y_t = nc.dram_tensor("y_t", [128, NCH, T], F32,
                                  kind="ExternalOutput").ap()

        def dscr(name, shape, dt):
            kind = "ExternalOutput" if name in self.debug else "Internal"
            return nc.dram_tensor(name, shape, dt, kind=kind).ap()

        self.qT_s = dscr("qT_s", [H, DH, TE], BF16)
        self.xh = dscr("xh", [128, NCH, TE], F32)
        self.nrm_d = dscr("nrm_d", [128, H + HB], F32)
        self.kag_in = [dscr(f"kag_in{s}", [KVH * DH, ST], BF16)
                       for s in range(2)]
        self.kag_out = [dscr(f"kag_out{s}", [4 * KVH * DH, ST], BF16)
                        for s in range(2)]
        self.vag_in = dscr("vag_in", [KVH * 8 * 128, DH], BF16)
        self.vag_out = dscr("vag_out", [4 * KVH * 8 * 128, DH], BF16)
        self.knag_in = dscr("knag_in", [1, KVH], F32)
        self.knag_out = dscr("knag_out", [4, KVH], F32)

        es = self.es
        self.ones_col = self.sb(es, "ones_col", [128, 1], BF16)
        self.ones_row = self.sb(es, "ones_row", [1, 128], F32)
        self.ones_mat = self.sb(es, "ones_mat", [128, 128], BF16)
        self.eps_t = self.sb(es, "eps_t", [1, 1], F32)
        self.negM = self.sb(es, "negM", [128, H], F32)
        self.negMb = self.sb(es, "negMb", [128, HB], F32)
        self.qn = self.sb(es, "qn", [1, H], F32)
        self.kn = self.sb(es, "kn", [1, KVH], F32)
        self.qnb = self.sb(es, "qnb", [1, HB], F32)
        self.knb = self.sb(es, "knb", [1, HB], F32)
        self.mrow = self.sb(es, "mrow", [1, ST], F32)
        self.msqrow = self.sb(es, "msqrow", [1, ST], F32)
        self.varrow = self.sb(es, "varrow", [1, ST], F32)
        self.m_bc = self.sb(es, "m_bc", [128, ST], BF16)
        self.rstd_bc = self.sb(es, "rstd_bc", [128, ST], BF16)
        self.nrm_t = self.sb(es, "nrm_t", [1, 1], F32)
        self.lntmp = self.sb(es, "lntmp", [128, ST], BF16)

        self.sem("T", "V", "S", "G", "Y")
        self.sem("xa0", "xa1", "cs", "ck", "sw0", "sw1", "sw2", "sw3",
                 "wres", "lnw0", "lnw1",
                 "ln0", "ln1", "kw0", "kw1", "wc0", "wc1", "wc2", "kv0", "kv1",
                 "qr0", "qr1", "qr2", "qr3", "xr0", "xr1", "xr2",
                 "st0", "st1", "st2", "ax",
                 "lr0", "lr1", "xb0", "xb1", "ag", "cc", "kx0", "kx1")

        self.t_init = self.op("vector", lambda v: (
            v.memset(self.ones_col[:], 1.0),
            v.memset(self.ones_row[:], 1.0),
            v.memset(self.eps_t[:], EPS),
            v.memset(self.qn[:], 0.0),
            v.memset(self.kn[:], 0.0),
            v.memset(self.ones_mat[:], 1.0))[-1])

        if "p0" in self.phases:
            self.p0()
        if "p2" in self.phases:
            self.barrier()
            sl = self.stLora = ExitStack()
            self.a_cat = self.sb(sl, "a_cat", [128, NCH, RC], BF16,
                                 side="right")
            self.a_catL = self.sb(sl, "a_catL", [128, NCH, RC], BF16,
                                  side="right")
            self.a_catR = self.sb(sl, "a_catR", [128, NCH, RC], BF16,
                                  side="right")
            self.b_cat = self.sb(sl, "b_cat", [RC, D], BF16, side="right")
            self.b_catL = self.sb(sl, "b_catL", [RC, D], BF16, side="right")
            self.b_catR = self.sb(sl, "b_catR", [RC, D], BF16, side="right")
            self.build_lora()
        if "p1" in self.phases:
            if "p2" not in self.phases:
                self.barrier()
            self.p1()
        if "p2" in self.phases:
            self.barrier()
            self.ffn(first=True)
        if "p3" in self.phases:
            self.barrier()
            self.p3()
        if "p4" in self.phases:
            self.barrier()
            self.ffn(first=False)
        if "p2" in self.phases:
            self.stLora.close()

        if "p4" not in self.phases:
            self.barrier()
            self.dma("sync", "st0",
                     lambda s: s.dma_start(out=self.y_t,
                                           in_=self.i["xown"][:, :, 1:1 + T]))
        if self.debug:
            self.barrier()
            self.dma("sync", "st0",
                     lambda s: s.dma_start(out=self.nrm_d[:, 0:H],
                                           in_=self.negM[:]))
            if "p3" in self.phases:
                self.dma("sync", "st0",
                         lambda s: s.dma_start(out=self.nrm_d[:, H:H + HB],
                                               in_=self.negMb[:]))

        self._waits("sync", [self.tok("st0"), self.tok("st1"),
                             self.tok("st2"), self.tok("kw0"),
                             self.tok("kw1"), self.tok("lnw0"),
                             self.tok("lnw1"), self.tok("ln0"),
                             self.tok("ln1"), self.tok("ag"),
                             self.tok("cc")])

        with nc.Block() as block:
            block.tensor(lambda t: [fn(t) for fn in self.prog["tensor"]])
            block.vector(lambda v: [fn(v) for fn in self.prog["vector"]])
            block.scalar(lambda sc: [fn(sc) for fn in self.prog["scalar"]])
            block.sync(lambda s: [fn(s) for fn in self.prog["sync"]])
            block.gpsimd(lambda g: [fn(g) for fn in self.prog["gpsimd"]])
        self.es.close()
        return nc

    def ln_stats(self, meansrc, sqsrc, sq_buf, sqb_off, wt, after,
                 ps_stat, ps_bcm, ps_bcr):
        p = self
        msh, mso = meansrc
        sqh, sqo = sqsrc
        t_sq = p.op("scalar", lambda sc: sc.activation(
            sq_buf[:, :, sqb_off:sqb_off + wt], sqh[:, :, sqo:sqo + wt],
            AF.Square), after=after)

        def stats(t):
            for kc in range(NCH):
                t.matmul(ps_stat[0:1, 0:wt], self.ones_col[:],
                         msh[:, kc, mso:mso + wt], start=(kc == 0),
                         stop=(kc == NCH - 1))
            for kc in range(NCH):
                last = t.matmul(ps_stat[0:1, ST:ST + wt], self.ones_col[:],
                                sq_buf[:, kc, sqb_off:sqb_off + wt],
                                start=(kc == 0), stop=(kc == NCH - 1))
            return last
        t_stat = p.op("tensor", stats, after=list(after) + [t_sq, self.t_init])
        t_rows = p.op("scalar", lambda sc: (
            sc.activation(self.mrow[0:1, 0:wt], ps_stat[0:1, 0:wt], AF.Copy,
                          scale=1.0 / D),
            sc.activation(self.msqrow[0:1, 0:wt], ps_stat[0:1, ST:ST + wt],
                          AF.Copy, scale=1.0 / D))[-1], after=[t_stat])
        t_v1 = p.op("vector", lambda v: v.tensor_tensor(
            self.varrow[0:1, 0:wt], self.mrow[0:1, 0:wt],
            self.mrow[0:1, 0:wt], op=OP.mult), after=[t_rows])
        t_v2 = p.op("vector", lambda v: v.tensor_tensor(
            self.varrow[0:1, 0:wt], self.msqrow[0:1, 0:wt],
            self.varrow[0:1, 0:wt], op=OP.subtract), after=[t_v1])
        t_l = p.op("scalar", lambda sc: sc.activation(
            self.varrow[0:1, 0:wt], self.varrow[0:1, 0:wt], AF.Ln,
            bias=self.eps_t[:]), after=[t_v2])
        t_r = p.op("scalar", lambda sc: sc.activation(
            self.varrow[0:1, 0:wt], self.varrow[0:1, 0:wt], AF.Exp,
            scale=-0.5), after=[t_l])
        t_bc = p.op("tensor", lambda t: (
            t.matmul(ps_bcm[:, 0:wt], self.ones_row[:], self.mrow[0:1, 0:wt],
                     start=True, stop=True),
            t.matmul(ps_bcr[:, 0:wt], self.ones_row[:],
                     self.varrow[0:1, 0:wt], start=True, stop=True))[-1],
            after=[t_r])
        t_e = p.op("vector", lambda v: (
            v.tensor_copy(self.m_bc[:, 0:wt], ps_bcm[:, 0:wt]),
            v.tensor_copy(self.rstd_bc[:, 0:wt], ps_bcr[:, 0:wt]))[-1],
            after=[t_bc])
        return t_e, t_stat, t_sq

    def ln_apply(self, src, soff, dst, doff, wt, after):
        p = self
        prev = None
        for kc in range(NCH):
            a = list(after) if kc == 0 else [prev]
            ta = p.op("vector", lambda v, kc=kc: v.tensor_tensor(
                self.lntmp[:, 0:wt], src[:, kc, soff:soff + wt],
                self.m_bc[:, 0:wt], op=OP.subtract), after=a)
            tb = p.op("vector", lambda v, kc=kc: v.tensor_tensor(
                dst[:, kc, doff:doff + wt], self.lntmp[:, 0:wt],
                self.rstd_bc[:, 0:wt], op=OP.mult), after=[ta])
            prev = tb
        return prev

    def p0(self):
        p = self
        stP = self.stP0 = ExitStack()
        lnst_b = [self.sb(stP, f"lnst{k}", [128, NCH, ST], BF16)
                  for k in range(2)]
        lntl = self.sb(stP, "lntl", [128, NCH, 2], BF16)
        st0 = ExitStack()
        xb_b = [self.sb(st0, "xb0", [128, NCH, ST], BF16)] * 2
        wres = self.sb(st0, "wres", [128, NCH, KVH * DH], BF16)
        cosd = self.sb(st0, "cosd", [128, ST], F32)
        sins = self.sb(st0, "sins", [128, ST], F32)
        kf = [self.sb(st0, f"kf{k}", [128, ST], F32) for k in range(4)]
        ksw = [self.sb(st0, f"ksw{k}", [128, ST], F32) for k in range(4)]
        kbf = [self.sb(st0, f"kbf{k}", [128, ST], BF16) for k in range(4)]
        ksq = self.sb(st0, "ksq", [128, ST], BF16)
        bk_sb = self.sb(st0, "bk_sb", [128, KVH], F32)
        bv_sb = self.sb(st0, "bv_sb", [1, KVH * DH], F32)
        ps_stat = self.ps(st0, "ps_stat0", [1, 2 * ST])
        ps_bcm = self.ps(st0, "ps_bcm0", [128, ST])
        ps_bcr = self.ps(st0, "ps_bcr0", [128, ST])
        ps_k = [self.ps(st0, f"ps_k{k}", [128, ST]) for k in range(2)]
        ps_n = self.ps(st0, "ps_n0", [1, 2 * ST])

        p.dma("sync", "cs", lambda s: s.dma_start(out=bk_sb[:],
                                                  in_=self.i["bk_r"]))
        p.dma("sync", "cs", lambda s: s.dma_start(out=bv_sb[:],
                                                  in_=self.i["bv_row"]))
        t_bk = t_bv = p.tok("cs")
        t_wk = p.dma("gpsimd", "wres", lambda g: g.dma_start(
            out=wres[:], in_=self.i["wk"].rearrange("(n p) d -> p n d", p=128),
            max_dma_last_dim=4096))

        kag_v = [t.rearrange("(j p) t -> j p t", p=DH) for t in self.kag_in]
        STRIPS = ((0, ST, lnst_b[0], True), (ST, ST, lnst_b[1], True),
                  (2 * ST, 2, lntl, False))
        xa = ["xa0", "xa1"]
        t_x = {}
        t_statdone = {}
        t_lnapp = {}
        t_kmm = {}
        t_kevac = {}
        t_kr1 = {}
        t_kr3 = {}
        t_kw = {}
        t_ksq = {}
        t_knm = None
        t_knr = None
        t_knrd = {}
        t_knx = None
        t_rope_last = None
        t_cs2 = None
        def emit_xload(s, a):
            off, wt = STRIPS[s][0], STRIPS[s][1]
            xbuf = xb_b[s % 2]
            t_x[s] = p.dma("gpsimd", xa[s % 2], lambda g, off=off, wt=wt,
                           xbuf=xbuf:
                           g.dma_start(out=xbuf[:, :, 0:wt],
                                       in_=self.i["xown"][:, :, off:off + wt],
                                       max_dma_last_dim=4096), after=a)
        def flat(ap, rows):
            n = 1
            for st_, sz in ap.ap:
                n *= sz
            return bass.AP(tensor=ap.tensor, offset=ap.offset,
                           ap=[[n // rows, rows], [1, n // rows]])
        emit_xload(0, [])
        gi = -1
        for s, (off, wt, lnst, do_k) in enumerate(STRIPS):
            xbuf = xb_b[s % 2]
            if do_k:
                a = [t_rope_last] if t_rope_last else []
                t_cs1 = p.dma("sync", "ck", lambda ss, off=off: ss.dma_start(
                    out=cosd[:], in_=self.i["cosd_own"][:, off:off + ST]),
                    after=a)
                t_cs2 = p.dma("sync", "ck", lambda ss, off=off: ss.dma_start(
                    out=sins[:], in_=self.i["sins_own"][:, off:off + ST]))
            st_after = [t_x[s]]
            t_bce, t_stat, _ = self.ln_stats((xbuf, 0), (xbuf, 0), lnst, 0,
                                             wt, st_after, ps_stat, ps_bcm,
                                             ps_bcr)
            t_statdone[s] = t_stat
            t_lnapp[s] = self.ln_apply(xbuf, 0, lnst, 0, wt, [t_bce, t_stat])
            if s + 1 < len(STRIPS):
                emit_xload(s + 1, [t_statdone[s], t_lnapp[s]])
            if not do_k:
                continue
            for ko in range(KVH):
                gi += 1
                kp = gi % 2
                k4 = gi % 4
                sw = f"sw{k4}"
                kw = ("kw0" if kp == 0 else "kw1") if s == 0 else \
                     ("kx0" if kp == 0 else "kx1")
                mm_after = [t_lnapp[s], t_x[s], t_wk]
                if kp in t_kevac:
                    mm_after.append(t_kevac[kp])
                t_kmm[(s, ko)] = p.op("tensor", lambda t, ko=ko, kp=kp,
                                      lnst=lnst: [
                    t.matmul(ps_k[kp][:], wres[:, kc, ko * DH:(ko + 1) * DH],
                             lnst[:, kc, :], start=(kc == 0),
                             stop=(kc == NCH - 1))
                    for kc in range(NCH)][-1], after=mm_after)
                ev_after = [t_kmm[(s, ko)], t_bk]
                if k4 in t_kr3:
                    ev_after.append(t_kr3[k4])
                t_kevac[kp] = p.op("scalar", lambda sc, ko=ko, kp=kp, k4=k4:
                                   sc.activation(kf[k4][:], ps_k[kp][:],
                                                 AF.Identity,
                                                 bias=bk_sb[:, ko:ko + 1]),
                                   after=ev_after)
                sq_after = [t_kevac[kp]]
                if t_knm is not None:
                    sq_after.append(t_knm)
                t_ksq[k4] = p.op("scalar", lambda sc, k4=k4: sc.activation(
                    ksq[:], kf[k4][:], AF.Square), after=sq_after)
                np2 = gi % 2
                nm_after = [t_ksq[k4]]
                if t_knrd.get(np2) is not None:
                    nm_after.append(t_knrd[np2])
                t_knm = p.op("tensor", lambda t, np2=np2: t.matmul(
                    ps_n[0:1, np2 * ST:(np2 + 1) * ST], self.ones_col[:],
                    ksq[:], start=True, stop=True), after=nm_after)
                nr_after = [t_knm]
                if t_knx is not None:
                    nr_after.append(t_knx)
                t_knr = p.op("vector", lambda v, np2=np2: v.reduce_max(
                    self.nrm_t[:], ps_n[0:1, np2 * ST:(np2 + 1) * ST],
                    axis=AX.X), after=nr_after)
                t_knrd[np2] = t_knr
                t_knx = p.op("vector", lambda v, ko=ko: v.tensor_tensor(
                    self.kn[0:1, ko:ko + 1], self.kn[0:1, ko:ko + 1],
                    self.nrm_t[:], op=OP.max), after=[t_knr])
                sw_after = [t_kevac[kp]]
                if k4 in t_kr3:
                    sw_after.append(t_kr3[k4])
                p.dma("sync", sw, lambda ss, k4=k4: ss.dma_start(
                    out=ksw[k4][0:64, :], in_=kf[k4][64:128, :]),
                    after=sw_after)
                t_sw2 = p.dma("sync", sw, lambda ss, k4=k4: ss.dma_start(
                    out=ksw[k4][64:128, :], in_=kf[k4][0:64, :]))
                t_kr1[k4] = p.op("gpsimd", lambda v, k4=k4: v.tensor_tensor(
                    kf[k4][:], kf[k4][:], cosd[:], op=OP.mult),
                    after=[t_cs2, t_sw2, t_kevac[kp], t_ksq[k4]])
                t_r2 = p.op("gpsimd", lambda v, k4=k4: v.tensor_tensor(
                    ksw[k4][:], ksw[k4][:], sins[:], op=OP.mult),
                    after=[t_kr1[k4]])
                r3_after = [t_r2]
                if k4 in t_kw:
                    r3_after += [self.tok(t_kw[k4][0])]
                t_kr3[k4] = p.op("gpsimd", lambda v, k4=k4: v.tensor_tensor(
                    kbf[k4][:], kf[k4][:], ksw[k4][:], op=OP.add),
                    after=r3_after)
                t_rope_last = t_kr3[k4]
                t_kw[k4] = p.dma("sync", kw,
                                 lambda ss, ko=ko, s=s, k4=k4:
                                 ss.dma_start(
                                     out=kag_v[s][ko],
                                     in_=kbf[k4][:]), after=[t_kr3[k4]])
            if s == 0:
                t_ccK0 = self.cc(
                    "AllGather", ins=[flat(self.kag_in[0], 64)],
                    outs=[flat(self.kag_out[0], 256)],
                    after=[self.tok("kw0"), self.tok("kw1")])

        p.dma("sync", "kx0", lambda ss: ss.dma_start(
            out=self.knag_in, in_=self.kn[:]), after=[t_knx])
        t_ccK = self.t_ccK = self.cc(
            "AllGather", ins=[flat(self.kag_in[1], 64)],
            outs=[flat(self.kag_out[1], 256)],
            after=[self.tok("kx0"), self.tok("kx1")])
        t_ccN = self.cc("AllGather", ins=[self.knag_in.opt()],
                        outs=[self.knag_out.opt()])

        vag_v = self.vag_in.rearrange("(j g p) d -> j g p d", g=8, p=128)
        t_passa_T = self.tok("T")
        t_passa_V = self.tok("V")
        t_passa_S = self.tok("S")
        t_passa_k0 = self.tok("kx0")
        t_passa_k1 = self.tok("kx1")
        wv_v = self.i["wv"].rearrange("(n p) d -> p n d", p=128)
        t_wvh = []
        for hf, strm in enumerate(("wres", "xa1")):
            t_wvh.append(p.dma("gpsimd", strm, lambda g, hf=hf:
                               g.dma_start(
                                   out=wres[:, :, hf * 512:(hf + 1) * 512],
                                   in_=wv_v[:, :, hf * 512:(hf + 1) * 512],
                                   max_dma_last_dim=4096),
                               after=[t_passa_T] if hf == 0 else []))
        t_vmm = []
        t_vev = []
        t_vw = []
        for s in range(2):
            lbuf = lnst_b[s]
            for pc in range(4):
                for hf in range(2):
                    vi = s * 8 + pc * 2 + hf
                    kp = vi % 2
                    a = [t_lnapp[s], t_wvh[hf], t_bv]
                    if vi >= 2:
                        a.append(t_vev[vi - 2])
                    else:
                        a.append(t_passa_S)
                    t_vmm.append(p.op("tensor",
                        lambda t, pc=pc, hf=hf, kp=kp, lbuf=lbuf: (
                            t.matmul(ps_k[kp][:], self.ones_row[:],
                                     bv_sb[0:1, hf * 512:(hf + 1) * 512],
                                     start=True, stop=False),
                            [t.matmul(ps_k[kp][:],
                                      lbuf[:, kc, pc * 128:(pc + 1) * 128],
                                      wres[:, kc, hf * 512:(hf + 1) * 512],
                                      start=False, stop=(kc == NCH - 1))
                             for kc in range(NCH)])[1][-1], after=a))
                    a = [t_vmm[vi]]
                    if vi >= 2:
                        a.append(t_vw[vi - 2])
                    else:
                        a += [t_passa_k0, t_passa_k1, t_passa_S]
                    t_vev.append(p.op("scalar", lambda sc, kp=kp:
                                      sc.activation(kbf[kp][:], ps_k[kp][:],
                                                    AF.Copy), after=a))
                    g_ = s * 4 + pc
                    t_vw.append(p.dma("sync", "lnw0" if kp == 0 else "lnw1",
                        lambda ss, g_=g_, hf=hf, kp=kp: ss.dma_start(
                            out=vag_v[4 * hf:4 * hf + 4, g_, :, :]
                                .rearrange("k p d -> p k d"),
                            in_=kbf[kp][:].rearrange("p (k d) -> p k d", k=4)),
                        after=[t_vev[vi]]))

        t_ccV = self.t_ccV = self.cc(
            "AllGather", ins=[flat(self.vag_in, 64)],
            outs=[flat(self.vag_out, 256)],
            after=[self.tok("lnw0"), self.tok("lnw1")])

        st0.close()
        self.barrier()
        stC = ExitStack()
        wqcb = [self.sb(stC, f"wqcb{k}", [128, NCH, 128], BF16)
                for k in range(2)]
        cosdo = self.sb(stC, "cosdo", [128, TE], F32)
        sinso = self.sb(stC, "sinso", [128, TE], F32)
        qf = [self.sb(stC, f"qf{k}", [128, TE], F32) for k in range(4)]
        qsw = [self.sb(stC, f"qsw{k}", [128, TE], F32) for k in range(4)]
        qbf = [self.sb(stC, f"qbf{k}", [128, TE], BF16) for k in range(4)]
        qsq = self.sb(stC, "qsq", [128, TE], BF16)
        bq_sb = self.sb(stC, "bq_sb", [128, H], F32)
        ps_q = [self.ps(stC, f"ps_q{k}", [128, 1536]) for k in range(2)]
        ps_qn = self.ps(stC, "ps_qn", [1, 1024])

        p.dma("sync", "cs", lambda s: s.dma_start(
            out=cosdo[:], in_=self.i["cosd_own"]))
        p.dma("sync", "cs", lambda s: s.dma_start(
            out=sinso[:], in_=self.i["sins_own"]))
        p.dma("sync", "cs", lambda s: s.dma_start(out=bq_sb[:],
                                                  in_=self.i["bq_r"]))
        t_lno = t_cso2 = t_bq = p.tok("cs")
        lnseg = ((lnst_b[0], 0, ST), (lnst_b[1], ST, ST), (lntl, 2 * ST, 2))
        wc = ["wc0", "wc1"]
        t_wq = {}
        t_qmm = {}
        t_qev = {}
        t_qr1c = {}
        t_qr3c = {}
        t_qwc = {}
        t_qsqc = {}
        t_qnm = None
        t_qnr = None
        t_qnrd = {}
        for qh in range(H):
            qp = qh % 2
            q4 = qh % 4
            swq = f"sw{q4}"
            kwq = "ln0" if qp == 0 else "ln1"
            a = [t_qmm[qh - 2]] if qh >= 2 else []
            t_wq[qh] = p.dma("gpsimd", wc[qp], lambda g, qh=qh, qp=qp:
                             g.dma_start(out=wqcb[qp][:],
                                         in_=self.i["wq_pk"][:, qh],
                                         max_dma_last_dim=8192), after=a)
            a = [t_wq[qh], t_lno]
            if qh >= 2:
                a.append(t_qev[qh - 2])
            t_qmm[qh] = p.op("tensor", lambda t, qp=qp: [
                t.matmul(ps_q[qp][:, c0:c0 + wt], wqcb[qp][:, kc, :],
                         seg_h[:, kc, 0:wt], start=(kc == 0),
                         stop=(kc == NCH - 1))
                for (seg_h, c0, wt) in lnseg
                for kc in range(NCH)][-1], after=a)
            a = [t_qmm[qh], t_bq]
            if q4 in t_qr3c:
                a.append(t_qr3c[q4])
            t_qev[qh] = p.op("scalar", lambda sc, qh=qh, qp=qp, q4=q4:
                             sc.activation(
                                 qf[q4][:, 0:TE], ps_q[qp][:, 0:TE],
                                 AF.Identity,
                                 bias=bq_sb[:, qh:qh + 1]), after=a)
            sqa = [t_qev[qh]]
            if t_qnm is not None:
                sqa.append(t_qnm)
            t_qsqc[q4] = p.op("scalar", lambda sc, q4=q4: sc.activation(
                qsq[:], qf[q4][:], AF.Square), after=sqa)
            t_qnx = None
            for si, (sc0, sw) in enumerate(((0, 512), (512, 512), (1024, 2))):
                np2 = (qh * 3 + si) % 2
                nma = [t_qsqc[q4]]
                if t_qnrd.get(np2) is not None:
                    nma.append(t_qnrd[np2])
                t_qnm = p.op("tensor", lambda t, sc0=sc0, sw=sw, np2=np2:
                             t.matmul(
                    ps_qn[0:1, np2 * 512:np2 * 512 + sw], self.ones_col[:],
                    qsq[:, sc0:sc0 + sw], start=True, stop=True), after=nma)
                nra = [t_qnm]
                if t_qnx is not None:
                    nra.append(t_qnx)
                t_qnr = p.op("vector", lambda v, sw=sw, np2=np2: v.reduce_max(
                    self.nrm_t[:], ps_qn[0:1, np2 * 512:np2 * 512 + sw],
                    axis=AX.X), after=nra)
                t_qnrd[np2] = t_qnr
                t_qnx = p.op("vector", lambda v, qh=qh: v.tensor_tensor(
                    self.qn[0:1, qh:qh + 1], self.qn[0:1, qh:qh + 1],
                    self.nrm_t[:], op=OP.max), after=[t_qnr])
            t_qnr = t_qnx
            sw_after = [t_qev[qh]]
            if q4 in t_qr3c:
                sw_after.append(t_qr3c[q4])
            p.dma("sync", swq, lambda ss, q4=q4: ss.dma_start(
                out=qsw[q4][0:64, :], in_=qf[q4][64:128, :]),
                after=sw_after)
            t_s2 = p.dma("sync", swq, lambda ss, q4=q4: ss.dma_start(
                out=qsw[q4][64:128, :], in_=qf[q4][0:64, :]))
            t_qr1c[q4] = p.op("vector", lambda v, q4=q4: v.tensor_tensor(
                qf[q4][:], qf[q4][:], cosdo[:], op=OP.mult),
                after=[t_cso2, t_s2, t_qev[qh], t_qsqc[q4]])
            t_r2 = p.op("vector", lambda v, q4=q4: v.tensor_tensor(
                qsw[q4][:], qsw[q4][:], sinso[:], op=OP.mult),
                after=[t_qr1c[q4]])
            r3a = [t_r2]
            if q4 in t_qwc:
                r3a += [self.tok(kwq)]
            t_qr3c[q4] = p.op("vector", lambda v, q4=q4: v.tensor_tensor(
                qbf[q4][:], qf[q4][:], qsw[q4][:], op=OP.add), after=r3a)
            t_qwc[q4] = p.dma("sync", kwq,
                              lambda ss, qh=qh, q4=q4:
                              ss.dma_start(out=self.qT_s[qh],
                                           in_=qbf[q4][:]),
                              after=[t_qr3c[q4]])

        t_agl = None
        for r in range(4):
            a = [t_ccN, self.tok("S")] if r == 0 else []
            t_agl = p.dma("sync", "ag", lambda ss, r=r: ss.dma_start(
                out=self.mrow[0:1, 64 + r * KVH:64 + (r + 1) * KVH],
                in_=self.knag_out[r:r + 1, :]), after=a)
        t_kn = p.op("vector", lambda v: v.tensor_tensor(
            self.kn[0:1, :], self.mrow[0:1, 64:64 + KVH],
            self.mrow[0:1, 72:72 + KVH], op=OP.max),
            after=[t_agl, self.tok("V")])
        for c0 in (80, 88):
            t_kn = p.op("vector", lambda v, c0=c0: v.tensor_tensor(
                self.kn[0:1, :], self.kn[0:1, :],
                self.mrow[0:1, c0:c0 + KVH], op=OP.max), after=[t_kn])

        knb_ap = self.kn[:]
        kn_exp = bass.AP(tensor=knb_ap.tensor, offset=knb_ap.offset,
                         ap=[[knb_ap.ap[0][0], 1], [1, KVH], [0, 4]])
        t_m1 = p.op("vector", lambda v: v.tensor_tensor(
            self.mrow[0:1, 0:H].rearrange("p (a b) -> p a b", a=KVH),
            self.qn[:].rearrange("p (a b) -> p a b", a=KVH),
            kn_exp, op=OP.mult), after=[t_qnr, t_kn, self.tok("V")])
        t_m2 = p.op("scalar", lambda sc: sc.activation(
            self.mrow[0:1, 0:H], self.mrow[0:1, 0:H], AF.Ln,
            bias=self.eps_t[:]), after=[t_m1, self.tok("S")])
        t_m3 = p.op("scalar", lambda sc: sc.activation(
            self.mrow[0:1, 0:H], self.mrow[0:1, 0:H], AF.Exp, scale=0.5),
            after=[t_m2])
        t_m4 = p.op("scalar", lambda sc: sc.activation(
            self.msqrow[0:1, 0:H], self.mrow[0:1, 0:H], AF.Copy, scale=-ISQD,
            bias=-1.0), after=[t_m3])
        t_m5 = p.op("tensor", lambda t: t.matmul(
            ps_q[0][:, 0:H], self.ones_row[:], self.msqrow[0:1, 0:H],
            start=True, stop=True), after=[t_m4, self.tok("T")])
        t_m6 = p.op("vector", lambda v: v.tensor_copy(
            self.negM[:], ps_q[0][:, 0:H]), after=[t_m5])
        stC.close()
        stP.close()

    def p1(self):
        p = self
        guard = getattr(self, "t_psg_guard", None)
        st1 = ExitStack()
        kjb = [self.sb(st1, f"kjb{k}", [128, SEQ], BF16) for k in range(2)]
        vjb = [self.sb(st1, f"vjb{k}", [128, NCH, DH], BF16) for k in range(2)]
        qhb = [self.sb(st1, f"qhb{k}", [128, TE], BF16) for k in range(4)]
        pb = [self.sb(st1, f"pb{k}", [128, 1024], BF16) for k in range(2)]
        ptl = self.sb(st1, "ptl", [128, 16], BF16)
        dnl_sb = self.sb(st1, "dnl_sb", [128, ST], F32)
        rc_sb = self.sb(st1, "rc_sb", [128, ST], F32)
        tt_sb = self.sb(st1, "tt_sb", [128, ST], F32)
        x0c = [self.sb(st1, f"x0c{k}", [128, ST], F32) for k in range(2)]
        xoc = [self.sb(st1, f"xoc{k}", [128, ST], F32) for k in range(2)]
        x0t = self.sb(st1, "x0t", [128, 2], F32)
        xot = self.sb(st1, "xot", [128, 2], F32)
        ps_s = [self.ps(st1, f"ps_s{k}", [128, 1024]) for k in range(2)]
        ps_pv_b = [self.ps(st1, f"ps_pv{k}", [128, ST]) for k in range(2)]
        ps_dn = self.ps(st1, "ps_dn", [128, ST])
        ps_tl = self.ps(st1, "ps_tl", [128, 16])

        kv = ["kv0", "kv1"]
        qr = ["qr0", "qr1", "qr2", "qr3"]
        xr = ["xr0", "xr1"]
        kago = [t.rearrange("(r j p) t -> j p r t", r=4, p=DH)
                for t in self.kag_out]
        vago = self.vag_out.rearrange("(r j g p) d -> j p r g d", r=4, g=8,
                                      p=128)
        t_kvld = {}
        t_qld = {}
        t_jdone = {}
        t_qdone = {}
        t_exp = {}
        t_pvdn = {}
        t_upd = {}
        t_rc1p = None
        t_rc2p = None
        t_t1p = None
        t_uprev = None
        t_t1prev = [None, None]
        t_xwp = {}

        for j in range(KVH):
            jp = j % 2
            a = [t_jdone[j - 2]] if j >= 2 else [self.t_ccK]
            kjb_v = kjb[jp][:].rearrange("p (r s t) -> p r s t", r=4, s=2)
            for s_ in range(2):
                p.dma("gpsimd", kv[jp], lambda g, j=j, jp=jp, s_=s_,
                      kjb_v=kjb_v: g.dma_start(
                          out=kjb_v[:, :, s_, :], in_=kago[s_][j],
                          max_dma_last_dim=8192),
                      after=a if s_ == 0 else [])
            a2 = [] if j >= 2 else [self.t_ccV]
            for r in range(4):
                t_kvld[j] = p.dma("gpsimd", kv[jp], lambda g, j=j, jp=jp,
                                  r=r: g.dma_start(
                                      out=vjb[jp][:, r * 8:(r + 1) * 8, :],
                                      in_=vago[j][:, r],
                                      max_dma_last_dim=8192),
                                  after=a2 if r == 0 else [])
            for u in range(4):
                qh = 4 * j + u
                qp = qh % 4
                a = [t_qdone[qh - 4]] if qh >= 4 else []
                t_qld[qh] = p.dma("sync", qr[qp], lambda s, qh=qh, qp=qp:
                                  s.dma_start(out=qhb[qp][:],
                                              in_=self.qT_s[qh]), after=a)
                for qt in range(2):
                    xp = qt % 2
                    a = []
                    prev = (t_upd.get((qh - 1, 2)) if qt == 0
                            else t_upd.get((qh - 1, 1)))
                    if prev:
                        a.append(prev)
                    t_xld = p.dma("sync", xr[xp], lambda s, qh=qh, qt=qt,
                                  xp=xp: s.dma_start(
                        out=x0c[xp][:],
                        in_=self.i["xown"][:, qh, qt * ST:(qt + 1) * ST]),
                        after=a)
                    pvp = (qh * 2 + qt) % 2
                    ps_pv = ps_pv_b[pvp]

                    def mk_pvdn(pr, jp, ps_pv=ps_pv):
                        def pvdn(t, jp=jp, pr=pr, sp=pr % 2):
                            first = (pr == 0)
                            last = None
                            for h2 in range(2):
                                kc = pr * 2 + h2
                                lst = (pr == 15) and h2 == 1
                                t.matmul(ps_pv[:], vjb[jp][:, kc, :],
                                         pb[sp][:, h2 * ST:(h2 + 1) * ST],
                                         start=(first and h2 == 0), stop=lst)
                                last = t.matmul(ps_dn[:], self.ones_mat[:],
                                                pb[sp][:, h2 * ST:(h2 + 1) * ST],
                                                start=(first and h2 == 0),
                                                stop=lst)
                            return last
                        return pvdn

                    for pr in range(16):
                        sp = pr % 2

                        def smm(t, jp=jp, qp=qp, qt=qt, pr=pr, sp=sp):
                            last = None
                            for h2 in range(2):
                                kc = pr * 2 + h2
                                last = t.matmul(
                                    ps_s[sp][:, h2 * ST:(h2 + 1) * ST],
                                    kjb[jp][:, kc * 128:(kc + 1) * 128],
                                    qhb[qp][:, qt * ST:(qt + 1) * ST],
                                    start=True, stop=True)
                            return last
                        a = [t_kvld[j], t_qld[qh]]
                        if guard is not None and (j, u, qt, pr) == (0, 0, 0, 0):
                            a.append(guard)
                        if sp in t_exp:
                            a.append(t_exp[sp])
                        t_smm = p.op("tensor", smm, after=a)
                        if pr >= 1:
                            a = [t_exp[1 - sp]]
                            if pr == 1:
                                if t_t1prev[pvp] is not None:
                                    a.append(t_t1prev[pvp])
                                if t_rc1p:
                                    a.append(t_rc1p)
                            t_pvdn[1 - sp] = p.op("tensor", mk_pvdn(pr - 1, jp),
                                                  after=a)
                        a = [t_smm]
                        if sp in t_pvdn:
                            a.append(t_pvdn[sp])
                        t_exp[sp] = p.op("scalar", lambda sc, sp=sp, qh=qh:
                                         sc.activation(
                                             pb[sp][:], ps_s[sp][:], AF.Exp,
                                             scale=ISQD,
                                             bias=self.negM[:, qh:qh + 1]),
                                         after=a)
                    t_pvdn[1] = p.op("tensor", mk_pvdn(15, jp),
                                     after=[t_exp[1]])
                    a = [t_pvdn[1]]
                    if t_rc2p:
                        a.append(t_rc2p)
                    t_rc1p = p.op("scalar", lambda sc: sc.activation(
                        dnl_sb[:], ps_dn[:], AF.Ln), after=a)
                    a = [t_rc1p]
                    if t_t1p:
                        a.append(t_t1p)
                    t_rc2p = p.op("scalar", lambda sc: sc.activation(
                        rc_sb[:], dnl_sb[:], AF.Exp, scale=-1.0), after=a)
                    t_t1p = p.op("vector", lambda v, ps_pv=ps_pv:
                                 v.tensor_tensor(
                                     tt_sb[:], ps_pv[:], rc_sb[:],
                                     op=OP.mult),
                        after=[t_rc2p, t_pvdn[1]] +
                              ([t_uprev] if t_uprev else []))
                    t_t1prev[pvp] = t_t1p
                    t_u = p.op("vector", lambda v, xp=xp: v.tensor_tensor(
                        xoc[xp][:], tt_sb[:], x0c[xp][:], op=OP.add),
                        after=[t_t1p, t_xld] +
                              ([t_xwp[(qh - 1, qt)]] if (qh - 1, qt) in t_xwp
                               else []))
                    t_upd[(qh, qt)] = t_u
                    t_uprev = t_u
                    t_xwp[(qh, qt)] = p.dma("sync", "st0" if qt == 0 else "st1",
                        lambda s, qh=qh, qt=qt, xp=xp: s.dma_start(
                            out=self.xh[:, qh, qt * ST:(qt + 1) * ST],
                            in_=xoc[xp][:]), after=[t_u])
                t_xldt = p.dma("sync", xr[0], lambda s, qh=qh: s.dma_start(
                    out=x0t[:], in_=self.i["xown"][:, qh, T:TE]),
                    after=[t_upd[(qh, 1)]] +
                          ([t_xwp[(qh - 1, 2)]] if (qh - 1, 2) in t_xwp else []))
                t_expt = None
                t_pvt = None
                for kg in range(4):
                    def smmt(t, jp=jp, qp=qp, kg=kg):
                        last = None
                        for k8 in range(8):
                            kc = kg * 8 + k8
                            last = t.matmul(ps_tl[:, k8 * 2:(k8 + 1) * 2],
                                            kjb[jp][:, kc * 128:(kc + 1) * 128],
                                            qhb[qp][:, T:TE], start=True,
                                            stop=True)
                        return last
                    a = [t_expt] if t_expt else [t_pvdn[1]]
                    t_smt = p.op("tensor", smmt, after=a)
                    a = [t_smt]
                    if t_pvt:
                        a.append(t_pvt)
                    t_expt = p.op("scalar", lambda sc, qh=qh: sc.activation(
                        ptl[:], ps_tl[:], AF.Exp, scale=ISQD,
                        bias=self.negM[:, qh:qh + 1]), after=a)

                    def pvdnt(t, jp=jp, kg=kg):
                        last = None
                        for k8 in range(8):
                            kc = kg * 8 + k8
                            first = (kc == 0)
                            lst = (kc == 31)
                            t.matmul(ps_pv_b[0][:, 0:2], vjb[jp][:, kc, :],
                                     ptl[:, k8 * 2:(k8 + 1) * 2],
                                     start=first, stop=lst)
                            last = t.matmul(ps_dn[:, 0:2], self.ones_mat[:],
                                            ptl[:, k8 * 2:(k8 + 1) * 2],
                                            start=first, stop=lst)
                        return last
                    a = [t_expt]
                    if kg == 0:
                        a += [t_t1prev[0] or ("T", 0), t_rc1p]
                    t_pvt = p.op("tensor", pvdnt, after=a)
                t_rc1p = p.op("scalar", lambda sc: sc.activation(
                    dnl_sb[:, 0:2], ps_dn[:, 0:2], AF.Ln),
                    after=[t_pvt, t_rc2p])
                t_rc2p = p.op("scalar", lambda sc: sc.activation(
                    rc_sb[:, 0:2], dnl_sb[:, 0:2], AF.Exp, scale=-1.0),
                    after=[t_rc1p, t_t1p])
                t_t1p = p.op("vector", lambda v: v.tensor_tensor(
                    tt_sb[:, 0:2], ps_pv_b[0][:, 0:2], rc_sb[:, 0:2],
                    op=OP.mult), after=[t_rc2p, t_pvt] +
                    ([t_uprev] if t_uprev else []))
                t_t1prev[0] = t_t1p
                t_u = p.op("vector", lambda v: v.tensor_tensor(
                    xot[:], tt_sb[:, 0:2], x0t[:], op=OP.add),
                    after=[t_t1p, t_xldt])
                t_upd[(qh, 2)] = t_u
                t_uprev = t_u
                t_xwp[(qh, 2)] = p.dma("sync", "st2", lambda s, qh=qh:
                                       s.dma_start(out=self.xh[:, qh, T:TE],
                                                   in_=xot[:]), after=[t_u])
                t_qdone[qh] = self.tok("T")
            t_jdone[j] = self.tok("T")
        st1.close()

    def ffn(self, first):
        p = self
        tag = "2" if first else "4"
        stF = ExitStack()
        if first:
            mtiles = ((0, ST), (ST, ST), (T, 2))
            ltiles = [(k * LT, LT) for k in range(4)] + [(T, 2)]
        else:
            mtiles = ((1, ST), (1 + ST, ST))
            ltiles = [(1 + k * LT, LT) for k in range(4)]
        lnx = self.sb(stF, "lnx" + tag, [128, NCH, TE], BF16)
        axs = self.sb(stF, "axs" + tag, [RC, 3 * ST + 8], BF16)
        fb_sb = self.sb(stF, "fb" + tag, [128, NCH], F32)
        ps_stat = self.ps(stF, "ps_stat" + tag, [1, 2 * ST])
        ps_bcm = self.ps(stF, "ps_bcm" + tag, [128, ST])
        ps_bcr = self.ps(stF, "ps_bcr" + tag, [128, ST])
        ps_f = [self.ps(stF, f"ps_f{tag}{k}", [128, ST]) for k in range(3)]
        ps_ax = self.ps(stF, "ps_ax" + tag, [RC, ST])

        p.dma("sync", "cs", lambda s: s.dma_start(
            out=fb_sb[:], in_=self.i["fbias_r"]))
        t_lora = self.t_lora_done
        t_fb = p.tok("cs")

        stL = ExitStack()
        xbstr = [self.sb(stL, f"xb{tag}{k}", [128, NCH, LT], BF16)
                 for k in range(2)]
        t_prev = None
        t_lnd = {}
        t_statD = {}
        base_toks = [self.tok("st0"), self.tok("st1"), self.tok("st2"),
                     self.tok("T"), self.tok("V"), self.tok("S"), t_lora,
                     self.tok("ax")]
        t_xsD = {}
        for li, (c0, wt) in enumerate(ltiles):
            lp = li % 2
            a = list(base_toks)
            if li >= 2:
                a = [t_lnd[li - 2], t_statD[li - 2]]
            t_xsD[li] = p.dma("gpsimd", "xb0" if lp == 0 else "xb1",
                              lambda g, c0=c0, wt=wt, lp=lp: g.dma_start(
                                  out=xbstr[lp][:, :, 0:wt],
                                  in_=self.xh[:, :, c0:c0 + wt],
                                  max_dma_last_dim=4096), after=a)
            t_bce, t_stat, t_sq = self.ln_stats(
                (xbstr[lp], 0), (xbstr[lp], 0), lnx, c0, wt,
                [t_xsD[li]] + ([t_prev] if t_prev else []),
                ps_stat, ps_bcm, ps_bcr)
            t_statD[li] = t_stat
            t_lnd[li] = self.ln_apply(xbstr[lp], 0, lnx, c0, wt,
                                      [t_bce, t_stat])
            t_prev = t_lnd[li]
        t_ln_all = t_prev
        stL.close()

        def lndep(c0, wt):
            need = 0
            for li_, (s0, sw) in enumerate(ltiles):
                if s0 < c0 + wt and s0 + sw > c0:
                    need = li_
            return t_lnd[need]

        t_axe = {}
        t_ax_prev = None
        for ti, (c0, wt) in enumerate(mtiles):
            a = [lndep(c0, wt), t_lora]
            if t_ax_prev:
                a.append(t_ax_prev)
            t_ax = p.op("tensor", lambda t, c0=c0, wt=wt: [
                t.matmul(ps_ax[:, 0:wt], self.a_cat[:, kc, :],
                         lnx[:, kc, c0:c0 + wt], start=(kc == 0),
                         stop=(kc == NCH - 1)) for kc in range(NCH)][-1],
                after=a)
            t_axe[ti] = p.op("vector", lambda v, ti=ti, wt=wt: v.tensor_copy(
                axs[:, ti * ST:ti * ST + wt], ps_ax[:, 0:wt]), after=[t_ax])
            t_ax_prev = t_axe[ti]
        if first:
            t_axh = p.op("tensor", lambda t: ([
                t.matmul(ps_ax[:, 0:1], self.a_catL[:, kc, :],
                         lnx[:, kc, 0:1], start=(kc == 0),
                         stop=(kc == NCH - 1)) for kc in range(NCH)],
                [t.matmul(ps_ax[:, 4:5], self.a_catR[:, kc, :],
                          lnx[:, kc, TE - 1:TE], start=(kc == 0),
                          stop=(kc == NCH - 1)) for kc in range(NCH)])[1][-1],
                after=[t_ln_all, t_lora, t_ax_prev])
            t_axhe = p.op("vector", lambda v: v.tensor_copy(
                axs[:, 3 * ST:3 * ST + 8], ps_ax[:, 0:8]), after=[t_axh])
            t_axn = p.op("vector", lambda v: (
                v.tensor_scalar_mul(axs[:, 3 * ST + 2:3 * ST + 3],
                                    axs[:, 0:1], -1.0),
                v.tensor_scalar_mul(axs[:, 3 * ST + 6:3 * ST + 7],
                                    axs[:, 2 * ST + 1:2 * ST + 2], -1.0))[-1],
                after=[t_axhe])
            t_axall = t_axn
        else:
            t_axall = t_ax_prev

        stM = ExitStack()
        wcb = [self.sb(stM, f"fw{tag}{k}", [128, NCH, 128], BF16)
               for k in range(3)]
        x1c = [self.sb(stM, f"x1c{tag}{k}", [128, ST], F32) for k in range(2)]
        xoc = [self.sb(stM, f"xoc{tag}{k}", [128, ST], F32) for k in range(2)]
        wcs = ["wc0", "wc1"]
        t_w = {}
        t_mmL = []
        t_uL = []
        t_stL = []
        fi = 0
        nt = len(mtiles)
        for do in range(NCH):
            dp = do % 3
            a = ([t_mmL[(do - 2) * nt - 1]] if do >= 3 else
                 [t_lora, t_ln_all])
            t_w[do] = p.dma("gpsimd", f"wc{do % 3}", lambda g, do=do, dp=dp:
                            g.dma_start(out=wcb[dp][:],
                                        in_=self.i["fker_pk"][:, do],
                                        max_dma_last_dim=8192), after=a)
            for ti, (c0, wt) in enumerate(mtiles):
                fp = fi % 3
                xp1 = fi % 2
                a = []
                if fi >= 2:
                    a.append(t_uL[fi - 2])
                else:
                    a += [t_ln_all]
                t_x1 = p.dma("scalar", "xr0" if xp1 == 0 else "xr1",
                             lambda s, do=do, c0=c0, wt=wt, xp1=xp1:
                             s.dma_start(out=x1c[xp1][:, 0:wt],
                                         in_=self.xh[:, do, c0:c0 + wt]),
                             after=a)

                def fmm(t, do=do, dp=do % 3, ti=ti, wt=wt, fp=fp, c0=c0):
                    for kc in range(NCH):
                        t.matmul(ps_f[fp][:, 0:wt], wcb[dp][:, kc, :],
                                 lnx[:, kc, c0:c0 + wt], start=(kc == 0),
                                 stop=False)
                    last = t.matmul(ps_f[fp][:, 0:wt],
                                    self.b_cat[:, do * 128:(do + 1) * 128],
                                    axs[:, ti * ST:ti * ST + wt],
                                    start=False,
                                    stop=not (first and ti in (0, 2)))
                    if first and ti == 0:
                        t.matmul(ps_f[fp][:, 0:1],
                                 self.b_cat[:, do * 128:(do + 1) * 128],
                                 axs[:, 3 * ST + 2:3 * ST + 3],
                                 start=False, stop=False)
                        last = t.matmul(
                            ps_f[fp][:, 0:1],
                            self.b_catL[:, do * 128:(do + 1) * 128],
                            axs[:, 3 * ST:3 * ST + 1], start=False, stop=True)
                    if first and ti == 2:
                        t.matmul(ps_f[fp][:, 1:2],
                                 self.b_cat[:, do * 128:(do + 1) * 128],
                                 axs[:, 3 * ST + 6:3 * ST + 7],
                                 start=False, stop=False)
                        last = t.matmul(
                            ps_f[fp][:, 1:2],
                            self.b_catR[:, do * 128:(do + 1) * 128],
                            axs[:, 3 * ST + 4:3 * ST + 5], start=False,
                            stop=True)
                    return last
                a = [t_w[do], lndep(c0, wt), t_axe[ti]]
                if first and ti in (0, 2):
                    a.append(t_axall)
                if fi >= 3:
                    a.append(t_uL[fi - 3])
                t_mmL.append(p.op("tensor", fmm, after=a))
                a = [t_mmL[fi], t_x1, t_fb]
                if fi >= 2:
                    a.append(t_stL[fi - 2])
                t_uL.append(p.op("vector", lambda v, do=do, wt=wt, fp=fp,
                                 xp1=xp1:
                                 v.scalar_tensor_tensor(
                                     xoc[xp1][:, 0:wt], ps_f[fp][:, 0:wt],
                                     fb_sb[:, do:do + 1], x1c[xp1][:, 0:wt],
                                     op0=OP.add, op1=OP.add), after=a))
                dst_ = self.xh if first else self.y_t
                dco = c0 if first else c0 - 1
                t_stL.append(p.dma("sync", "st0" if xp1 == 0 else "st1",
                    lambda s, do=do, dco=dco, wt=wt, xp1=xp1, dst_=dst_:
                    s.dma_start(out=dst_[:, do, dco:dco + wt],
                                in_=xoc[xp1][:, 0:wt]), after=[t_uL[fi]]))
                fi += 1
        stM.close()
        stF.close()

    def build_lora(self):
        p = self
        stO = ExitStack()
        g3bc = self.sb(stO, "g3bc", [128, 3 * NL], F32, side="right")
        g3row = self.sb(stO, "g3row", [1, 3 * NL], F32, side="right")
        aacc = self.sb(stO, "aacc", [128, 3, NCH, RK], F32, side="right")
        bacc = self.sb(stO, "bacc", [128, 3, 1024], F32, side="right")
        anb = [self.sb(stO, f"anb{k}", [128, NCH, RK], BF16, side="right")
               for k in range(2)]
        bnb = [self.sb(stO, f"bnb{k}", [128, 1024], BF16, side="right")
               for k in range(2)]
        ps_g = self.ps(stO, "ps_g", [128, 3 * NL])

        p.dma("sync", "cs", lambda s: s.dma_start(out=g3row[:],
                                                  in_=self.i["g3_row"]))
        t_g = p.tok("cs")
        t_gb = p.op("tensor", lambda t: t.matmul(
            ps_g[:], self.ones_row[:], g3row[:], start=True, stop=True),
            after=[t_g, self.t_init, self.tok("T"), self.tok("S"),
                   self.tok("V")])
        self.t_psg_guard = t_gb
        t_gbe = p.op("vector", lambda v: v.tensor_copy(g3bc[:], ps_g[:]),
                     after=[t_gb])
        lr = ["lr0", "lr1"]
        t_ld = {}
        t_acc = {}
        prev = None
        for n in range(NL):
            npr = n % 2
            a = [t_acc[n - 2]] if n >= 2 else [self.tok("T"), self.tok("S"),
                                               self.tok("V")]
            p.dma("gpsimd", lr[npr], lambda g, n=n, npr=npr:
                  g.dma_start(out=anb[npr][:], in_=self.i["a_pk"][:, n],
                              max_dma_last_dim=8192), after=a)
            t_ld2 = p.dma("gpsimd", lr[npr], lambda g, n=n, npr=npr:
                          g.dma_start(out=bnb[npr][:],
                                      in_=self.i["b_pk"][:, n],
                                      max_dma_last_dim=8192))
            t_ld[n] = t_ld2
            for gi in range(3):
                col = gi * NL + n
                if n == 0:
                    t_a = p.op("vector", lambda v, gi=gi, npr=npr, col=col:
                               v.tensor_scalar_mul(
                                   aacc[:, gi], anb[npr][:],
                                   g3bc[:, col:col + 1]),
                               after=[t_ld[n], t_gbe] +
                                     ([prev] if prev else []))
                    t_b = p.op("vector", lambda v, gi=gi, npr=npr, col=col:
                               v.tensor_scalar_mul(
                                   bacc[:, gi], bnb[npr][:],
                                   g3bc[:, col:col + 1]),
                               after=[t_ld[n], t_a])
                else:
                    t_a = p.op("vector", lambda v, gi=gi, npr=npr, col=col:
                               v.scalar_tensor_tensor(
                                   aacc[:, gi], anb[npr][:],
                                   g3bc[:, col:col + 1], aacc[:, gi],
                                   op0=OP.mult, op1=OP.add),
                               after=[t_ld[n], t_gbe] +
                                     ([prev] if prev else []))
                    t_b = p.op("vector", lambda v, gi=gi, npr=npr, col=col:
                               v.scalar_tensor_tensor(
                                   bacc[:, gi], bnb[npr][:],
                                   g3bc[:, col:col + 1], bacc[:, gi],
                                   op0=OP.mult, op1=OP.add),
                               after=[t_ld[n], t_a])
                prev = t_b
            t_acc[n] = prev
        t_cp = []
        for dstA, gi in ((self.a_cat, 1), (self.a_catL, 0), (self.a_catR, 2)):
            t_cp.append(p.op("vector", lambda v, dstA=dstA, gi=gi:
                             v.tensor_copy(dstA[:, :, 0:RK], aacc[:, gi]),
                             after=[t_acc[NL - 1]]))
        t_dma = []
        for k, (dstA, gi) in enumerate(((self.a_cat, 1), (self.a_catL, 0),
                                        (self.a_catR, 2))):
            t_dma.append(p.dma("gpsimd", "ax", lambda g, dstA=dstA:
                               g.dma_start(out=dstA[:, :, RK:RC],
                                           in_=self.i["a_sh"],
                                           max_dma_last_dim=8192),
                               after=[t_cp[k]]))
        for dstB, gi in ((self.b_cat, 1), (self.b_catL, 0), (self.b_catR, 2)):
            for q in range(4):
                t_dma.append(p.dma("gpsimd", "ax",
                    lambda g, dstB=dstB, gi=gi, q=q: g.dma_start(
                        out=dstB[0:RK, q * 1024:(q + 1) * 1024],
                        in_=bacc[q * RK:(q + 1) * RK, gi],
                        max_dma_last_dim=4096), after=[t_acc[NL - 1]]))
            t_dma.append(p.dma("gpsimd", "ax", lambda g, dstB=dstB:
                               g.dma_start(out=dstB[RK:RC, :],
                                           in_=self.i["b_sh"],
                                           max_dma_last_dim=8192)))
        stO.close()
        self.t_lora_done = self.tok("ax")
        return self.t_lora_done

    def p3(self):
        p = self
        st3 = ExitStack()
        st_ln3 = ExitStack()
        hm_sb = self.sb(st3, "hm_sb", [128, 2], F32)
        cb_sb = self.sb(st3, "cb_sb", [128, 3 * WCH], F32)
        bqb_sb = self.sb(st3, "bqb_sb", [128, WCH], F32)
        bkb_sb = self.sb(st3, "bkb_sb", [128, WCH], F32)
        bvb_sb = self.sb(st3, "bvb_sb", [1, WB], F32)
        bo_sb = self.sb(st3, "bo_sb", [128, WCH], F32)
        pjb_sb = self.sb(st3, "pjb_sb", [128, NCH], F32)
        mask_sb = self.sb(st3, "mask_sb", [128, BLK], F32)
        ps_c = [self.ps(st3, f"ps_c{k}", [128, ST]) for k in range(2)]

        for nm, dst in (("hmask", hm_sb), ("conv_b_r", cb_sb),
                        ("bqb_r", bqb_sb), ("bkb_r", bkb_sb),
                        ("bvb_row", bvb_sb), ("bo_r", bo_sb),
                        ("projb_r", pjb_sb), ("mask8", mask_sb)):
            p.dma("sync", "cs", lambda s, nm=nm, dst=dst: s.dma_start(
                out=dst[:], in_=self.i[nm]))
        t_c7 = t_c0 = t_c1 = t_c2 = t_c3 = t_c4 = t_c5 = t_c6 = p.tok("cs")
        ln3 = self.sb(st_ln3, "ln3", [128, NCH, TE], BF16)
        stL = ExitStack()
        xbstr = [self.sb(stL, f"xb3{k}", [128, NCH, LT], BF16)
                 for k in range(2)]
        ps_stat = self.ps(stL, "ps_stat3", [1, 2 * ST])
        ps_bcm = self.ps(stL, "ps_bcm3", [128, ST])
        ps_bcr = self.ps(stL, "ps_bcr3", [128, ST])
        ltiles = [(k * LT, LT) for k in range(4)] + [(T, 2)]
        t_prev = None
        t_statD = {}
        t_lnd3 = {}
        base_toks = [self.tok("st0"), self.tok("st1"), self.tok("st2"),
                     self.tok("T"), self.tok("V"), self.tok("S")]
        for li, (c0, wt) in enumerate(ltiles):
            lp = li % 2
            a = list(base_toks) if li < 2 else [t_lnd3[li - 2],
                                               t_statD[li - 2]]
            t_xs = p.dma("gpsimd", "xb0" if lp == 0 else "xb1",
                         lambda g, c0=c0, wt=wt, lp=lp: g.dma_start(
                             out=xbstr[lp][:, :, 0:wt],
                             in_=self.xh[:, :, c0:c0 + wt],
                             max_dma_last_dim=4096), after=a)
            t_bce, t_stat, t_sq = self.ln_stats(
                (xbstr[lp], 0), (xbstr[lp], 0), ln3, c0, wt,
                [t_xs] + ([t_prev] if t_prev else []),
                ps_stat, ps_bcm, ps_bcr)
            t_statD[li] = t_stat
            t_lnd3[li] = self.ln_apply(xbstr[lp], 0, ln3, c0, wt,
                                       [t_bce, t_stat])
            t_prev = t_lnd3[li]
        stL.close()
        t_hm = p.op("vector", lambda v: (
            v.tensor_scalar_mul(ln3[:, :, 0:1], ln3[:, :, 0:1],
                                hm_sb[:, 0:1]),
            v.tensor_scalar_mul(ln3[:, :, TE - 1:TE], ln3[:, :, TE - 1:TE],
                                hm_sb[:, 1:2]))[-1], after=[t_prev, t_c7])
        t_ln3 = t_hm

        stCV = ExitStack()
        cwb = [self.sb(stCV, f"cwb{k}", [128, NCH, 128], BF16, side="right")
               for k in range(3)]
        qcv = self.sb(stCV, "qcv", [128, WCH, T], BF16, side="right")
        kcv = self.sb(stCV, "kcv", [128, WCH, T], BF16, side="right")
        vcv = self.sb(stCV, "vcv", [128, WCH, T], BF16, side="right")
        cwq = ["wc0", "wc1"]
        t_cwL = []
        t_cmmL = []
        t_cevL = []
        ci = 0
        for o in range(3 * WCH):
            dstbuf = (qcv, kcv, vcv)[o // WCH]
            oc = o % WCH
            for tap in range(3):
                wi_ = o * 3 + tap
                opr = wi_ % 3
                a = [t_cmmL[2 * (wi_ - 3) + 1]] if wi_ >= 3 else [t_ln3]
                t_cwL.append(p.dma("gpsimd", f"wc{wi_ % 3}",
                             lambda g, o=o, tap=tap, opr=opr: g.dma_start(
                                 out=cwb[opr][:],
                                 in_=self.i["conv_pk"][:, o, tap],
                                 max_dma_last_dim=8192), after=a))
                for qt in range(2):
                    fp = ci % 2

                    def cmm(t, opr=opr, tap=tap, qt=qt, fp=fp):
                        for kc in range(NCH):
                            last = t.matmul(
                                ps_c[fp][:], cwb[opr][:, kc, :],
                                ln3[:, kc, qt * ST + tap:qt * ST + tap + ST],
                                start=(tap == 0 and kc == 0),
                                stop=(tap == 2 and kc == NCH - 1))
                        return last
                    a = [t_cwL[wi_], t_ln3]
                    if tap == 0 and o >= 1:
                        a.append(t_cevL[(o - 1) * 2 + qt])
                    t_cmmL.append(p.op("tensor", cmm, after=a))
                    if tap == 2:
                        t_cevL.append(p.op("scalar",
                            lambda sc, o=o, oc=oc, qt=qt, fp=fp,
                            dstbuf=dstbuf: sc.activation(
                                dstbuf[:, oc, qt * ST:(qt + 1) * ST],
                                ps_c[fp][:], AF.Identity,
                                bias=cb_sb[:, o:o + 1]),
                            after=[t_cmmL[-1], t_c0]))
                    ci += 1
        t_conv_done = self.tok("S")
        st_ln3.close()

        stH = ExitStack()
        wsm = [self.sb(stH, f"wsm{k}", [128, WCH, 128], BF16)
               for k in range(3)]
        wvb_sb = self.sb(stH, "wvb_sb", [128, WCH, HB * DHB], BF16)
        qhx = self.sb(stH, "qhx", [128, WCH, T], BF16)
        khx = self.sb(stH, "khx", [128, WCH, T], BF16)
        vhx = self.sb(stH, "vhx", [128, NBLK, WB], BF16)
        t_wvb = p.dma("gpsimd", "wres", lambda g: g.dma_start(
            out=wvb_sb[:], in_=self.i["wvb"].rearrange("(n p) d -> p n d",
                                                       p=128),
            max_dma_last_dim=4096), after=[self.tok("T")])
        t_hwL = {}
        t_hmmL = []
        t_hevL = []
        hi = 0
        for which, (wpk, srcbuf, dstbuf, bias_sb, bias_tok) in enumerate((
                ("wqb_pk", qcv, qhx, bqb_sb, t_c1),
                ("wkb_pk", kcv, khx, bkb_sb, t_c2))):
            for he in range(WCH):
                idx = which * WCH + he
                hp = idx % 3
                a = [t_hmmL[-5]] if idx >= 3 else [self.tok("T")]
                t_hwL[idx] = p.dma("gpsimd", f"wc{idx % 3}",
                                   lambda g, wpk=wpk, he=he, hp=hp:
                                   g.dma_start(out=wsm[hp][:],
                                               in_=self.i[wpk][:, he],
                                               max_dma_last_dim=8192),
                                   after=a)
                for qt in range(2):
                    fp = hi % 2

                    def hmm(t, hp=hp, qt=qt, fp=fp, srcbuf=srcbuf):
                        for kc in range(WCH):
                            last = t.matmul(
                                ps_c[fp][:], wsm[hp][:, kc, :],
                                srcbuf[:, kc, qt * ST:(qt + 1) * ST],
                                start=(kc == 0), stop=(kc == WCH - 1))
                        return last
                    a = [t_hwL[idx], t_conv_done]
                    if hi >= 2:
                        a.append(t_hevL[hi - 2])
                    t_hmmL.append(p.op("tensor", hmm, after=a))
                    t_hevL.append(p.op("scalar", lambda sc, he=he, qt=qt,
                                       fp=fp, dstbuf=dstbuf, bias_sb=bias_sb:
                                       sc.activation(
                                           dstbuf[:, he, qt * ST:(qt + 1) * ST],
                                           ps_c[fp][:], AF.Identity,
                                           bias=bias_sb[:, he:he + 1]),
                                       after=[t_hmmL[hi], bias_tok]))
                    hi += 1
        t_vmmL = []
        t_vevL = []
        vi = 0
        for tc in range(NBLK):
            for hf in range(2):
                fp = vi % 2

                def vmm(t, tc=tc, hf=hf, fp=fp):
                    t.matmul(ps_c[fp][:], self.ones_row[:],
                             bvb_sb[0:1, hf * 512:(hf + 1) * 512], start=True,
                             stop=False)
                    for kc in range(WCH):
                        last = t.matmul(ps_c[fp][:],
                                        vcv[:, kc, tc * 128:(tc + 1) * 128],
                                        wvb_sb[:, kc, hf * 512:(hf + 1) * 512],
                                        start=False, stop=(kc == WCH - 1))
                    return last
                a = [t_conv_done, t_c3, t_wvb]
                if vi >= 2:
                    a.append(t_vevL[vi - 2])
                if vi == 0 and t_hevL:
                    a.append(t_hevL[-1])
                t_vmmL.append(p.op("tensor", vmm, after=a))
                t_vevL.append(p.op("scalar", lambda sc, tc=tc, hf=hf, fp=fp:
                                   sc.activation(
                                       vhx[:, tc, hf * 512:(hf + 1) * 512],
                                       ps_c[fp][:], AF.Copy),
                                   after=[t_vmmL[vi]]))
                vi += 1
        t_heads_done = self.tok("S")
        stCV.close()

        stN = ExitStack()
        nsq = self.sb(stN, "nsq", [128, WCH, T], BF16)
        ps_n3 = self.ps(stN, "ps_n3", [1, 1024])
        t_nprev = None
        t_nrd = None
        for which, (buf, dst) in enumerate(((qhx, self.qnb), (khx, self.knb))):
            a = [t_heads_done]
            if t_nprev:
                a.append(t_nprev)
            t_sqx = p.op("scalar", lambda sc, buf=buf: sc.activation(
                nsq[:], buf[:], AF.Square), after=a)
            for h in range(HB):
                base = 64 * (h % 2)
                ch = h // 2
                a = [t_sqx]
                if t_nrd:
                    a.append(t_nrd)
                t_nmm = p.op("tensor", lambda t, base=base, ch=ch: (
                    t.matmul(ps_n3[0:1, 0:512],
                             self.ones_col[base:base + 64, :],
                             nsq[base:base + 64, ch, 0:512], start=True,
                             stop=True),
                    t.matmul(ps_n3[0:1, 512:1024],
                             self.ones_col[base:base + 64, :],
                             nsq[base:base + 64, ch, 512:1024], start=True,
                             stop=True))[-1], after=a)
                t_nrd = p.op("vector", lambda v, h=h, dst=dst: v.reduce_max(
                    dst[0:1, h:h + 1], ps_n3[0:1, :], axis=AX.X),
                    after=[t_nmm])
            t_nprev = t_nmm
        t_mb1 = p.op("vector", lambda v: v.tensor_tensor(
            self.mrow[0:1, 0:HB], self.qnb[:], self.knb[:], op=OP.mult),
            after=[t_nrd])
        t_mb2 = p.op("scalar", lambda sc: sc.activation(
            self.mrow[0:1, 0:HB], self.mrow[0:1, 0:HB], AF.Ln,
            bias=self.eps_t[:]), after=[t_mb1, self.tok("S")])
        t_mb3 = p.op("scalar", lambda sc: sc.activation(
            self.mrow[0:1, 0:HB], self.mrow[0:1, 0:HB], AF.Exp, scale=0.5),
            after=[t_mb2])
        t_mb4 = p.op("scalar", lambda sc: sc.activation(
            self.msqrow[0:1, 0:HB], self.mrow[0:1, 0:HB], AF.Copy,
            scale=-ISQB, bias=-1.0), after=[t_mb3])
        t_mb5 = p.op("tensor", lambda t: t.matmul(
            ps_c[0][:, 0:HB], self.ones_row[:], self.msqrow[0:1, 0:HB],
            start=True, stop=True), after=[t_mb4, self.tok("T")])
        t_mb6 = p.op("vector", lambda v: v.tensor_copy(
            self.negMb[:], ps_c[0][:, 0:HB]), after=[t_mb5])
        stN.close()

        stB = ExitStack()
        sm_sb = self.sb(stB, "sm_sb", [128, NBLK * BLK], F32, side="right")
        pbx = self.sb(stB, "pbx", [128, NBLK * BLK], BF16, side="right")
        rcb_row = self.sb(stB, "rcb_row", [128, NBLK * BLK], F32, side="right")
        rcb_sb = self.sb(stB, "rcb_sb", [128, NBLK * BLK], F32, side="right")
        ohx = self.sb(stB, "ohx", [128, WCH, T], BF16, side="right")
        ps_sb3 = self.ps(stB, "ps_sb3", [128, NBLK * BLK])
        ps_db3 = self.ps(stB, "ps_db3", [128, NBLK * BLK])
        ps_pv3 = self.ps(stB, "ps_pv3", [128, NBLK * BLK])
        mb_ap = mask_sb[:]
        mask_bc = bass.AP(tensor=mb_ap.tensor, offset=mb_ap.offset,
                          ap=[[mb_ap.ap[0][0], 128], [0, NBLK], [1, BLK]])
        t_bs = None
        t_msk = None
        t_pb = None
        t_bd = None
        t_br1 = None
        t_br2 = None
        t_br4 = None
        t_bp = None
        t_orc = None
        for h in range(HB):
            base = 64 * (h % 2)
            ch = h // 2

            def bsm(t, base=base, ch=ch):
                for b in range(NBLK):
                    last = t.matmul(
                        ps_sb3[:, b * BLK:(b + 1) * BLK],
                        khx[base:base + 64, ch, b * BLK:(b + 1) * BLK],
                        qhx[base:base + 64, ch, b * BLK:(b + 1) * BLK],
                        start=True, stop=True)
                return last
            a = [t_heads_done, t_mb6]
            if t_msk:
                a.append(t_msk)
            t_bs = p.op("tensor", bsm, after=a)
            a = [t_bs, t_c6]
            if t_pb:
                a.append(t_pb)
            t_msk = p.op("vector", lambda v: v.tensor_tensor(
                sm_sb[:].rearrange("p (b c) -> p b c", b=NBLK),
                ps_sb3[:].rearrange("p (b c) -> p b c", b=NBLK),
                mask_bc, op=OP.add), after=a)
            a = [t_msk]
            if t_bd:
                a.append(t_bd)
            if t_bp:
                a.append(t_bp)
            t_pb = p.op("scalar", lambda sc, h=h: sc.activation(
                pbx[:], sm_sb[:], AF.Exp, scale=ISQB,
                bias=self.negMb[:, h:h + 1]), after=a)

            def bdn(t):
                for b in range(NBLK):
                    last = t.matmul(ps_db3[:, b * BLK:(b + 1) * BLK],
                                    self.ones_mat[:],
                                    pbx[:, b * BLK:(b + 1) * BLK],
                                    start=True, stop=True)
                return last
            a = [t_pb]
            if t_br1:
                a.append(t_br1)
            if t_br4:
                a.append(t_br4)
            t_bd = p.op("tensor", bdn, after=a)
            a = [t_bd]
            if t_br2:
                a.append(t_br2)
            t_br1 = p.op("scalar", lambda sc: sc.activation(
                rcb_row[:], ps_db3[:], AF.Ln), after=a)
            a = [t_br1]
            if t_br4:
                a.append(t_br4)
            t_br2 = p.op("scalar", lambda sc: sc.activation(
                rcb_sb[:], rcb_row[:], AF.Exp, scale=-1.0), after=a)
            t_br4 = t_br2

            def bpv(t, h=h, base=base):
                for b in range(NBLK):
                    last = t.matmul(
                        ps_pv3[base:base + 64, b * BLK:(b + 1) * BLK],
                        vhx[:, b, h * DHB:(h + 1) * DHB],
                        pbx[:, b * BLK:(b + 1) * BLK], start=True, stop=True)
                return last
            a = [t_pb]
            if t_orc:
                a.append(t_orc)
            t_bp = p.op("tensor", bpv, after=a)
            t_orc = p.op("vector", lambda v, base=base, ch=ch:
                         v.tensor_tensor(ohx[base:base + 64, ch, :],
                                         ps_pv3[base:base + 64, :],
                                         rcb_sb[base:base + 64, :],
                                         op=OP.mult), after=[t_bp, t_br4])
        t_attn_done = self.tok("V")
        stH.close()
        stB_keepohx = ohx
        stW = ExitStack()
        wsm2 = [self.sb(stW, f"wsm2{k}", [128, WCH, 128], BF16)
                for k in range(3)]
        o1x = self.sb(stW, "o1x", [128, WCH, T], BF16)
        t_woL = {}
        t_womL = []
        t_woeL = []
        wi = 0
        for wc_ in range(WCH):
            wp = wc_ % 3
            a = [t_womL[-5]] if wc_ >= 3 else [self.tok("T")]
            t_woL[wc_] = p.dma("gpsimd", f"wc{wc_ % 3}",
                               lambda g, wc_=wc_, wp=wp:
                               g.dma_start(out=wsm2[wp][:],
                                           in_=self.i["wo_pk"][:, wc_],
                                           max_dma_last_dim=8192), after=a)
            for qt in range(2):
                fp = wi % 2

                def womm(t, wp=wp, qt=qt, fp=fp):
                    for kc in range(WCH):
                        last = t.matmul(ps_c[fp][:], wsm2[wp][:, kc, :],
                                        ohx[:, kc, qt * ST:(qt + 1) * ST],
                                        start=(kc == 0), stop=(kc == WCH - 1))
                    return last
                a = [t_woL[wc_], t_attn_done]
                if wi >= 2:
                    a.append(t_woeL[wi - 2])
                t_womL.append(p.op("tensor", womm, after=a))
                t_woeL.append(p.op("scalar", lambda sc, wc_=wc_, qt=qt, fp=fp:
                                   sc.activation(
                                       o1x[:, wc_, qt * ST:(qt + 1) * ST],
                                       ps_c[fp][:], AF.Identity,
                                       bias=bo_sb[:, wc_:wc_ + 1]),
                                   after=[t_womL[wi], t_c4]))
                wi += 1
        t_wo_done = self.tok("S")
        stB.close()

        stP = ExitStack()
        wsm3 = [self.sb(stP, f"wsm3{k}", [128, WCH, 128], BF16, side="right")
                for k in range(3)]
        x2c = [self.sb(stP, f"x2c{k}", [128, ST], F32, side="right")
               for k in range(2)]
        xoc = [self.sb(stP, f"xoc3{k}", [128, ST], F32, side="right")
               for k in range(2)]
        t_pwL = {}
        t_pmmL = []
        t_puL = []
        t_pstL = []
        pi = 0
        for do in range(NCH):
            dp = do % 3
            a = [t_pmmL[-5]] if do >= 3 else [self.tok("T")]
            t_pwL[do] = p.dma("gpsimd", f"wc{do % 3}", lambda g, do=do, dp=dp:
                              g.dma_start(out=wsm3[dp][:],
                                          in_=self.i["projw_pk"][:, do],
                                          max_dma_last_dim=8192), after=a)
            for qt in range(2):
                fp = pi % 2
                a = []
                if pi >= 2:
                    a.append(t_puL[pi - 2])
                else:
                    a += [self.tok("T"), self.tok("V"), self.tok("S")]
                t_x2 = p.dma("sync", "xr0" if fp == 0 else "xr1",
                             lambda s, do=do, qt=qt, fp=fp: s.dma_start(
                                 out=x2c[fp][:],
                                 in_=self.xh[:, do, 1 + qt * ST:1 + (qt + 1) * ST]),
                             after=a)

                def pmm(t, dp=dp, qt=qt, fp=fp):
                    for kc in range(WCH):
                        last = t.matmul(ps_c[fp][:], wsm3[dp][:, kc, :],
                                        o1x[:, kc, qt * ST:(qt + 1) * ST],
                                        start=(kc == 0), stop=(kc == WCH - 1))
                    return last
                a = [t_pwL[do], t_wo_done]
                if pi >= 2:
                    a.append(t_puL[pi - 2])
                t_pmmL.append(p.op("tensor", pmm, after=a))
                a = [t_pmmL[pi], t_x2, t_c5]
                if pi >= 2:
                    a.append(t_pstL[pi - 2])
                t_puL.append(p.op("vector", lambda v, do=do, fp=fp:
                                  v.scalar_tensor_tensor(
                                      xoc[fp][:], ps_c[fp][:],
                                      pjb_sb[:, do:do + 1], x2c[fp][:],
                                      op0=OP.add, op1=OP.add), after=a))
                t_pstL.append(p.dma("sync", "st0" if fp == 0 else "st1",
                    lambda s, do=do, qt=qt, fp=fp: s.dma_start(
                        out=self.xh[:, do, 1 + qt * ST:1 + (qt + 1) * ST],
                        in_=xoc[fp][:]), after=[t_puL[pi]]))
                pi += 1
        stW.close()
        stP.close()
        st3.close()


def _host_prep(inputs, core):
    x = np.asarray(inputs["L0_attn_a_x"])
    b, s0 = core // 4, (core % 4) * T
    out = {}
    xt = np.ascontiguousarray(x[b].T)
    ext = np.roll(xt, -(s0 - 1), axis=1)[:, 0:TE]
    out["xown"] = np.ascontiguousarray(
        ext.reshape(NCH, 128, TE).transpose(1, 0, 2))
    inv = (1.0 / 10000.0 ** (np.arange(0, DH, 2, dtype=np.float32) / DH)
           ).astype(np.float32)
    ang = np.arange(SEQ, dtype=np.float32)[:, None] * inv[None, :]
    cos = np.cos(ang).astype(np.float32).T
    sin = np.sin(ang).astype(np.float32).T
    cosd = np.concatenate([cos, cos], axis=0)
    sins = np.concatenate([-sin, sin], axis=0)
    out["cosd_own"] = np.ascontiguousarray(
        np.roll(cosd, -(s0 - 1), axis=1)[:, 0:TE])
    out["sins_own"] = np.ascontiguousarray(
        np.roll(sins, -(s0 - 1), axis=1)[:, 0:TE])
    out["wk"] = np.asarray(inputs["L0_attn_a_wk"])
    out["wv"] = np.asarray(inputs["L0_attn_a_wv"])
    wq = np.asarray(inputs["L0_attn_a_wq"])
    out["wq_pk"] = np.ascontiguousarray(
        wq.reshape(NCH, 128, H, 128).transpose(1, 2, 0, 3))
    out["bq_r"] = np.ascontiguousarray(
        np.asarray(inputs["L0_attn_a_bq"]).reshape(H, 128).T)
    out["bk_r"] = np.ascontiguousarray(
        np.asarray(inputs["L0_attn_a_bk"]).reshape(KVH, 128).T)
    out["bv_row"] = np.asarray(inputs["L0_attn_a_bv"]).reshape(1, KVH * DH)
    cw = np.asarray(inputs["L0_attn_b_conv_w"])
    out["conv_pk"] = np.ascontiguousarray(
        cw.reshape(3, NCH, 128, 3 * WCH, 128).transpose(2, 3, 0, 1, 4))
    out["conv_b_r"] = np.ascontiguousarray(
        np.asarray(inputs["L0_attn_b_conv_b"]).reshape(3 * WCH, 128).T)
    for nm, key in (("wqb_pk", "L0_attn_b_wq"), ("wkb_pk", "L0_attn_b_wk")):
        w = np.asarray(inputs[key]).reshape(WB, HB * DHB)
        out[nm] = np.ascontiguousarray(
            w.reshape(WCH, 128, WCH, 128).transpose(1, 2, 0, 3))
    out["wvb"] = np.ascontiguousarray(
        np.asarray(inputs["L0_attn_b_wv"]).reshape(WB, HB * DHB))
    out["bqb_r"] = np.ascontiguousarray(
        np.asarray(inputs["L0_attn_b_bq"]).reshape(WCH, 128).T)
    out["bkb_r"] = np.ascontiguousarray(
        np.asarray(inputs["L0_attn_b_bk"]).reshape(WCH, 128).T)
    out["bvb_row"] = np.asarray(inputs["L0_attn_b_bv"]).reshape(1, HB * DHB)
    wo = np.asarray(inputs["L0_attn_b_wo"]).reshape(HB * DHB, WB)
    out["wo_pk"] = np.ascontiguousarray(
        wo.reshape(WCH, 128, WCH, 128).transpose(1, 2, 0, 3))
    out["bo_r"] = np.ascontiguousarray(
        np.asarray(inputs["L0_attn_b_bo"]).reshape(WCH, 128).T)
    pw = np.asarray(inputs["L0_attn_b_proj_w"])
    out["projw_pk"] = np.ascontiguousarray(
        pw.reshape(WCH, 128, NCH, 128).transpose(1, 2, 0, 3))
    out["projb_r"] = np.ascontiguousarray(
        np.asarray(inputs["L0_attn_b_proj_b"]).reshape(NCH, 128).T)
    fk = np.asarray(inputs["L0_ffn_kernel"])
    out["fker_pk"] = np.ascontiguousarray(
        fk.reshape(NCH, 128, NCH, 128).transpose(1, 2, 0, 3))
    out["fbias_r"] = np.ascontiguousarray(
        np.asarray(inputs["L0_ffn_bias"]).reshape(NCH, 128).T)
    la = np.asarray(inputs["L0_ffn_lora_A"])
    out["a_pk"] = np.ascontiguousarray(
        la.reshape(NL, RK, NCH, 128).transpose(3, 0, 2, 1))
    lb = np.asarray(inputs["L0_ffn_lora_B"])
    out["b_pk"] = np.ascontiguousarray(
        lb.transpose(0, 2, 1).reshape(NL, RK, 4, 1024)
        .transpose(2, 1, 0, 3).reshape(128, NL, 1024))
    out["a_sh"] = np.ascontiguousarray(
        np.asarray(inputs["L0_ffn_lora_A_shared"]).T.reshape(NCH, 128, RK)
        .transpose(1, 0, 2))
    out["b_sh"] = np.ascontiguousarray(
        np.asarray(inputs["L0_ffn_lora_B_shared"]).T)
    gate = np.asarray(inputs["L0_ffn_gate"])
    g3 = np.zeros((3, NL), dtype=np.float32)
    g3[1] = gate[core]
    if s0 > 0:
        g3[0] = gate[core - 1]
    if s0 + T < SEQ:
        g3[2] = gate[core + 1]
    out["g3_row"] = np.ascontiguousarray(g3.reshape(1, 3 * NL))
    hm = np.zeros((128, 2), dtype=np.float32)
    hm[:, 0] = 1.0 if s0 > 0 else 0.0
    hm[:, 1] = 1.0 if s0 + T < SEQ else 0.0
    out["hmask"] = hm
    mk = np.where(np.arange(BLK)[:, None] <= np.arange(BLK)[None, :], 0.0,
                  NEGBIG).astype(np.float32)
    out["mask8"] = np.ascontiguousarray(mk)
    for k in out:
        out[k] = np.ascontiguousarray(out[k], dtype=np.float32)
    return out


_NC_CACHE = {}


def get_nc(phases=("p0", "p1", "p2", "p3", "p4"), debug=()):
    key = (tuple(phases), tuple(debug))
    if key not in _NC_CACHE:
        kb = KB(phases=phases, debug=debug)
        _NC_CACHE[key] = kb.build()
    return _NC_CACHE[key]


def kernel(**inputs):
    nc = get_nc()
    in_maps = [_host_prep(inputs, c) for c in range(NCORES)]
    res = run_bass_kernel_spmd(nc, in_maps, core_ids=list(range(NCORES)))
    y = np.empty((BATCH, SEQ, D), dtype=np.float32)
    for c in range(NCORES):
        b, s0 = c // 4, (c % 4) * T
        yt = res.results[c]["y_t"]
        y[b, s0:s0 + T, :] = yt.transpose(2, 1, 0).reshape(T, D)
    return y
